# version 37
# baseline (speedup 1.0000x reference)
import numpy as np
import concourse.bass as bass
import concourse.mybir as mybir
from concourse.bass_utils import run_bass_kernel_spmd

F32 = mybir.dt.float32
BF16 = mybir.dt.bfloat16
AF = mybir.ActivationFunctionType
ALU = mybir.AluOpType
AX = mybir.AxisListType

ENGINES = ("sync", "tensor", "scalar", "vector", "gpsimd")
SBUF_BASE = 16512
SBUF_END = 229376
N_CORES = 8


def nslices(width, maxn=512):
    out = []
    o = 0
    while o < width:
        n = min(maxn, width - o)
        out.append((o, n))
        o += n
    return out


class Sched:

    def __init__(self, nc):
        self.nc = nc
        self.q = {e: [] for e in ENGINES}
        self.sems = {}
        self.cnt = {}
        self.waited = {e: {} for e in ENGINES}
        self.written = {e: set() for e in ENGINES}
        self.readset = {e: set() for e in ENGINES}

    def sem(self, name):
        self.sems[name] = self.nc.alloc_semaphore(name)
        self.cnt[name] = 0

    def op(self, eng, fn, inc=None, n=1, reads=(), writes=()):
        q = self.q[eng]
        if eng in ("vector", "scalar", "gpsimd"):
            r, w = set(reads), set(writes)
            if (r & self.written[eng]) or (w & self.written[eng]) \
                    or (w & self.readset[eng]):
                q.append(("drain",))
                self.written[eng].clear()
                self.readset[eng].clear()
            self.readset[eng].update(r)
        tok = None
        if inc is not None:
            self.cnt[inc] += n
            tok = (inc, self.cnt[inc])
        q.append(("op", fn, inc, self.cnt[inc] if inc else 0, n))
        self.written[eng].update(writes)
        return tok

    def tok(self, sem):
        return (sem, self.cnt[sem])

    def wait(self, eng, tok):
        if tok is None:
            return
        sem, val = tok
        if val <= 0:
            return
        if self.waited[eng].get(sem, -1) >= val:
            return
        self.waited[eng][sem] = val
        self.q[eng].append(("wait", sem, val))

    def close(self, eng, sem, n=1):
        for i in range(len(self.q[eng]) - 1, -1, -1):
            if self.q[eng][i][0] == "op":
                _, fn, inc0, _, _ = self.q[eng][i]
                assert inc0 is None, "op already increments"
                self.cnt[sem] += n
                self.q[eng][i] = ("op", fn, sem, self.cnt[sem], n)
                return (sem, self.cnt[sem])
        raise RuntimeError("no op to close")

    def dma_batch(self, eng, sem):
        self.wait(eng, (sem, self.cnt[sem]))

    def drain(self, eng):
        self.q[eng].append(("drain",))
        self.written[eng].clear()
        self.readset[eng].clear()

    def replay(self, eng_name, eng):
        for item in self.q[eng_name]:
            kind = item[0]
            if kind == "op":
                _, fn, inc, val, n = item
                inst = fn(eng)
                if inc is not None:
                    inst.then_inc(self.sems[inc], n)
            elif kind == "wait":
                _, sem, val = item
                eng.wait_ge(self.sems[sem], val)
            elif kind == "drain":
                eng.drain()

    def build(self):
        with self.nc.Block() as block:
            for e in ENGINES:
                if not self.q[e]:
                    continue
                getattr(block, e)(lambda eng, e=e: self.replay(e, eng))


class Mem:

    def __init__(self, nc):
        self.nc = nc
        self.tensors = {}

    def at(self, name, shape, dtype, off):
        size = int(np.prod(shape[1:])) * mybir.dt.size(dtype)
        assert SBUF_BASE + off + size <= SBUF_END, (name, off, size)
        assert (SBUF_BASE + off) % 32 == 0, (name, off)
        t = self.nc.alloc_sbuf_tensor_at(name, list(shape), dtype,
                                         offset=SBUF_BASE + off)
        self.tensors[name] = t
        return t


def build_program(T=2048):
    CH = 8
    D = CH * 128
    SL = T // 8
    NSL = 8
    QBL = T // 4
    NQB = 4
    KT = T // 128
    H = 16
    inv_d = 1.0 / D

    nc = bass.Bass("TRN2", target_bir_lowering=False, debug=False)

    def din(name, shape):
        return nc.dram_tensor(name, list(shape), F32, kind="ExternalInput").ap()

    xT = din("xT", [D, T])
    Wqkv = din("Wqkv", [D, 3 * D])
    bqkv = din("bqkv", [3 * D])
    Aq = din("Aq", [D, 8])
    Bq = din("Bq", [8, 3 * D])
    Wproj = din("Wproj", [D, D])
    bproj = din("bproj", [D])
    Apj = din("Apj", [D, 8])
    Bpj = din("Bpj", [8, D])
    cw1 = din("cw1", [D, D])
    cb1 = din("cb1", [D])
    cw2 = din("cw2", [D, D // 2])
    cb2 = din("cb2", [D // 2])
    cw3 = din("cw3", [D // 2, D // 4])
    cb3 = din("cb3", [D // 4])
    cw4 = din("cw4", [D // 4, 1])
    cb4 = din("cb4", [1])
    gw1 = din("gw1", [D, D // 2])
    gb1 = din("gb1", [D // 2])
    gw2 = din("gw2", [D // 2, D // 4])
    gb2 = din("gb2", [D // 4])
    gw3 = din("gw3", [D // 4, 1])
    gb3 = din("gb3", [1])
    ar_in = din("ar", [1])
    y = nc.dram_tensor("y", [D, T], F32, kind="ExternalOutput").ap()
    wph = nc.dram_tensor("wph", [CH, 128, CH * 128], BF16, kind="Internal").ap()

    M = Mem(nc)
    f32b, bf16b = 4, 2
    off = 0
    xr = M.at("xr", [128, CH * T], F32, off); off += CH * T * f32b
    A0 = off
    off += CH * T * f32b
    B0 = off
    off += 37024
    C0 = off

    pingA = M.at("pingA", [128, CH * SL], F32, A0)
    pongA = M.at("pongA", [128, CH * SL], F32, A0 + CH * SL * f32b)
    pingB = M.at("pingB", [128, CH * SL], F32, A0 + 2 * CH * SL * f32b)
    pongB = M.at("pongB", [128, CH * SL], F32, A0 + 3 * CH * SL * f32b)
    a_ffn = A0 + 4 * CH * SL * f32b
    srcbA0 = M.at("srcbA0", [128, CH * SL], BF16, a_ffn)
    srcbA1 = M.at("srcbA1", [128, CH * SL], BF16, a_ffn + CH * SL * bf16b)
    srcbB0 = M.at("srcbB0", [128, CH * SL], BF16, a_ffn + 2 * CH * SL * bf16b)
    srcbB1 = M.at("srcbB1", [128, CH * SL], BF16, a_ffn + 3 * CH * SL * bf16b)
    b_ffn = a_ffn + 4 * CH * SL * bf16b
    h1b = M.at("h1b", [128, CH * SL], BF16, b_ffn)
    h2b = M.at("h2b", [128, (CH // 2) * SL], BF16, b_ffn + CH * SL * bf16b)
    h3b = M.at("h3b", [128, (CH // 4) * SL], BF16,
               b_ffn + CH * SL * bf16b + (CH // 2) * SL * bf16b)
    assert b_ffn + (CH + CH // 2 + CH // 4) * SL * bf16b <= B0
    xbf = M.at("xbf", [128, T], BF16, a_ffn)
    xbf2 = M.at("xbf2", [128, T], BF16, a_ffn + T * bf16b)
    xsq = M.at("xsq", [128, T], BF16, a_ffn + 2 * T * bf16b)
    xsq2 = M.at("xsq2", [128, T], BF16, a_ffn + 3 * T * bf16b)
    assert a_ffn + 4 * T * bf16b <= B0
    gtmp = M.at("gtmp", [128, T], F32, A0)

    bo = B0
    wcn1 = M.at("wcn1", [128, CH * D], BF16, bo); bo += CH * D * bf16b
    wcn2 = M.at("wcn2", [128, CH * (D // 2)], BF16, bo); bo += CH * (D // 2) * bf16b
    wcn3 = M.at("wcn3", [128, (CH // 2) * (D // 4)], BF16, bo)
    bo += (CH // 2) * (D // 4) * bf16b
    wcn4 = M.at("wcn4", [128, 2], BF16, bo); bo += 32
    wgc1 = M.at("wgc1", [128, CH * (D // 2)], BF16, bo); bo += CH * (D // 2) * bf16b
    wgc2 = M.at("wgc2", [128, (CH // 2) * (D // 4)], BF16, bo)
    bo += (CH // 2) * (D // 4) * bf16b
    wgc3 = M.at("wgc3", [128, 2], BF16, bo); bo += 32
    b1s = M.at("b1s", [128, CH], F32, bo); bo += 32
    b2s = M.at("b2s", [128, CH // 2], F32, bo); bo += 32
    b3s = M.at("b3s", [128, CH // 4], F32, bo); bo += 32
    assert bo - B0 <= 37024, bo - B0
    vaug = M.at("vaug", [128, KT * H * 65], BF16, B0)

    assert T >= 1024
    xn2 = M.at("xn2", [128, CH * T], BF16, A0)
    Wv_sb = M.at("Wv_sb", [128, CH * D], BF16, A0 + CH * T * bf16b)
    Ks = M.at("Ks2", [128, CH * T], BF16, A0 + CH * T * bf16b)
    ao = A0
    def alloA(name, shape, dtype):
        nonlocal ao
        t = M.at(name, shape, dtype, ao)
        ao += ((int(np.prod(shape[1:])) * mybir.dt.size(dtype) + 31) // 32) * 32
        return t
    attnq = alloA("attnq", [128, CH * QBL], BF16)
    attnq2 = alloA("attnq2", [128, CH * QBL], BF16)
    PT0 = alloA("PT0", [128, 2 * QBL], BF16)
    PT1 = alloA("PT1", [128, 2 * QBL], BF16)
    lrow0 = alloA("lrow0", [1, 2 * QBL], F32)
    lrow1 = alloA("lrow1", [1, 2 * QBL], F32)
    ARAW_OFF = ao
    araw0 = alloA("araw0", [64, 2 * QBL], BF16)
    araw1 = alloA("araw1", [64, 2 * QBL], BF16)
    qsq = M.at("qsq", [128, T], BF16, A0)
    qsq2 = M.at("qsq2", [128, T], BF16, ARAW_OFF)
    assert ao <= A0 + CH * T * bf16b

    co = C0
    def calloc(name, shape, dtype):
        nonlocal co
        t = M.at(name, shape, dtype, co)
        co += ((int(np.prod(shape[1:])) * mybir.dt.size(dtype) + 31) // 32) * 32
        return t

    sc8 = calloc("sc8", [1, 16], F32)
    meanb_col = calloc("meanb_col", [128, 1], F32)
    ones_f = calloc("ones_f", [1, 128], F32)
    ones_b = calloc("ones_b", [1, 512], BF16)
    onesc_b = calloc("onesc_b", [128, 1], BF16)
    selq = calloc("selq", [128, 128], BF16)
    biasH = calloc("biasH", [128, H], F32)
    biasP = calloc("biasP", [128, H // 2], F32)
    sq32 = calloc("sq32", [32, 32], F32)
    sq32b = calloc("sq32b", [32, 32], F32)
    cpool = calloc("cpool", [128, 8], F32)
    P0 = co

    po = P0
    def alloP(name, shape, dtype):
        nonlocal po
        t = M.at(name, shape, dtype, po)
        po += ((int(np.prod(shape[1:])) * mybir.dt.size(dtype) + 31) // 32) * 32
        return t
    a_row = alloP("a_row", [1, T], F32)
    b_row = alloP("b_row", [1, T], F32)
    atmp = alloP("atmp", [128, SL], F32)
    v_row = alloP("v_row", [1, T], F32)
    tf_row = alloP("tf_row", [1, SL], F32)
    cpl_row = alloP("cpl_row", [1, SL], F32)
    ncpl_row = alloP("ncpl_row", [1, SL], F32)
    cplB_row = alloP("cplB_row", [1, SL], F32)
    ttmp0 = alloP("ttmp0", [128, SL], F32)
    ttmp1 = alloP("ttmp1", [128, SL], F32)
    gcsum = alloP("gcsum", [128, CH], F32)
    gcbf = alloP("gcbf", [128, CH], BF16)
    hg1p_sb = alloP("hg1bf", [128, 4], BF16)
    hg2p_sb = alloP("hg2bf", [128, 2], BF16)
    gb1s = alloP("gb1s", [128, 4], F32)
    gb2s = alloP("gb2s", [128, 2], F32)
    P_FFN_END = po

    po = P0
    Qs = alloP("Qs", [128, CH * T], BF16)
    tq = alloP("tq", [8, T], BF16)
    AV0 = po
    wqt = alloP("wqt", [128, CH * 128], BF16)
    wkt = alloP("wkt", [128, CH * 128], BF16)
    bqqk0 = alloP("bqqk0", [8, 256], BF16)
    brqk0 = alloP("brqk0", [1, 256], BF16)
    P_ATT1_END = po
    po = AV0
    Aq_sb = alloP("Aq_sb", [128, CH * 8], BF16)
    Bqv = alloP("Bqv", [8, D], BF16)
    bqv = alloP("bqv", [1, D], BF16)
    P_VT_END = po

    po = P0 + ((CH * T * bf16b + 31) // 32) * 32
    WPT_OFF = po
    wpt = alloP("wpt", [128, CH * 128], BF16)
    wpt3 = alloP("wpt3", [128, CH * 128], BF16)
    PT2 = M.at("PT2", [128, 2 * QBL], BF16, WPT_OFF + CH * 128 * bf16b)
    wpf = M.at("wpf", [128, CH * 128], F32, WPT_OFF)
    ApjT = alloP("ApjT", [32, CH * 128], BF16)
    aptmp = alloP("aptmp", [64, 128], BF16)
    BPSB_OFF = po
    Bp_sb = alloP("Bp_sb", [8, D], BF16)
    PT3 = M.at("PT3", [128, 2 * QBL], BF16, BPSB_OFF)
    wpt2 = M.at("wpt2", [128, CH * 128], BF16, B0 + 33280)
    wstage = M.at("wstage", [128, CH * 128], BF16, B0 + 33280)
    assert 33280 + CH * 128 * bf16b <= 37024
    bp_sl0 = alloP("bp_sl0", [1, 128], BF16)
    bp_sl1 = alloP("bp_sl1", [1, 128], BF16)
    bp_sl2 = alloP("bp_sl2", [1, 128], BF16)
    P_ATT2_END = po
    PLIM = SBUF_END - SBUF_BASE
    assert max(P_FFN_END, P_ATT1_END, P_VT_END, P_ATT2_END) <= PLIM, (
        P_FFN_END, P_ATT1_END, P_VT_END, P_ATT2_END, PLIM)

    SC_AR, SC_B1, SC_A1, SC_GT, SC_T0, SC_MB, SC_B4H, SC_GB3H = range(8)

    def sc(i):
        return sc8[0:1, i:i + 1]

    PS = nc.alloc_psum_tensor("PS", [128, 4096], F32)

    from concourse.hw_specs import get_activation_tables
    _tabs = list(get_activation_tables(nc.m.arch).values())
    NLE_IDX = next(i for i, fs in enumerate(_tabs)
                   if AF.Ln in fs and AF.Exp in fs)

    S = Sched(nc)

    def preload_set6():
        S.op("scalar", lambda e: e.add_instruction(
            mybir.InstLoadActFuncSet(
                name=nc.get_next_instruction_name(),
                act_func_set_id=NLE_IDX, ins=[], outs=[])))
    for s in ("hw", "sw", "sw2", "te", "ac", "ve", "gp"):
        S.sem(s)
    for c in range(CH):
        S.sem("x%d" % c)

    x_toks = {}
    for c in range(CH):
        S.op("sync", lambda e, c=c: e.dma_start(
            xr[:, c * T:(c + 1) * T], xT[c * 128:(c + 1) * 128, :]),
            inc="x%d" % c, n=16)
        x_toks[c] = S.tok("x%d" % c)

    def load_w_chunks(dst, src, ncols, nch, tag):
        for c in range(nch):
            S.op("gpsimd", lambda e, c=c, dst=dst, src=src, ncols=ncols: e.dma_start(
                dst[:, c * ncols:(c + 1) * ncols],
                src[c * 128:(c + 1) * 128, :]), inc="sw", n=16)
        return S.tok("sw")

    def load_bias_cols(dst, src, nch):
        S.dma_batch("sync", "hw")
        src_r = src.rearrange("(c p o) -> c p o", p=128, o=1)
        for c in range(nch):
            S.op("sync", lambda e, c=c, dst=dst, src_r=src_r: e.dma_start(
                dst[:, c:c + 1], src_r[c]), inc="hw", n=16)
        return S.tok("hw")

    def emit_ffn_weight_loads():
        S.dma_batch("gpsimd", "sw")
        load_w_chunks(wcn1, cw1, D, CH, "cw1")
        load_w_chunks(wcn2, cw2, D // 2, CH, "cw2")
        load_w_chunks(wcn3, cw3, D // 4, CH // 2, "cw3")
        for c in range(2):
            S.op("gpsimd", lambda e, c=c: e.dma_start(
                wcn4[:, c:c + 1], cw4.rearrange("(c p) o -> c p o", p=128)[c]),
                inc="sw", n=16)
        load_w_chunks(wgc1, gw1, D // 2, CH, "gw1")
        load_w_chunks(wgc2, gw2, D // 4, CH // 2, "gw2")
        for c in range(2):
            S.op("gpsimd", lambda e, c=c: e.dma_start(
                wgc3[:, c:c + 1], gw3.rearrange("(c p) o -> c p o", p=128)[c]),
                inc="sw", n=16)
        load_bias_cols(b1s, cb1, CH)
        load_bias_cols(b2s, cb2, CH // 2)
        load_bias_cols(b3s, cb3, CH // 4)
        load_bias_cols(gb1s, gb1, 4)
        load_bias_cols(gb2s, gb2, 2)
        return S.tok("sw"), S.tok("hw")

    ffnw_sw, ffnw_hw = emit_ffn_weight_loads()
    S.dma_batch("sync", "hw")
    S.op("sync", lambda e: e.dma_start(sc8[0:1, SC_AR:SC_AR + 1],
                                       ar_in.rearrange("(a o) -> a o", a=1)), inc="hw", n=16)
    S.op("sync", lambda e: e.dma_start(sc8[0:1, SC_T0:SC_T0 + 1],
                                       cb4.rearrange("(a o) -> a o", a=1)), inc="hw", n=16)
    S.op("sync", lambda e: e.dma_start(sc8[0:1, SC_MB:SC_MB + 1],
                                       gb3.rearrange("(a o) -> a o", a=1)), inc="hw", n=16)
    tiny_tok = S.tok("hw")

    for i, cv in enumerate((1e-5, 0.125, 2.0, 0.5, 1.0 / D, -0.5)):
        nc.const_aps.register(F32, cv, cpool[:, i:i + 1])
        S.op("vector", lambda e, i=i, cv=cv: e.memset(cpool[:, i:i + 1], cv),
             inc="ve", writes=("cpool",))
    S.op("vector", lambda e: e.memset(ones_f[:], 1.0), inc="ve", writes=("ones_f",))
    S.op("vector", lambda e: e.memset(ones_b[:], 1.0), inc="ve", writes=("ones_b",))
    S.op("vector", lambda e: e.memset(onesc_b[:], 1.0), inc="ve", writes=("onesc_b",))
    S.op("vector", lambda e: e.memset(sq32[:], 0.0), inc="ve", writes=("sq32",))
    S.op("vector", lambda e: e.memset(selq[:], 0.0), inc="ve", writes=("selq",))
    for c in range(CH):
        S.op("vector", lambda e, c=c: e.memset(
            selq[0:64, 16 * c + 2 * c:16 * c + 2 * c + 1], 1.0), writes=("selq",))
        S.op("vector", lambda e, c=c: e.memset(
            selq[64:128, 16 * c + 2 * c + 1:16 * c + 2 * c + 2], 1.0),
            inc="ve", writes=("selq",))
    consts_tok = S.tok("ve")

    S.wait("vector", tiny_tok)
    S.op("vector", lambda e: e.tensor_scalar_mul(sc(SC_B4H), sc(SC_T0), 0.5),
         inc="ve", reads=("sc8",), writes=("sc8",))
    S.op("vector", lambda e: e.tensor_scalar_mul(sc(SC_GB3H), sc(SC_MB), 0.5),
         inc="ve", reads=("sc8",), writes=("sc8",))

    def emit_ln(want_gc, gates=None):
        prev_mm = []
        for c in range(CH):
            xb = xbf if c % 2 == 0 else xbf2
            xq = xsq if c % 2 == 0 else xsq2
            if gates is not None:
                S.wait("vector", gates[c])
                S.wait("scalar", gates[c])
            if c >= 2:
                S.wait("vector", prev_mm[c - 2])
                S.wait("scalar", prev_mm[c - 2])
            S.op("vector", lambda e, c=c, xb=xb: e.tensor_copy(
                xb[:], xr[:, c * T:(c + 1) * T]), inc="ve",
                writes=(xb.name,))
            S.op("scalar", lambda e, c=c, xq=xq: e.activation(
                xq[:], xr[:, c * T:(c + 1) * T], AF.Square), inc="ac",
                writes=(xq.name,))
            S.wait("tensor", S.tok("ve"))
            S.wait("tensor", S.tok("ac"))
            for (o, n) in nslices(T):
                S.op("tensor", lambda e, c=c, o=o, n=n, xb=xb: e.matmul(
                    PS[0:1, o:o + n], onesc_b[:], xb[:, o:o + n],
                    start=(c == 0), stop=(c == CH - 1)))
                S.op("tensor", lambda e, c=c, o=o, n=n, xq=xq: e.matmul(
                    PS[0:1, 2048 + o:2048 + o + n], onesc_b[:], xq[:, o:o + n],
                    start=(c == 0), stop=(c == CH - 1)))
            prev_mm.append(S.close("tensor", "te"))
        sums_tok = S.tok("te")

        HT = T // 2
        S.wait("scalar", sums_tok)
        S.wait("vector", sums_tok)
        var_t = {}
        exp_t = {}
        for h in range(2):
            o = h * HT
            S.op("scalar", lambda e, o=o: e.activation(
                v_row[0:1, o:o + HT], PS[0:1, o:o + HT], AF.Square,
                scale=inv_d), inc="ac", writes=("v_row%d" % h,))
            sq_t = S.tok("ac")
            S.wait("vector", sq_t)
            S.op("vector", lambda e, o=o: e.scalar_tensor_tensor(
                v_row[0:1, o:o + HT], PS[0:1, 2048 + o:2048 + o + HT], inv_d,
                v_row[0:1, o:o + HT], ALU.mult, ALU.subtract), inc="ve",
                reads=("v_row%d" % h,), writes=("v_row%d" % h,))
            var_t[h] = S.tok("ve")
        for h in range(2):
            o = h * HT
            S.wait("scalar", var_t[h])
            S.op("scalar", lambda e, o=o: e.activation(
                v_row[0:1, o:o + HT], v_row[0:1, o:o + HT], AF.Ln,
                bias=1e-5), inc="ac", reads=("v_row%d" % h,),
                writes=("v_row%d" % h,))
            S.op("scalar", lambda e, o=o: e.activation(
                a_row[0:1, o:o + HT], v_row[0:1, o:o + HT], AF.Exp,
                scale=-0.5), inc="ac", reads=("v_row%d" % h,),
                writes=("a_row%d" % h,))
            exp_t[h] = S.tok("ac")
        for h in range(2):
            o = h * HT
            S.wait("vector", exp_t[h])
            S.op("vector", lambda e, o=o: e.scalar_tensor_tensor(
                b_row[0:1, o:o + HT], PS[0:1, o:o + HT], -inv_d,
                a_row[0:1, o:o + HT], ALU.mult, ALU.mult), inc="ve",
                reads=("a_row%d" % h,), writes=("b_row%d" % h, "b_row"))
        rows_tok = S.tok("ve")

        gc_tok = None
        if want_gc:
            S.op("vector", lambda e: e.reduce_sum(
                sc(SC_MB), b_row[:], axis=AX.X), inc="ve", reads=("b_row",),
                writes=("sc8",))
            S.op("vector", lambda e: e.tensor_scalar_mul(
                sc(SC_MB), sc(SC_MB), 1.0 / T), inc="ve", reads=("sc8",),
                writes=("sc8",))
            S.wait("tensor", S.tok("ve"))
            S.op("tensor", lambda e: e.matmul(
                PS[0:128, 3072:3073], ones_f[:], sc(SC_MB),
                start=True, stop=True), inc="te")
            S.wait("vector", S.tok("te"))
            S.op("vector", lambda e: e.tensor_copy(meanb_col[:],
                 PS[0:128, 3072:3073]), inc="ve", writes=("meanb_col",))
            S.wait("tensor", S.tok("ve"))
            for (o, n) in nslices(T):
                S.op("tensor", lambda e, o=o, n=n: e.matmul(
                    PS[0:128, 2048 + o:2048 + o + n], ones_f[:],
                    a_row[0:1, o:o + n], start=True, stop=True))
            abc_tok = S.close("tensor", "te")
            S.wait("vector", abc_tok)
            for c in range(CH):
                S.op("vector", lambda e, c=c: e.tensor_tensor(
                    gtmp[:], xr[:, c * T:(c + 1) * T], PS[0:128, 2048:2048 + T],
                    ALU.mult), inc="ve", reads=("xr%d" % c,), writes=("gtmp",))
                S.op("vector", lambda e, c=c: e.reduce_sum(
                    gcsum[:, c:c + 1], gtmp[:], axis=AX.X), inc="ve",
                    reads=("gtmp",), writes=("gcsum",))
            S.op("vector", lambda e: e.tensor_scalar(
                gcbf[:], gcsum[:], 1.0 / T, meanb_col[:], ALU.mult, ALU.add),
                inc="ve", reads=("gcsum", "meanb_col"), writes=("gcbf",))
            gc_tok = S.tok("ve")
        return rows_tok, gc_tok

    def emit_gcnet(gc_tok, w_tok, bias_tok):
        S.wait("tensor", gc_tok)
        S.wait("tensor", w_tok)
        S.wait("scalar", bias_tok)
        for mg in range(4):
            for kc in range(CH):
                S.op("tensor", lambda e, mg=mg, kc=kc: e.matmul(
                    PS[0:128, 3584 + mg:3584 + mg + 1],
                    wgc1[:, kc * (D // 2) + mg * 128:kc * (D // 2) + (mg + 1) * 128],
                    gcbf[:, kc:kc + 1], start=(kc == 0), stop=(kc == CH - 1)))
        t1 = S.close("tensor", "te")
        S.wait("scalar", t1)
        for mg in range(4):
            S.op("scalar", lambda e, mg=mg: e.activation(
                hg1p_sb[:, mg:mg + 1], PS[0:128, 3584 + mg:3584 + mg + 1],
                AF.Gelu, bias=gb1s[:, mg:mg + 1]), inc="ac",
                writes=("hg1bf",))
        S.wait("tensor", S.tok("ac"))
        for mg in range(2):
            for kc in range(4):
                S.op("tensor", lambda e, mg=mg, kc=kc: e.matmul(
                    PS[0:128, 3590 + mg:3590 + mg + 1],
                    wgc2[:, kc * (D // 4) + mg * 128:kc * (D // 4) + (mg + 1) * 128],
                    hg1p_sb[:, kc:kc + 1], start=(kc == 0), stop=(kc == 3)))
        t2 = S.close("tensor", "te")
        S.wait("scalar", t2)
        for mg in range(2):
            S.op("scalar", lambda e, mg=mg: e.activation(
                hg2p_sb[:, mg:mg + 1], PS[0:128, 3590 + mg:3590 + mg + 1],
                AF.Gelu, bias=gb2s[:, mg:mg + 1]), inc="ac", writes=("hg2bf",))
        S.wait("tensor", S.tok("ac"))
        for kc in range(2):
            S.op("tensor", lambda e, kc=kc: e.matmul(
                PS[0:1, 3594:3595], wgc3[:, kc:kc + 1], hg2p_sb[:, kc:kc + 1],
                start=(kc == 0), stop=(kc == 1)))
        t3 = S.close("tensor", "te")
        S.wait("scalar", t3)
        S.op("scalar", lambda e: e.activation(
            sc(SC_GT), PS[0:1, 3594:3595], AF.Tanh, bias=sc(SC_GB3H)[0:1, 0:1],
            scale=0.5), inc="ac", writes=("sc8",))
        S.wait("vector", S.tok("ac"))
        S.op("vector", lambda e: e.tensor_scalar_mul(sc(SC_B1), sc(SC_AR), 0.3),
             inc="ve", reads=("sc8",), writes=("sc8",))
        S.op("vector", lambda e: e.tensor_tensor(
            sc(SC_T0), sc(SC_AR), sc(SC_GT), ALU.mult), inc="ve",
            reads=("sc8",), writes=("sc8",))
        S.op("vector", lambda e: e.tensor_scalar(
            sc(SC_A1), sc(SC_T0), 0.7, 0.1, ALU.mult, ALU.add), inc="ve",
            reads=("sc8",), writes=("sc8",))
        return S.tok("ve")

    def emit_ffn(rows_tok, ab_tok, store_cb=None):
        S.wait("tensor", ab_tok)

        def emit_apply(ctx, bank_guard):
            so = ctx["so"]
            S.wait("tensor", rows_tok)
            if bank_guard is not None:
                S.wait("tensor", bank_guard)
            for (o, n) in nslices(SL):
                S.op("tensor", lambda e, o=o, n=n, so=so: e.matmul(
                    PS[0:128, 2560 + o:2560 + o + n], ones_f[:],
                    a_row[0:1, so + o:so + o + n], start=True, stop=True))
                S.op("tensor", lambda e, o=o, n=n, so=so: e.matmul(
                    PS[0:128, 2560 + SL + o:2560 + SL + o + n], ones_f[:],
                    b_row[0:1, so + o:so + o + n], start=True, stop=True))
            ab_bc = S.close("tensor", "te")
            S.wait("vector", ab_bc)
            ping = ctx["state"]
            cast_toks = {}
            sb = ctx["sb"][0]
            for c in range(CH):
                S.op("vector", lambda e, c=c, so=so: e.tensor_tensor(
                    atmp[:], xr[:, c * T + so:c * T + so + SL],
                    PS[0:128, 2560:2560 + SL], ALU.mult), inc="ve",
                    writes=("atmp",))
                S.op("vector", lambda e, c=c, ping=ping: e.tensor_tensor(
                    ping[:, c * SL:(c + 1) * SL], atmp[:],
                    PS[0:128, 2560 + SL:2560 + 2 * SL], ALU.add), inc="ve",
                    reads=("atmp",), writes=(ping.name + str(c),))
                ap_tok = S.tok("ve")
                S.wait("gpsimd", ap_tok)
                cast_toks[c] = S.op("gpsimd", lambda e, c=c, ping=ping, sb=sb: e.tensor_copy(
                    sb[:, c * SL:(c + 1) * SL], ping[:, c * SL:(c + 1) * SL]),
                    inc="gp", reads=(ping.name + str(c),),
                    writes=("%sc%d" % (sb.name, c),))
            ctx["cast_toks"] = cast_toks
            return S.tok("ve")

        def emit_sweeps(ctx, cyc, mid=None):
            srcb = ctx["sb"][cyc % 2]
            cast_toks = ctx["cast_toks"]
            h1e = {}
            for kc in range(CH):
                S.wait("tensor", cast_toks[kc])
                for mi in range(4):
                    S.op("tensor", lambda e, mi=mi, kc=kc, srcb=srcb: e.matmul(
                        PS[0:128, mi * 512:mi * 512 + SL],
                        wcn1[:, kc * D + mi * 128:kc * D + (mi + 1) * 128],
                        srcb[:, kc * SL:(kc + 1) * SL],
                        start=(kc == 0), stop=(kc == CH - 1)))
            sw_tok = S.close("tensor", "te")
            S.wait("scalar", sw_tok)
            for mi in range(4):
                h1e[mi] = S.op("scalar", lambda e, mi=mi: e.activation(
                    h1b[:, mi * SL:(mi + 1) * SL],
                    PS[0:128, mi * 512:mi * 512 + SL],
                    AF.Gelu, bias=b1s[:, mi:mi + 1]), inc="ac",
                    writes=("h1b%d" % mi,))
            if mid is not None:
                mid()
            for mi in range(4):
                m = 4 + mi
                S.wait("tensor", h1e[mi])
                for kc in range(CH):
                    S.op("tensor", lambda e, m=m, mi=mi, kc=kc, srcb=srcb: e.matmul(
                        PS[0:128, mi * 512:mi * 512 + SL],
                        wcn1[:, kc * D + m * 128:kc * D + (m + 1) * 128],
                        srcb[:, kc * SL:(kc + 1) * SL],
                        start=(kc == 0), stop=(kc == CH - 1)))
                t = S.close("tensor", "te")
                S.wait("scalar", t)
                h1e[m] = S.op("scalar", lambda e, m=m, mi=mi: e.activation(
                    h1b[:, m * SL:(m + 1) * SL],
                    PS[0:128, mi * 512:mi * 512 + SL],
                    AF.Gelu, bias=b1s[:, m:m + 1]), inc="ac",
                    writes=("h1b%d" % m,))
            h2e = {}
            for m in range(CH // 2):
                S.wait("tensor", h1e[m + 4])
                for kc in range(CH):
                    S.wait("tensor", h1e[kc])
                    S.op("tensor", lambda e, m=m, kc=kc: e.matmul(
                        PS[0:128, m * 512:m * 512 + SL],
                        wcn2[:, kc * (D // 2) + m * 128:kc * (D // 2) + (m + 1) * 128],
                        h1b[:, kc * SL:(kc + 1) * SL],
                        start=(kc == 0), stop=(kc == CH - 1)))
                mm_tok = S.close("tensor", "te")
                S.wait("scalar", mm_tok)
                h2e[m] = S.op("scalar", lambda e, m=m: e.activation(
                    h2b[:, m * SL:(m + 1) * SL], PS[0:128, m * 512:m * 512 + SL],
                    AF.Gelu, bias=b2s[:, m:m + 1]), inc="ac", writes=("h2b%d" % m,))
            h3e = {}
            for m in range(CH // 4):
                S.wait("tensor", h2e[m])
                for kc in range(CH // 2):
                    S.wait("tensor", h2e[kc])
                    S.op("tensor", lambda e, m=m, kc=kc: e.matmul(
                        PS[0:128, m * 512:m * 512 + SL],
                        wcn3[:, kc * (D // 4) + m * 128:kc * (D // 4) + (m + 1) * 128],
                        h2b[:, kc * SL:(kc + 1) * SL],
                        start=(kc == 0), stop=(kc == CH // 2 - 1)))
                mm_tok = S.close("tensor", "te")
                S.wait("scalar", mm_tok)
                h3e[m] = S.op("scalar", lambda e, m=m: e.activation(
                    h3b[:, m * SL:(m + 1) * SL], PS[0:128, m * 512:m * 512 + SL],
                    AF.Gelu, bias=b3s[:, m:m + 1]), inc="ac", writes=("h3b%d" % m,))
            for kc in range(CH // 4):
                S.wait("tensor", h3e[kc])
                S.op("tensor", lambda e, kc=kc: e.matmul(
                    PS[0:1, 3072:3072 + SL], wcn4[:, kc:kc + 1],
                    h3b[:, kc * SL:(kc + 1) * SL],
                    start=(kc == 0), stop=(kc == CH // 4 - 1)))
            tf_tok = S.close("tensor", "te")
            S.wait("scalar", tf_tok)
            S.wait("scalar", ab_tok)
            tfr, cpr = ctx["tfr"], ctx["cpr"]
            S.op("scalar", lambda e, tfr=tfr: e.activation(
                tfr[:], PS[0:1, 3072:3072 + SL], AF.Tanh,
                bias=sc(SC_B4H)[0:1, 0:1], scale=0.5), inc="ac",
                writes=(tfr.name,))
            S.op("scalar", lambda e, tfr=tfr, cpr=cpr: e.activation(
                cpr[:], tfr[:], AF.Identity, bias=sc(SC_A1)[0:1, 0:1],
                scale=sc(SC_B1)[0:1, 0:1]), inc="ac",
                reads=(tfr.name, "sc8"), writes=(cpr.name,))
            ctx["cplv"] = S.tok("ac")

        def emit_tail(ctx, cyc):
            so = ctx["so"]
            state, other = ctx["state"], ctx["other"]
            cpr = ctx["cpr"]
            cb = ctx["cb"]
            S.wait("tensor", ctx["cplv"])
            S.wait("tensor", S.tok("ve"))
            for (o, n) in nslices(SL):
                S.op("tensor", lambda e, o=o, n=n, cpr=cpr, cb=cb: e.matmul(
                    PS[0:128, cb + o:cb + o + n], ones_f[:],
                    cpr[0:1, o:o + n], start=True, stop=True))
            bc_tok = S.close("tensor", "te")
            S.wait("vector", bc_tok)
            S.wait("gpsimd", S.tok("te"))
            cast_toks = {}
            for c in range(CH):
                pc = (c + 4) % CH
                op1 = ALU.add if c in (0, 1, 6, 7) else ALU.subtract
                tt = ttmp0 if c % 2 == 0 else ttmp1
                S.op("vector", lambda e, c=c, pc=pc, tt=tt, cb=cb,
                     state=state: e.tensor_tensor(
                    tt[:], state[:, pc * SL:(pc + 1) * SL],
                    PS[0:128, cb:cb + SL], ALU.mult), inc="ve",
                    reads=("%s%d" % (state.name, pc),), writes=(tt.name,))
                tw = S.op("vector", lambda e, c=c, tt=tt, op1=op1, state=state,
                     other=other: e.tensor_tensor(
                    other[:, c * SL:(c + 1) * SL],
                    state[:, c * SL:(c + 1) * SL], tt[:], op1),
                    inc="ve", reads=(tt.name, "%s%d" % (state.name, c)),
                    writes=("%s%d" % (other.name, c),))
                if cyc < 2:
                    nsrcb = ctx["sb"][(cyc + 1) % 2]
                    S.wait("gpsimd", tw)
                    cast_toks[c] = S.op("gpsimd", lambda e, c=c, other=other,
                                        nsrcb=nsrcb: e.tensor_copy(
                        nsrcb[:, c * SL:(c + 1) * SL],
                        other[:, c * SL:(c + 1) * SL]), inc="gp",
                        reads=("%s%d" % (other.name, c),),
                        writes=("%sc%d" % (nsrcb.name, c),))
                else:
                    S.op("vector", lambda e, c=c, so=so, other=other: e.scalar_tensor_tensor(
                        xr[:, c * T + so:c * T + so + SL],
                        other[:, c * SL:(c + 1) * SL], 0.5,
                        xr[:, c * T + so:c * T + so + SL], ALU.mult, ALU.add),
                        inc="ve", reads=("%s%d" % (other.name, c), "xr%d" % c),
                        writes=("xr%d" % c,))
            if cyc == 2 and store_cb is not None:
                store_cb(so, S.tok("ve"))
            ctx["state"], ctx["other"] = other, state
            ctx["cast_toks"] = cast_toks

        ctxs = {0: dict(state=pingA, other=pongA, sb=(srcbA0, srcbA1),
                        tfr=tf_row, cpr=cpl_row, cb=2048),
                1: dict(state=pingB, other=pongB, sb=(srcbB0, srcbB1),
                        tfr=ncpl_row, cpr=cplB_row, cb=3584)}
        last_ab = None

        def do_apply(s):
            nonlocal last_ab
            ctx = ctxs[s % 2]
            ctx["so"] = s * SL
            ctx["state"], ctx["other"] = (
                (pingA, pongA) if s % 2 == 0 else (pingB, pongB))
            S.wait("vector", S.tok("gp"))
            S.wait("gpsimd", S.tok("te"))
            emit_apply(ctx, last_ab)
            last_ab = S.tok("ve")

        do_apply(0)
        do_apply(1)
        prev = None
        for wave in range(NSL // 2):
            a, b = 2 * wave, 2 * wave + 1
            for cyc in range(3):
                emit_sweeps(ctxs[0], cyc, mid=prev)
                prev = (lambda c=cyc: emit_tail(ctxs[0], c))
                if cyc == 2:
                    ta = prev
                    def mk1(ta=ta, s=a + 2):
                        def f():
                            ta()
                            if s < NSL:
                                do_apply(s)
                        return f
                    prev = mk1()
                emit_sweeps(ctxs[1], cyc, mid=prev)
                prev = (lambda c=cyc: emit_tail(ctxs[1], c))
            if b + 2 < NSL:
                tb = prev
                def mk2(tb=tb, s=b + 2):
                    def f():
                        tb()
                        do_apply(s)
                    return f
                prev = mk2()
        prev()
        return S.tok("ve")

    S.wait("vector", consts_tok)
    S.wait("tensor", consts_tok)
    preload_set6()
    rows1, gc1 = emit_ln(want_gc=True, gates=x_toks)
    S.op("scalar", lambda e: e.preload_act_table(AF.Gelu))
    ab1 = emit_gcnet(gc1, ffnw_sw, ffnw_hw)
    S.wait("tensor", ffnw_hw)
    S.wait("scalar", ffnw_hw)
    ffn1_done = emit_ffn(rows1, ab1)
    ffn1_gp = S.tok("gp")

    S.drain("vector")
    S.drain("scalar")
    S.wait("vector", ffn1_done)
    S.wait("vector", ffn1_gp)
    S.wait("scalar", S.tok("ve"))
    S.wait("scalar", ffn1_gp)
    S.wait("tensor", ffn1_gp)
    preload_set6()
    rows2, _ = emit_ln(want_gc=False)
    rows2_ac = S.tok("ac")

    S.wait("gpsimd", rows2)
    S.wait("gpsimd", rows2_ac)
    S.dma_batch("gpsimd", "sw")
    for c in range(CH):
        S.op("gpsimd", lambda e, c=c: e.dma_start(
            Aq_sb[:, c * 8:(c + 1) * 8], Aq[c * 128:(c + 1) * 128, :]),
            inc="sw", n=16)
    S.op("gpsimd", lambda e: e.dma_start(
        Bqv[:], Bq[:, 2 * D:3 * D]), inc="sw", n=16)
    S.op("gpsimd", lambda e: e.dma_start(
        bqv[:], bqkv.rearrange("(a o) -> a o", a=1)[0:1, 2 * D:3 * D]),
        inc="sw", n=16)
    for c in range(CH):
        S.op("gpsimd", lambda e, c=c: e.dma_start(
            Wv_sb[:, c * D:(c + 1) * D], Wqkv[c * 128:(c + 1) * 128, 2 * D:3 * D]),
            inc="sw", n=16)
    attw_tok = S.tok("sw")

    S.drain("vector")
    S.op("vector", lambda e: e.memset(
        vaug[:].rearrange("p (k h o) -> p (k h) o", h=H, o=65)[:, :, 64:65], 1.0),
        inc="ve", writes=("vaug_ones",))

    ap_toks = {}
    tq_ev = {}
    prev_vt = {}

    def emit_tq(s):
        so = s * SL
        S.wait("tensor", ap_toks[s])
        S.wait("tensor", attw_tok)
        if s >= 1:
            S.wait("tensor", tq_ev[s - 1])
        for kc in range(CH):
            S.op("tensor", lambda e, so=so, kc=kc: e.matmul(
                PS[0:8, 2048:2048 + SL], Aq_sb[:, kc * 8:(kc + 1) * 8],
                xn2[:, kc * T + so:kc * T + so + SL],
                start=(kc == 0), stop=(kc == CH - 1)))
        tq_mm = S.close("tensor", "te")
        S.wait("scalar", tq_mm)
        tq_ev[s] = S.op("scalar", lambda e, so=so: e.activation(
            tq[0:8, so:so + SL], PS[0:8, 2048:2048 + SL],
            AF.Copy, scale=2.0), inc="ac", writes=("tq",))

    def emit_vtok(kt):
        pc0 = 0 if kt % 2 == 0 else 1024
        S.wait("tensor", tq_ev[(kt * 128) // SL])
        if kt >= 2:
            S.wait("tensor", prev_vt[kt - 2])
        for j, (o, n) in enumerate(nslices(D)):
            for kc in range(CH):
                S.op("tensor", lambda e, kt=kt, kc=kc, o=o, n=n, pc0=pc0: e.matmul(
                    PS[0:128, pc0 + o:pc0 + o + n],
                    xn2[:, kc * T + kt * 128:kc * T + (kt + 1) * 128],
                    Wv_sb[:, kc * D + o:kc * D + o + n],
                    start=(kc == 0), stop=False))
            S.op("tensor", lambda e, kt=kt, o=o, n=n, pc0=pc0: e.matmul(
                PS[0:128, pc0 + o:pc0 + o + n],
                tq[0:8, kt * 128:(kt + 1) * 128],
                Bqv[0:8, o:o + n], start=False, stop=False))
            S.op("tensor", lambda e, kt=kt, o=o, n=n, pc0=pc0: e.matmul(
                PS[0:128, pc0 + o:pc0 + o + n],
                ones_b[0:1, 0:128], bqv[0:1, o:o + n],
                start=False, stop=True))
        vt_mm = S.close("tensor", "te")
        S.wait("vector", vt_mm)
        prev_vt[kt] = S.op("vector", lambda e, kt=kt, pc0=pc0: e.tensor_copy(
            vaug[:].rearrange("p (k h o) -> p k h o", h=H, o=65)[:, kt, :, 0:64],
            PS[0:128, pc0:pc0 + D].rearrange("p (h o) -> p h o", o=64)),
            inc="ve", writes=("vaug%d" % kt,))

    prev = None
    KT_PER_SL = SL // 128
    for s in range(NSL):
        so = s * SL
        S.wait("tensor", rows2)
        if prev is not None:
            S.wait("tensor", prev)
        for (o, n) in nslices(SL):
            S.op("tensor", lambda e, o=o, n=n, so=so: e.matmul(
                PS[0:128, 2560 + o:2560 + o + n], ones_f[:],
                a_row[0:1, so + o:so + o + n], start=True, stop=True))
            S.op("tensor", lambda e, o=o, n=n, so=so: e.matmul(
                PS[0:128, 3072 + o:3072 + o + n], ones_f[:],
                b_row[0:1, so + o:so + o + n], start=True, stop=True))
        ab_bc = S.close("tensor", "te")
        S.wait("vector", ab_bc)
        for c in range(CH):
            S.op("vector", lambda e, c=c, so=so: e.tensor_tensor(
                atmp[:], xr[:, c * T + so:c * T + so + SL],
                PS[0:128, 2560:2560 + SL], ALU.mult), inc="ve", writes=("atmp",))
            S.op("vector", lambda e, c=c, so=so: e.tensor_tensor(
                xn2[:, c * T + so:c * T + so + SL], atmp[:],
                PS[0:128, 3072:3072 + SL], ALU.add), inc="ve",
                reads=("atmp",), writes=("xn2_%d" % c,))
        ap_toks[s] = prev = S.tok("ve")
        if s >= 1:
            emit_tq(s - 1)
        if s >= 2:
            for k in range(KT_PER_SL):
                emit_vtok(KT_PER_SL * (s - 2) + k)
    emit_tq(NSL - 1)
    for kt in range(KT_PER_SL * (NSL - 2), KT):
        emit_vtok(kt)
    vaug_tok = S.tok("ve")

    S.wait("tensor", vaug_tok)
    bq_row = bqkv.rearrange("(a o) -> a o", a=1)
    Wqc = Wqkv.rearrange("(kc q) m -> q kc m", q=128)
    Bq3 = Bq.rearrange("r (b m) -> r b m", b=3)
    br3 = bq_row.rearrange("a (b m) -> a b m", b=3)
    qk_q_toks = {}
    qk_k_toks = {}
    for p in range(CH):
        if p == 0:
            S.wait("gpsimd", S.tok("te"))
        else:
            S.wait("gpsimd", qk_q_toks[p - 1])
        S.dma_batch("gpsimd", "sw2")
        S.op("gpsimd", lambda e, p=p: e.dma_start(
            wqt[:].rearrange("q (kc m) -> q kc m", m=128),
            Wqc[:, :, p * 128:(p + 1) * 128]), inc="sw2", n=16)
        wq_tok = S.tok("sw2")
        if p > 0:
            S.wait("gpsimd", qk_k_toks[p - 1])
        S.dma_batch("gpsimd", "sw")
        S.op("gpsimd", lambda e, p=p: e.dma_start(
            wkt[:].rearrange("q (kc m) -> q kc m", m=128),
            Wqc[:, :, D + p * 128:D + (p + 1) * 128]), inc="sw", n=16)
        S.op("gpsimd", lambda e, p=p: e.dma_start(
            bqqk0[:].rearrange("r (b m) -> r b m", b=2),
            Bq3[:, 0:2, p * 128:(p + 1) * 128]), inc="sw", n=16)
        S.op("gpsimd", lambda e, p=p: e.dma_start(
            brqk0[:].rearrange("a (b m) -> a b m", b=2),
            br3[:, 0:2, p * 128:(p + 1) * 128]), inc="sw", n=16)
        wtok = S.tok("sw")
        if p > 0:
            S.wait("tensor", S.tok("ve"))
        for which, bsl in (("q", 0), ("k", 1)):
            S.wait("tensor", wq_tok if which == "q" else wtok)
            blo = bqqk0[0:8, bsl * 128:(bsl + 1) * 128]
            bro = brqk0[0:1, bsl * 128:(bsl + 1) * 128]
            wt = wqt if which == "q" else wkt
            ps0 = 0 if which == "q" else 2048
            for (o, n) in nslices(T):
                for kc in range(CH):
                    S.op("tensor", lambda e, kc=kc, o=o, n=n, wt=wt, ps0=ps0: e.matmul(
                        PS[0:128, ps0 + o:ps0 + o + n],
                        wt[:, kc * 128:(kc + 1) * 128],
                        xn2[:, kc * T + o:kc * T + o + n],
                        start=(kc == 0), stop=False))
                S.wait("tensor", wtok)
                S.op("tensor", lambda e, o=o, n=n, blo=blo, ps0=ps0: e.matmul(
                    PS[0:128, ps0 + o:ps0 + o + n],
                    blo, tq[0:8, o:o + n], start=False, stop=False))
                S.op("tensor", lambda e, o=o, n=n, bro=bro, ps0=ps0: e.matmul(
                    PS[0:128, ps0 + o:ps0 + o + n],
                    bro, ones_b[0:1, 0:n], start=False, stop=True))
            mm_tok = S.close("tensor", "te")
            if which == "q":
                qk_q_toks[p] = mm_tok
            else:
                qk_k_toks[p] = mm_tok
            S.wait("vector", mm_tok)
            dst = Qs if which == "q" else Ks
            S.op("vector", lambda e, p=p, dst=dst, ps0=ps0: e.tensor_copy(
                dst[:, p * T:(p + 1) * T], PS[0:128, ps0:ps0 + T]), inc="ve",
                writes=("%s%d" % (dst.name, p),))
    qk_tok = S.tok("ve")

    S.wait("gpsimd", qk_tok)
    S.dma_batch("gpsimd", "sw")
    for c in range(CH):
        S.op("gpsimd", lambda e, c=c: e.dma_start(
            wstage[:, c * 8:(c + 1) * 8], Apj[c * 128:(c + 1) * 128, :]),
            inc="sw", n=16)
    S.op("gpsimd", lambda e: e.dma_start(Bp_sb[:], Bpj[:]), inc="sw", n=16)
    projw_tok = S.tok("sw")
    S.wait("vector", projw_tok)
    for w in range(2):
        for g in range(4):
            S.op("vector", lambda e, w=w, g=g: e.transpose(
                aptmp[w * 32:(w + 1) * 32, g * 32:(g + 1) * 32],
                wstage[g * 32:(g + 1) * 32, w * 32:(w + 1) * 32]), inc="ve",
                reads=("wstage",), writes=("aptmp",))
    for c in range(CH):
        mask = [8 * (c % 4) + min(i, 7) for i in range(32)]
        S.op("vector", lambda e, c=c, mask=mask: e.stream_shuffle(
            ApjT[0:32, c * 128:(c + 1) * 128],
            aptmp[(c // 4) * 32:(c // 4) * 32 + 32, 0:128], mask), inc="ve",
            reads=("aptmp",), writes=("ApjT",))
    apjt_ve = S.tok("ve")
    Wpc = Wproj.rearrange("(kc q) m -> q kc m", q=128)
    wpf_toks = {}
    wb_last = [None]
    wst_tok = [None]

    def emit_wpf_dma(m):
        if wb_last[0] is not None:
            S.wait("sync", wb_last[0])
        S.dma_batch("sync", "hw")
        S.op("sync", lambda e, m=m: e.dma_start(
            wpf[:].rearrange("q (kc m) -> q kc m", m=128),
            Wpc[:, :, m * 128:(m + 1) * 128]), inc="hw", n=16)
        wpf_toks[m] = S.tok("hw")

    S.wait("sync", S.tok("te"))
    emit_wpf_dma(0)
    bp_row = bproj.rearrange("(a o) -> a o", a=1)

    def emit_wbuild(m):
        S.wait("tensor", wpf_toks[m])
        S.wait("tensor", projw_tok)
        S.wait("tensor", apjt_ve)
        if wb_last[0] is not None:
            S.wait("tensor", wb_last[0])
        if wst_tok[0] is not None:
            S.wait("vector", wst_tok[0])
        vt = {}
        for kc in range(CH):
            c0 = 2048 + (kc % 2) * 512 + ((kc // 2) % 2) * 128
            if kc >= 2:
                S.wait("tensor", vt[kc - 2])
            S.op("tensor", lambda e, kc=kc, c0=c0, m=m: e.matmul(
                PS[0:128, c0:c0 + 128], ApjT[0:8, kc * 128:(kc + 1) * 128],
                Bp_sb[:, m * 128:(m + 1) * 128], start=True, stop=True))
            mm = S.close("tensor", "te")
            S.wait("vector", mm)
            vt[kc] = S.op("vector", lambda e, kc=kc, c0=c0: e.scalar_tensor_tensor(
                wstage[:, kc * 128:(kc + 1) * 128], PS[0:128, c0:c0 + 128],
                2.0, wpf[:, kc * 128:(kc + 1) * 128], ALU.mult, ALU.add),
                inc="ve", reads=("wpf",), writes=("wstage",))
        wb_last[0] = S.tok("ve")
        S.wait("gpsimd", wb_last[0])
        S.dma_batch("gpsimd", "sw2")
        S.op("gpsimd", lambda e, m=m: e.dma_start(wph[m], wstage[:]),
             inc="sw2", n=16)
        wst_tok[0] = S.tok("sw2")
        if m + 1 < CH:
            emit_wpf_dma(m + 1)

    S.drain("vector")
    S.drain("scalar")
    S.wait("scalar", qk_tok)
    bounds_mm = []
    slot = 0
    for which, src in (("q", Qs), ("k", Ks)):
        S.wait("tensor", S.tok("ve"))
        for c in range(CH):
            qb_ = qsq if c % 2 == 0 else qsq2
            if c % 2 == 0:
                S.op("scalar", lambda e, c=c, src=src, qb_=qb_: e.activation(
                    qb_[:], src[:, c * T:(c + 1) * T], AF.Square), inc="ac",
                    writes=(qb_.name,))
                sq_tok = S.tok("ac")
            else:
                S.op("vector", lambda e, c=c, src=src, qb_=qb_: e.tensor_tensor(
                    qb_[:], src[:, c * T:(c + 1) * T],
                    src[:, c * T:(c + 1) * T], ALU.mult), inc="ve",
                    writes=(qb_.name,))
                sq_tok = S.tok("ve")
            if slot % 2 == 1:
                emit_wbuild(slot // 2)
            slot += 1
            S.wait("tensor", sq_tok)
            for (o, n) in nslices(T):
                S.op("tensor", lambda e, c=c, o=o, n=n, qb_=qb_: e.matmul(
                    PS[0:16, o:o + n], selq[:, 16 * c:16 * (c + 1)],
                    qb_[:, o:o + n], start=(c == 0), stop=(c == CH - 1)))
            mm = S.close("tensor", "te")
            if c % 2 == 0:
                S.wait("scalar", mm)
            else:
                S.wait("vector", mm)
        bounds_mm.append(S.tok("te"))
        S.wait("vector", bounds_mm[-1])
        col = 4 if which == "q" else 5
        S.op("vector", lambda e, col=col: e.reduce_max(
            sq32[0:16, col:col + 1], PS[0:16, 0:T], axis=AX.X),
            inc="ve", writes=("sq32",))
    S.op("vector", lambda e: e.tensor_tensor(
        sq32[0:16, 6:7], sq32[0:16, 4:5], sq32[0:16, 5:6], ALU.mult),
        inc="ve", reads=("sq32",), writes=("sq32",))
    S.wait("scalar", S.tok("ve"))
    S.op("scalar", lambda e: e.activation(
        sq32[0:16, 7:8], sq32[0:16, 6:7], AF.Ln), inc="ac",
        reads=("sq32",), writes=("sq32",))
    S.op("scalar", lambda e: e.activation(
        sq32[0:16, 7:8], sq32[0:16, 7:8], AF.Exp, scale=0.5), inc="ac",
        reads=("sq32",), writes=("sq32",))
    S.wait("vector", S.tok("ac"))
    S.op("vector", lambda e: e.tensor_scalar(
        sq32[0:16, 0:1], sq32[0:16, 7:8], -1.05 / 8.0, -0.125,
        ALU.mult, ALU.add), inc="ve", reads=("sq32",), writes=("sq32",))
    S.op("vector", lambda e: e.transpose(sq32b[0:32, 0:32], sq32[0:32, 0:32]),
         inc="ve", reads=("sq32",), writes=("sq32b",))
    S.op("vector", lambda e: e.tensor_tensor(
        sq32b[0:1, 16:16 + H // 2],
        sq32b[0:1, 0:H].rearrange("p (h two) -> p h two", two=2)[:, :, 0],
        sq32b[0:1, 0:H].rearrange("p (h two) -> p h two", two=2)[:, :, 1],
        ALU.min), inc="ve", reads=("sq32b",), writes=("sq32b",))
    bias_row = S.tok("ve")
    S.wait("tensor", bias_row)
    S.op("tensor", lambda e: e.matmul(
        PS[0:128, 3072:3072 + H // 2], ones_f[:], sq32b[0:1, 16:16 + H // 2],
        start=True, stop=True), inc="te")
    S.wait("vector", S.tok("te"))
    S.op("vector", lambda e: e.tensor_copy(biasP[:], PS[0:128, 3072:3072 + H // 2]),
         inc="ve", writes=("biasP",))
    biasH_tok = S.tok("ve")

    S.wait("scalar", biasH_tok)
    S.wait("tensor", vaug_tok)
    st_last = {}
    pt_last = {}
    evict_toks = {}
    prev_resc = [None]
    pending = [None]
    RS_BASE = max(2, KT - 6)

    def emit_rescale_step(step):
        if pending[0] is None:
            return
        lrow, araw, p, rcp_tok, rb0c, aq = pending[0][:6]
        if step == 0:
            S.wait("tensor", rcp_tok)
            if wb_last[0] is not None:
                S.wait("tensor", wb_last[0])
            if prev_resc[0] is not None:
                S.wait("tensor", prev_resc[0])
            S.op("tensor", lambda e, lrow=lrow, rb0c=rb0c: e.matmul(
                PS[0:64, rb0c:rb0c + QBL], ones_f[0:1, 0:64],
                lrow[0:1, 0:QBL], start=True, stop=True), inc="te")
            pending[0] = pending[0][:6] + (S.tok("te"),)
        elif step == 1:
            rbA = pending[0][6]
            S.wait("vector", rbA)
            S.op("vector", lambda e, p=p, araw=araw, rb0c=rb0c, aq=aq: e.tensor_tensor(
                aq[0:64, p * QBL:(p + 1) * QBL], araw[:, 0:QBL],
                PS[0:64, rb0c:rb0c + QBL], ALU.mult), inc="ve",
                reads=(araw.name,), writes=("%sA%d" % (aq.name, p),))
            pending[0] = pending[0][:6] + (S.tok("ve"),)
        elif step == 2:
            ttA = pending[0][6]
            S.wait("tensor", ttA)
            S.op("tensor", lambda e, lrow=lrow, rb0c=rb0c: e.matmul(
                PS[0:64, rb0c:rb0c + QBL], ones_f[0:1, 0:64],
                lrow[0:1, QBL:2 * QBL], start=True, stop=True), inc="te")
            pending[0] = pending[0][:6] + (S.tok("te"),)
        else:
            rbB = pending[0][6]
            S.wait("vector", rbB)
            S.op("vector", lambda e, p=p, araw=araw, rb0c=rb0c, aq=aq: e.tensor_tensor(
                aq[64:128, p * QBL:(p + 1) * QBL], araw[:, QBL:2 * QBL],
                PS[0:64, rb0c:rb0c + QBL], ALU.mult), inc="ve",
                reads=(araw.name,), writes=("%sB%d" % (aq.name, p),))
            prev_resc[0] = S.tok("ve")
            pending[0] = None

    def flush_rescale():
        for st in range(4):
            emit_rescale_step(st)

    prm_te = {}
    wp_toks = {}
    prev_projev = {}
    NCHUNK = NQB * CH

    def emit_wp_dma(gi):
        wp = (wpt, wpt2)[gi % 2]
        bpr = (bp_sl0, bp_sl1)[gi % 2]
        mf = gi % CH
        dsem = ("sw", "sw2")[gi % 2]
        if gi >= 2:
            S.wait("gpsimd", prm_te[gi - 2])
        S.dma_batch("gpsimd", dsem)
        S.op("gpsimd", lambda e, wp=wp, mf=mf: e.dma_start(wp[:], wph[mf]),
             inc=dsem, n=16)
        S.op("gpsimd", lambda e, mf=mf, bpr=bpr: e.dma_start(
            bpr[:], bp_row[0:1, mf * 128:(mf + 1) * 128]), inc=dsem, n=16)
        wp_toks[gi] = S.tok(dsem)

    def emit_proj_chunk(gi, fb, claim_key, ev_tok):
        src_qb, mf = divmod(gi, CH)
        aq = (attnq, attnq2)[src_qb % 2]
        qo2 = src_qb * QBL
        wp = (wpt, wpt2)[gi % 2]
        bpr = (bp_sl0, bp_sl1)[gi % 2]
        S.wait("tensor", wp_toks[gi])
        if ev_tok is not None:
            S.wait("tensor", ev_tok)
        if prev_resc[0] is not None:
            S.wait("tensor", prev_resc[0])
        for kc in range(CH):
            S.op("tensor", lambda e, kc=kc, wp=wp, aq=aq, fb=fb: e.matmul(
                PS[0:128, fb:fb + QBL], wp[:, kc * 128:(kc + 1) * 128],
                aq[:, kc * QBL:kc * QBL + QBL],
                start=(kc == 0), stop=False))
        S.op("tensor", lambda e, bpr=bpr, fb=fb: e.matmul(
            PS[0:128, fb:fb + QBL], bpr[0:1, 0:128], ones_b[0:1, 0:QBL],
            start=False, stop=True))
        prm = S.close("tensor", "te")
        prm_te[gi] = prm
        S.wait("vector", prm)
        ev = S.op("vector", lambda e, mf=mf, qo2=qo2, fb=fb: e.tensor_tensor(
            xr[:, mf * T + qo2:mf * T + qo2 + QBL], PS[0:128, fb:fb + QBL],
            xr[:, mf * T + qo2:mf * T + qo2 + QBL], ALU.add), inc="ve",
            reads=("xr%d" % mf,), writes=("xr%d" % mf,))
        if claim_key is not None:
            prev_projev[claim_key] = ev
        if gi + 2 < NCHUNK:
            emit_wp_dma(gi + 2)
        return ev

    PTS = (PT0, PT1, PT2, PT3)
    exp_all = {}
    NSLOT = NQB * CH * KT
    ev_final = {}

    def emit_st_exp(gs):
        pp, kt = divmod(gs, KT)
        qb, p = divmod(pp, CH)
        qo = qb * QBL
        pt = PTS[gs % 4]
        if gs >= 2:
            S.wait("tensor", exp_all[gs - 2])
        S.op("tensor", lambda e, kt=kt, p=p, qo=qo, gs=gs: e.matmul(
            PS[0:128, 2 * QBL * (gs % 2):2 * QBL * (gs % 2) + QBL],
            Ks[0:64, p * T + kt * 128:p * T + (kt + 1) * 128],
            Qs[0:64, p * T + qo:p * T + qo + QBL],
            start=True, stop=True, tile_position=(0, 0)))
        S.op("tensor", lambda e, kt=kt, p=p, qo=qo, gs=gs: e.matmul(
            PS[0:128, 2 * QBL * (gs % 2) + QBL:2 * QBL * (gs % 2) + 2 * QBL],
            Ks[64:128, p * T + kt * 128:p * T + (kt + 1) * 128],
            Qs[64:128, p * T + qo:p * T + qo + QBL],
            start=True, stop=True, tile_position=(64, 0)))
        st_tok = S.close("tensor", "te")
        S.wait("scalar", st_tok)
        exp_all[gs] = S.op("scalar", lambda e, gs=gs, pt=pt, p=p: e.activation(
            pt[:], PS[0:128, 2 * QBL * (gs % 2):2 * QBL * (gs % 2) + 2 * QBL],
            AF.Exp, bias=biasP[:, p:p + 1], scale=0.125), inc="ac",
            writes=(pt.name,))

    def emit_pv_slot(gp):
        pp, kt = divmod(gp, KT)
        qb, p = divmod(pp, CH)
        par_p = pp % 2
        lrow = (lrow0, lrow1)[par_p]
        araw = (araw0, araw1)[par_p]
        pv0 = 2048 + 1024 * par_p
        pt = PTS[gp % 4]
        if kt == 0:
            S.wait("tensor", evict_toks.get(pp - 2))
            if pp <= 1 and wb_last[0] is not None:
                S.wait("tensor", wb_last[0])
            S.wait("tensor", prev_projev.get(pp - 1))
            if prev_resc[0] is not None:
                S.wait("tensor", prev_resc[0])
        S.op("tensor", lambda e, kt=kt, pt=pt, p=p, pv0=pv0: e.matmul(
            PS[0:65, pv0:pv0 + QBL],
            vaug[:, kt * H * 65 + 2 * p * 65:kt * H * 65 + (2 * p + 1) * 65],
            pt[:, 0:QBL], start=(kt == 0), stop=(kt == KT - 1)))
        S.op("tensor", lambda e, kt=kt, pt=pt, p=p, pv0=pv0: e.matmul(
            PS[0:65, pv0 + 512:pv0 + 512 + QBL],
            vaug[:, kt * H * 65 + (2 * p + 1) * 65:kt * H * 65 + (2 * p + 2) * 65],
            pt[:, QBL:2 * QBL], start=(kt == 0), stop=(kt == KT - 1)))
        pvc = S.close("tensor", "te")
        if kt == KT - 1:
            S.wait("vector", pvc)
            pv2 = PS[:, pv0:pv0 + 1024].rearrange(
                "p (b c) -> p b c", b=2)[:, :, 0:QBL]
            S.op("vector", lambda e, lrow=lrow, pv2=pv2: e.tensor_copy(
                lrow[:].rearrange("p (b c) -> p b c", b=2), pv2[64:65]),
                inc="ve", writes=(lrow.name,))
            S.op("vector", lambda e, araw=araw, pv2=pv2: e.tensor_copy(
                araw[:].rearrange("p (b c) -> p b c", b=2), pv2[0:64]),
                inc="ve", writes=(araw.name,))
            evict_toks[pp] = S.tok("ve")
            S.op("vector", lambda e, lrow=lrow: e.reciprocal(
                lrow[:], lrow[:]), inc="ve", reads=(lrow.name,),
                writes=(lrow.name,))
            pending[0] = (lrow, araw, p, S.tok("ve"), pv0,
                          (attnq, attnq2)[qb % 2])
            if p == CH - 1 and qb == 0:
                for g in range(2):
                    emit_wp_dma(g)
        else:
            if RS_BASE <= kt < RS_BASE + 4:
                emit_rescale_step(kt - RS_BASE)
            if kt == 5 and qb >= 1 and p >= 1:
                emit_proj_chunk((qb - 1) * CH + (p - 1),
                                2048 + 1024 * ((pp - 1) % 2) + 512,
                                pp, evict_toks.get(pp - 1))
            if kt == KT - 2 and p == CH - 1 and qb >= 1:
                emit_proj_chunk((qb - 1) * CH + 7,
                                2048 + 1024 * ((pp - 1) % 2) + 512,
                                pp, prev_projev.get(pp))

    for gs in range(NSLOT + 2):
        if gs < NSLOT:
            emit_st_exp(gs)
        else:
            S.wait("tensor", exp_all[gs - 2])
        if gs >= 2:
            emit_pv_slot(gs - 2)
    flush_rescale()
    prev_prm = {}
    for mf in range(CH):
        gi = (NQB - 1) * CH + mf
        fb = 1024 + 512 * (mf % 4)
        if mf >= 4:
            S.wait("tensor", prev_prm[mf - 4])
        prev_prm[mf] = emit_proj_chunk(gi, fb, None, None)
    attn_done = S.tok("ve")

    S.wait("gpsimd", S.tok("te"))
    S.wait("sync", S.tok("ac"))
    S.wait("sync", S.tok("te"))
    ffnw2_sw, ffnw2_hw = emit_ffn_weight_loads()
    S.drain("vector")
    S.drain("scalar")
    S.wait("vector", attn_done)
    S.wait("scalar", attn_done)
    preload_set6()
    rows3, gc3 = emit_ln(want_gc=True)
    S.op("scalar", lambda e: e.preload_act_table(AF.Gelu))
    S.wait("tensor", ffnw2_sw)
    S.wait("scalar", ffnw2_sw)
    S.wait("tensor", ffnw2_hw)
    S.wait("scalar", ffnw2_hw)
    ab3 = emit_gcnet(gc3, ffnw2_sw, ffnw2_hw)

    def store_slice(so, tok):
        S.wait("sync", tok)
        S.dma_batch("sync", "hw")
        for c in range(CH):
            S.op("sync", lambda e, c=c, so=so: e.dma_start(
                y[c * 128:(c + 1) * 128, so:so + SL],
                xr[:, c * T + so:c * T + so + SL]), inc="hw", n=16)

    ffn2_done = emit_ffn(rows3, ab3, store_cb=store_slice)
    S.wait("sync", S.tok("hw"))

    S.build()
    return nc


_CACHE = {}


def _get_program(T):
    if T not in _CACHE:
        _CACHE[T] = build_program(T)
    return _CACHE[T]


def kernel(**inputs):
    x = np.asarray(inputs["L0_attn_x"], dtype=np.float32)
    B, T, D = x.shape
    nc = _get_program(T)
    wmap = {
        "Wqkv": "L0_attn_W_qkv", "bqkv": "L0_attn_b_qkv",
        "Aq": "L0_attn_A_qkv", "Bq": "L0_attn_B_qkv",
        "Wproj": "L0_attn_W_proj", "bproj": "L0_attn_b_proj",
        "Apj": "L0_attn_A_proj", "Bpj": "L0_attn_B_proj",
        "cw1": "L0_ffn_cn_w1", "cb1": "L0_ffn_cn_b1",
        "cw2": "L0_ffn_cn_w2", "cb2": "L0_ffn_cn_b2",
        "cw3": "L0_ffn_cn_w3", "cb3": "L0_ffn_cn_b3",
        "cw4": "L0_ffn_cn_w4", "cb4": "L0_ffn_cn_b4",
        "gw1": "L0_ffn_gc_w1", "gb1": "L0_ffn_gc_b1",
        "gw2": "L0_ffn_gc_w2", "gb2": "L0_ffn_gc_b2",
        "gw3": "L0_ffn_gc_w3", "gb3": "L0_ffn_gc_b3",
        "ar": "L0_ffn_adaptive_range",
    }
    shared = {k: np.ascontiguousarray(np.asarray(inputs[v], dtype=np.float32))
              for k, v in wmap.items()}
    in_maps = []
    for b in range(B):
        m = dict(shared)
        m["xT"] = np.ascontiguousarray(x[b].T)
        in_maps.append(m)
    res = run_bass_kernel_spmd(nc, in_maps, core_ids=list(range(N_CORES)))
    out = np.stack([np.asarray(res.results[b]["y"]).T for b in range(B)])
    return out.astype(np.float32)


# version 38
# speedup vs baseline: 1.0024x; 1.0024x over previous
import numpy as np
import concourse.bass as bass
import concourse.mybir as mybir
from concourse.bass_utils import run_bass_kernel_spmd

F32 = mybir.dt.float32
BF16 = mybir.dt.bfloat16
AF = mybir.ActivationFunctionType
ALU = mybir.AluOpType
AX = mybir.AxisListType

ENGINES = ("sync", "tensor", "scalar", "vector", "gpsimd")
SBUF_BASE = 16512
SBUF_END = 229376
N_CORES = 8


def nslices(width, maxn=512):
    out = []
    o = 0
    while o < width:
        n = min(maxn, width - o)
        out.append((o, n))
        o += n
    return out


class Sched:

    def __init__(self, nc):
        self.nc = nc
        self.q = {e: [] for e in ENGINES}
        self.sems = {}
        self.cnt = {}
        self.waited = {e: {} for e in ENGINES}
        self.written = {e: set() for e in ENGINES}
        self.readset = {e: set() for e in ENGINES}

    def sem(self, name):
        self.sems[name] = self.nc.alloc_semaphore(name)
        self.cnt[name] = 0

    def op(self, eng, fn, inc=None, n=1, reads=(), writes=()):
        q = self.q[eng]
        if eng in ("vector", "scalar", "gpsimd"):
            r, w = set(reads), set(writes)
            if (r & self.written[eng]) or (w & self.written[eng]) \
                    or (w & self.readset[eng]):
                q.append(("drain",))
                self.written[eng].clear()
                self.readset[eng].clear()
            self.readset[eng].update(r)
        tok = None
        if inc is not None:
            self.cnt[inc] += n
            tok = (inc, self.cnt[inc])
        q.append(("op", fn, inc, self.cnt[inc] if inc else 0, n))
        self.written[eng].update(writes)
        return tok

    def tok(self, sem):
        return (sem, self.cnt[sem])

    def wait(self, eng, tok):
        if tok is None:
            return
        sem, val = tok
        if val <= 0:
            return
        if self.waited[eng].get(sem, -1) >= val:
            return
        self.waited[eng][sem] = val
        self.q[eng].append(("wait", sem, val))

    def close(self, eng, sem, n=1):
        for i in range(len(self.q[eng]) - 1, -1, -1):
            if self.q[eng][i][0] == "op":
                _, fn, inc0, _, _ = self.q[eng][i]
                assert inc0 is None, "op already increments"
                self.cnt[sem] += n
                self.q[eng][i] = ("op", fn, sem, self.cnt[sem], n)
                return (sem, self.cnt[sem])
        raise RuntimeError("no op to close")

    def dma_batch(self, eng, sem):
        self.wait(eng, (sem, self.cnt[sem]))

    def drain(self, eng):
        self.q[eng].append(("drain",))
        self.written[eng].clear()
        self.readset[eng].clear()

    def replay(self, eng_name, eng):
        for item in self.q[eng_name]:
            kind = item[0]
            if kind == "op":
                _, fn, inc, val, n = item
                inst = fn(eng)
                if inc is not None:
                    inst.then_inc(self.sems[inc], n)
            elif kind == "wait":
                _, sem, val = item
                eng.wait_ge(self.sems[sem], val)
            elif kind == "drain":
                eng.drain()

    def build(self):
        with self.nc.Block() as block:
            for e in ENGINES:
                if not self.q[e]:
                    continue
                getattr(block, e)(lambda eng, e=e: self.replay(e, eng))


class Mem:

    def __init__(self, nc):
        self.nc = nc
        self.tensors = {}

    def at(self, name, shape, dtype, off):
        size = int(np.prod(shape[1:])) * mybir.dt.size(dtype)
        assert SBUF_BASE + off + size <= SBUF_END, (name, off, size)
        assert (SBUF_BASE + off) % 32 == 0, (name, off)
        t = self.nc.alloc_sbuf_tensor_at(name, list(shape), dtype,
                                         offset=SBUF_BASE + off)
        self.tensors[name] = t
        return t


def build_program(T=2048):
    CH = 8
    D = CH * 128
    SL = T // 8
    NSL = 8
    QBL = T // 4
    NQB = 4
    KT = T // 128
    H = 16
    inv_d = 1.0 / D

    nc = bass.Bass("TRN2", target_bir_lowering=False, debug=False)

    def din(name, shape):
        return nc.dram_tensor(name, list(shape), F32, kind="ExternalInput").ap()

    xT = din("xT", [D, T])
    Wqkv = din("Wqkv", [D, 3 * D])
    bqkv = din("bqkv", [3 * D])
    Aq = din("Aq", [D, 8])
    Bq = din("Bq", [8, 3 * D])
    Wproj = din("Wproj", [D, D])
    bproj = din("bproj", [D])
    Apj = din("Apj", [D, 8])
    Bpj = din("Bpj", [8, D])
    cw1 = din("cw1", [D, D])
    cb1 = din("cb1", [D])
    cw2 = din("cw2", [D, D // 2])
    cb2 = din("cb2", [D // 2])
    cw3 = din("cw3", [D // 2, D // 4])
    cb3 = din("cb3", [D // 4])
    cw4 = din("cw4", [D // 4, 1])
    cb4 = din("cb4", [1])
    gw1 = din("gw1", [D, D // 2])
    gb1 = din("gb1", [D // 2])
    gw2 = din("gw2", [D // 2, D // 4])
    gb2 = din("gb2", [D // 4])
    gw3 = din("gw3", [D // 4, 1])
    gb3 = din("gb3", [1])
    ar_in = din("ar", [1])
    y = nc.dram_tensor("y", [D, T], F32, kind="ExternalOutput").ap()
    wph = nc.dram_tensor("wph", [CH, 128, CH * 128], BF16, kind="Internal").ap()

    M = Mem(nc)
    f32b, bf16b = 4, 2
    off = 0
    xr = M.at("xr", [128, CH * T], F32, off); off += CH * T * f32b
    A0 = off
    off += CH * T * f32b
    B0 = off
    off += 37024
    C0 = off

    pingA = M.at("pingA", [128, CH * SL], F32, A0)
    pongA = M.at("pongA", [128, CH * SL], F32, A0 + CH * SL * f32b)
    pingB = M.at("pingB", [128, CH * SL], F32, A0 + 2 * CH * SL * f32b)
    pongB = M.at("pongB", [128, CH * SL], F32, A0 + 3 * CH * SL * f32b)
    a_ffn = A0 + 4 * CH * SL * f32b
    srcbA0 = M.at("srcbA0", [128, CH * SL], BF16, a_ffn)
    srcbA1 = M.at("srcbA1", [128, CH * SL], BF16, a_ffn + CH * SL * bf16b)
    srcbB0 = M.at("srcbB0", [128, CH * SL], BF16, a_ffn + 2 * CH * SL * bf16b)
    srcbB1 = M.at("srcbB1", [128, CH * SL], BF16, a_ffn + 3 * CH * SL * bf16b)
    b_ffn = a_ffn + 4 * CH * SL * bf16b
    h1b = M.at("h1b", [128, CH * SL], BF16, b_ffn)
    h2b = M.at("h2b", [128, (CH // 2) * SL], BF16, b_ffn + CH * SL * bf16b)
    h3b = M.at("h3b", [128, (CH // 4) * SL], BF16,
               b_ffn + CH * SL * bf16b + (CH // 2) * SL * bf16b)
    assert b_ffn + (CH + CH // 2 + CH // 4) * SL * bf16b <= B0
    xbf = M.at("xbf", [128, T], BF16, a_ffn)
    xbf2 = M.at("xbf2", [128, T], BF16, a_ffn + T * bf16b)
    xsq = M.at("xsq", [128, T], BF16, a_ffn + 2 * T * bf16b)
    xsq2 = M.at("xsq2", [128, T], BF16, a_ffn + 3 * T * bf16b)
    assert a_ffn + 4 * T * bf16b <= B0
    gtmp = M.at("gtmp", [128, T], F32, A0)

    bo = B0
    wcn1 = M.at("wcn1", [128, CH * D], BF16, bo); bo += CH * D * bf16b
    wcn2 = M.at("wcn2", [128, CH * (D // 2)], BF16, bo); bo += CH * (D // 2) * bf16b
    wcn3 = M.at("wcn3", [128, (CH // 2) * (D // 4)], BF16, bo)
    bo += (CH // 2) * (D // 4) * bf16b
    wcn4 = M.at("wcn4", [128, 2], BF16, bo); bo += 32
    wgc1 = M.at("wgc1", [128, CH * (D // 2)], BF16, bo); bo += CH * (D // 2) * bf16b
    wgc2 = M.at("wgc2", [128, (CH // 2) * (D // 4)], BF16, bo)
    bo += (CH // 2) * (D // 4) * bf16b
    wgc3 = M.at("wgc3", [128, 2], BF16, bo); bo += 32
    b1s = M.at("b1s", [128, CH], F32, bo); bo += 32
    b2s = M.at("b2s", [128, CH // 2], F32, bo); bo += 32
    b3s = M.at("b3s", [128, CH // 4], F32, bo); bo += 32
    assert bo - B0 <= 37024, bo - B0
    vaug = M.at("vaug", [128, KT * H * 65], BF16, B0)

    assert T >= 1024
    xn2 = M.at("xn2", [128, CH * T], BF16, A0)
    Wv_sb = M.at("Wv_sb", [128, CH * D], BF16, A0 + CH * T * bf16b)
    Ks = M.at("Ks2", [128, CH * T], BF16, A0 + CH * T * bf16b)
    ao = A0
    def alloA(name, shape, dtype):
        nonlocal ao
        t = M.at(name, shape, dtype, ao)
        ao += ((int(np.prod(shape[1:])) * mybir.dt.size(dtype) + 31) // 32) * 32
        return t
    attnq = alloA("attnq", [128, CH * QBL], BF16)
    attnq2 = alloA("attnq2", [128, CH * QBL], BF16)
    PT0 = alloA("PT0", [128, 2 * QBL], BF16)
    PT1 = alloA("PT1", [128, 2 * QBL], BF16)
    lrow0 = alloA("lrow0", [1, 2 * QBL], F32)
    lrow1 = alloA("lrow1", [1, 2 * QBL], F32)
    ARAW_OFF = ao
    araw0 = alloA("araw0", [64, 2 * QBL], BF16)
    araw1 = alloA("araw1", [64, 2 * QBL], BF16)
    qsq = M.at("qsq", [128, T], BF16, A0)
    qsq2 = M.at("qsq2", [128, T], BF16, ARAW_OFF)
    assert ao <= A0 + CH * T * bf16b

    co = C0
    def calloc(name, shape, dtype):
        nonlocal co
        t = M.at(name, shape, dtype, co)
        co += ((int(np.prod(shape[1:])) * mybir.dt.size(dtype) + 31) // 32) * 32
        return t

    sc8 = calloc("sc8", [1, 16], F32)
    meanb_col = calloc("meanb_col", [128, 1], F32)
    ones_f = calloc("ones_f", [1, 128], F32)
    ones_b = calloc("ones_b", [1, 512], BF16)
    onesc_b = calloc("onesc_b", [128, 1], BF16)
    selq = calloc("selq", [128, 128], BF16)
    biasH = calloc("biasH", [128, H], F32)
    biasP = calloc("biasP", [128, H // 2], F32)
    sq32 = calloc("sq32", [32, 32], F32)
    sq32b = calloc("sq32b", [32, 32], F32)
    cpool = calloc("cpool", [128, 8], F32)
    P0 = co

    po = P0
    def alloP(name, shape, dtype):
        nonlocal po
        t = M.at(name, shape, dtype, po)
        po += ((int(np.prod(shape[1:])) * mybir.dt.size(dtype) + 31) // 32) * 32
        return t
    a_row = alloP("a_row", [1, T], F32)
    b_row = alloP("b_row", [1, T], F32)
    atmp = alloP("atmp", [128, SL], F32)
    v_row = alloP("v_row", [1, T], F32)
    tf_row = alloP("tf_row", [1, SL], F32)
    cpl_row = alloP("cpl_row", [1, SL], F32)
    ncpl_row = alloP("ncpl_row", [1, SL], F32)
    cplB_row = alloP("cplB_row", [1, SL], F32)
    ttmp0 = alloP("ttmp0", [128, SL], F32)
    ttmp1 = alloP("ttmp1", [128, SL], F32)
    gcsum = alloP("gcsum", [128, CH], F32)
    gcbf = alloP("gcbf", [128, CH], BF16)
    hg1p_sb = alloP("hg1bf", [128, 4], BF16)
    hg2p_sb = alloP("hg2bf", [128, 2], BF16)
    gb1s = alloP("gb1s", [128, 4], F32)
    gb2s = alloP("gb2s", [128, 2], F32)
    P_FFN_END = po

    po = P0
    Qs = alloP("Qs", [128, CH * T], BF16)
    tq = alloP("tq", [8, T], BF16)
    AV0 = po
    wqt = alloP("wqt", [128, CH * 128], BF16)
    wkt = alloP("wkt", [128, CH * 128], BF16)
    bqqk0 = alloP("bqqk0", [8, 256], BF16)
    brqk0 = alloP("brqk0", [1, 256], BF16)
    P_ATT1_END = po
    po = AV0
    Aq_sb = alloP("Aq_sb", [128, CH * 8], BF16)
    Bqv = alloP("Bqv", [8, D], BF16)
    bqv = alloP("bqv", [1, D], BF16)
    P_VT_END = po

    po = P0 + ((CH * T * bf16b + 31) // 32) * 32
    WPT_OFF = po
    wpt = alloP("wpt", [128, CH * 128], BF16)
    wpt3 = alloP("wpt3", [128, CH * 128], BF16)
    PT2 = M.at("PT2", [128, 2 * QBL], BF16, WPT_OFF + CH * 128 * bf16b)
    wpf = M.at("wpf", [128, CH * 128], F32, WPT_OFF)
    ApjT = alloP("ApjT", [32, CH * 128], BF16)
    aptmp = alloP("aptmp", [64, 128], BF16)
    BPSB_OFF = po
    Bp_sb = alloP("Bp_sb", [8, D], BF16)
    PT3 = M.at("PT3", [128, 2 * QBL], BF16, BPSB_OFF)
    wpt2 = M.at("wpt2", [128, CH * 128], BF16, B0 + 33280)
    wstage = M.at("wstage", [128, CH * 128], BF16, B0 + 33280)
    assert 33280 + CH * 128 * bf16b <= 37024
    bp_sl0 = alloP("bp_sl0", [1, 128], BF16)
    bp_sl1 = alloP("bp_sl1", [1, 128], BF16)
    bp_sl2 = alloP("bp_sl2", [1, 128], BF16)
    P_ATT2_END = po
    PLIM = SBUF_END - SBUF_BASE
    assert max(P_FFN_END, P_ATT1_END, P_VT_END, P_ATT2_END) <= PLIM, (
        P_FFN_END, P_ATT1_END, P_VT_END, P_ATT2_END, PLIM)

    SC_AR, SC_B1, SC_A1, SC_GT, SC_T0, SC_MB, SC_B4H, SC_GB3H = range(8)

    def sc(i):
        return sc8[0:1, i:i + 1]

    PS = nc.alloc_psum_tensor("PS", [128, 4096], F32)

    from concourse.hw_specs import get_activation_tables
    _tabs = list(get_activation_tables(nc.m.arch).values())
    NLE_IDX = next(i for i, fs in enumerate(_tabs)
                   if AF.Ln in fs and AF.Exp in fs)

    S = Sched(nc)

    def preload_set6():
        S.op("scalar", lambda e: e.add_instruction(
            mybir.InstLoadActFuncSet(
                name=nc.get_next_instruction_name(),
                act_func_set_id=NLE_IDX, ins=[], outs=[])))
    for s in ("hw", "sw", "sw2", "te", "ac", "ve", "gp"):
        S.sem(s)
    for c in range(CH):
        S.sem("x%d" % c)

    x_toks = {}
    for c in range(CH):
        S.op("sync", lambda e, c=c: e.dma_start(
            xr[:, c * T:(c + 1) * T], xT[c * 128:(c + 1) * 128, :]),
            inc="x%d" % c, n=16)
        x_toks[c] = S.tok("x%d" % c)

    def load_w_chunks(dst, src, ncols, nch, tag):
        for c in range(nch):
            S.op("gpsimd", lambda e, c=c, dst=dst, src=src, ncols=ncols: e.dma_start(
                dst[:, c * ncols:(c + 1) * ncols],
                src[c * 128:(c + 1) * 128, :]), inc="sw", n=16)
        return S.tok("sw")

    def load_bias_cols(dst, src, nch):
        S.dma_batch("sync", "hw")
        src_r = src.rearrange("(c p o) -> c p o", p=128, o=1)
        for c in range(nch):
            S.op("sync", lambda e, c=c, dst=dst, src_r=src_r: e.dma_start(
                dst[:, c:c + 1], src_r[c]), inc="hw", n=16)
        return S.tok("hw")

    def emit_ffn_weight_loads():
        S.dma_batch("gpsimd", "sw")
        load_w_chunks(wcn1, cw1, D, CH, "cw1")
        load_w_chunks(wcn2, cw2, D // 2, CH, "cw2")
        load_w_chunks(wcn3, cw3, D // 4, CH // 2, "cw3")
        for c in range(2):
            S.op("gpsimd", lambda e, c=c: e.dma_start(
                wcn4[:, c:c + 1], cw4.rearrange("(c p) o -> c p o", p=128)[c]),
                inc="sw", n=16)
        load_w_chunks(wgc1, gw1, D // 2, CH, "gw1")
        load_w_chunks(wgc2, gw2, D // 4, CH // 2, "gw2")
        for c in range(2):
            S.op("gpsimd", lambda e, c=c: e.dma_start(
                wgc3[:, c:c + 1], gw3.rearrange("(c p) o -> c p o", p=128)[c]),
                inc="sw", n=16)
        load_bias_cols(b1s, cb1, CH)
        load_bias_cols(b2s, cb2, CH // 2)
        load_bias_cols(b3s, cb3, CH // 4)
        load_bias_cols(gb1s, gb1, 4)
        load_bias_cols(gb2s, gb2, 2)
        return S.tok("sw"), S.tok("hw")

    ffnw_sw, ffnw_hw = emit_ffn_weight_loads()
    S.dma_batch("sync", "hw")
    S.op("sync", lambda e: e.dma_start(sc8[0:1, SC_AR:SC_AR + 1],
                                       ar_in.rearrange("(a o) -> a o", a=1)), inc="hw", n=16)
    S.op("sync", lambda e: e.dma_start(sc8[0:1, SC_T0:SC_T0 + 1],
                                       cb4.rearrange("(a o) -> a o", a=1)), inc="hw", n=16)
    S.op("sync", lambda e: e.dma_start(sc8[0:1, SC_MB:SC_MB + 1],
                                       gb3.rearrange("(a o) -> a o", a=1)), inc="hw", n=16)
    tiny_tok = S.tok("hw")

    for i, cv in enumerate((1e-5, 0.125, 2.0, 0.5, 1.0 / D, -0.5)):
        nc.const_aps.register(F32, cv, cpool[:, i:i + 1])
        S.op("vector", lambda e, i=i, cv=cv: e.memset(cpool[:, i:i + 1], cv),
             inc="ve", writes=("cpool",))
    S.op("vector", lambda e: e.memset(ones_f[:], 1.0), inc="ve", writes=("ones_f",))
    S.op("vector", lambda e: e.memset(ones_b[:], 1.0), inc="ve", writes=("ones_b",))
    S.op("vector", lambda e: e.memset(onesc_b[:], 1.0), inc="ve", writes=("onesc_b",))
    S.op("vector", lambda e: e.memset(sq32[:], 0.0), inc="ve", writes=("sq32",))
    S.op("vector", lambda e: e.memset(selq[:], 0.0), inc="ve", writes=("selq",))
    for c in range(CH):
        S.op("vector", lambda e, c=c: e.memset(
            selq[0:64, 16 * c + 2 * c:16 * c + 2 * c + 1], 1.0), writes=("selq",))
        S.op("vector", lambda e, c=c: e.memset(
            selq[64:128, 16 * c + 2 * c + 1:16 * c + 2 * c + 2], 1.0),
            inc="ve", writes=("selq",))
    consts_tok = S.tok("ve")

    S.wait("vector", tiny_tok)
    S.op("vector", lambda e: e.tensor_scalar_mul(sc(SC_B4H), sc(SC_T0), 0.5),
         inc="ve", reads=("sc8",), writes=("sc8",))
    S.op("vector", lambda e: e.tensor_scalar_mul(sc(SC_GB3H), sc(SC_MB), 0.5),
         inc="ve", reads=("sc8",), writes=("sc8",))

    def emit_ln(want_gc, gates=None):
        prev_mm = []
        for c in range(CH):
            xb = xbf if c % 2 == 0 else xbf2
            xq = xsq if c % 2 == 0 else xsq2
            if gates is not None:
                S.wait("vector", gates[c])
                S.wait("scalar", gates[c])
            if c >= 2:
                S.wait("vector", prev_mm[c - 2])
                S.wait("scalar", prev_mm[c - 2])
            S.op("vector", lambda e, c=c, xb=xb: e.tensor_copy(
                xb[:], xr[:, c * T:(c + 1) * T]), inc="ve",
                writes=(xb.name,))
            S.op("scalar", lambda e, c=c, xq=xq: e.activation(
                xq[:], xr[:, c * T:(c + 1) * T], AF.Square), inc="ac",
                writes=(xq.name,))
            S.wait("tensor", S.tok("ve"))
            S.wait("tensor", S.tok("ac"))
            for (o, n) in nslices(T):
                S.op("tensor", lambda e, c=c, o=o, n=n, xb=xb: e.matmul(
                    PS[0:1, o:o + n], onesc_b[:], xb[:, o:o + n],
                    start=(c == 0), stop=(c == CH - 1)))
                S.op("tensor", lambda e, c=c, o=o, n=n, xq=xq: e.matmul(
                    PS[0:1, 2048 + o:2048 + o + n], onesc_b[:], xq[:, o:o + n],
                    start=(c == 0), stop=(c == CH - 1)))
            prev_mm.append(S.close("tensor", "te"))
        sums_tok = S.tok("te")

        HT = T // 2
        S.wait("scalar", sums_tok)
        S.wait("vector", sums_tok)
        var_t = {}
        exp_t = {}
        for h in range(2):
            o = h * HT
            S.op("scalar", lambda e, o=o: e.activation(
                v_row[0:1, o:o + HT], PS[0:1, o:o + HT], AF.Square,
                scale=inv_d), inc="ac", writes=("v_row%d" % h,))
            sq_t = S.tok("ac")
            S.wait("vector", sq_t)
            S.op("vector", lambda e, o=o: e.scalar_tensor_tensor(
                v_row[0:1, o:o + HT], PS[0:1, 2048 + o:2048 + o + HT], inv_d,
                v_row[0:1, o:o + HT], ALU.mult, ALU.subtract), inc="ve",
                reads=("v_row%d" % h,), writes=("v_row%d" % h,))
            var_t[h] = S.tok("ve")
        for h in range(2):
            o = h * HT
            S.wait("scalar", var_t[h])
            S.op("scalar", lambda e, o=o: e.activation(
                v_row[0:1, o:o + HT], v_row[0:1, o:o + HT], AF.Ln,
                bias=1e-5), inc="ac", reads=("v_row%d" % h,),
                writes=("v_row%d" % h,))
            S.op("scalar", lambda e, o=o: e.activation(
                a_row[0:1, o:o + HT], v_row[0:1, o:o + HT], AF.Exp,
                scale=-0.5), inc="ac", reads=("v_row%d" % h,),
                writes=("a_row%d" % h,))
            exp_t[h] = S.tok("ac")
        for h in range(2):
            o = h * HT
            S.wait("vector", exp_t[h])
            S.op("vector", lambda e, o=o: e.scalar_tensor_tensor(
                b_row[0:1, o:o + HT], PS[0:1, o:o + HT], -inv_d,
                a_row[0:1, o:o + HT], ALU.mult, ALU.mult), inc="ve",
                reads=("a_row%d" % h,), writes=("b_row%d" % h, "b_row"))
        rows_tok = S.tok("ve")

        gc_tok = None
        if want_gc:
            S.op("vector", lambda e: e.reduce_sum(
                sc(SC_MB), b_row[:], axis=AX.X), inc="ve", reads=("b_row",),
                writes=("sc8",))
            S.op("vector", lambda e: e.tensor_scalar_mul(
                sc(SC_MB), sc(SC_MB), 1.0 / T), inc="ve", reads=("sc8",),
                writes=("sc8",))
            S.wait("tensor", S.tok("ve"))
            S.op("tensor", lambda e: e.matmul(
                PS[0:128, 3072:3073], ones_f[:], sc(SC_MB),
                start=True, stop=True), inc="te")
            S.wait("vector", S.tok("te"))
            S.op("vector", lambda e: e.tensor_copy(meanb_col[:],
                 PS[0:128, 3072:3073]), inc="ve", writes=("meanb_col",))
            S.wait("tensor", S.tok("ve"))
            for (o, n) in nslices(T):
                S.op("tensor", lambda e, o=o, n=n: e.matmul(
                    PS[0:128, 2048 + o:2048 + o + n], ones_f[:],
                    a_row[0:1, o:o + n], start=True, stop=True))
            abc_tok = S.close("tensor", "te")
            S.wait("vector", abc_tok)
            for c in range(CH):
                S.op("vector", lambda e, c=c: e.tensor_tensor(
                    gtmp[:], xr[:, c * T:(c + 1) * T], PS[0:128, 2048:2048 + T],
                    ALU.mult), inc="ve", reads=("xr%d" % c,), writes=("gtmp",))
                S.op("vector", lambda e, c=c: e.reduce_sum(
                    gcsum[:, c:c + 1], gtmp[:], axis=AX.X), inc="ve",
                    reads=("gtmp",), writes=("gcsum",))
            S.op("vector", lambda e: e.tensor_scalar(
                gcbf[:], gcsum[:], 1.0 / T, meanb_col[:], ALU.mult, ALU.add),
                inc="ve", reads=("gcsum", "meanb_col"), writes=("gcbf",))
            gc_tok = S.tok("ve")
        return rows_tok, gc_tok

    def emit_gcnet(gc_tok, w_tok, bias_tok):
        S.wait("tensor", gc_tok)
        S.wait("tensor", w_tok)
        S.wait("scalar", bias_tok)
        for mg in range(4):
            for kc in range(CH):
                S.op("tensor", lambda e, mg=mg, kc=kc: e.matmul(
                    PS[0:128, 3584 + mg:3584 + mg + 1],
                    wgc1[:, kc * (D // 2) + mg * 128:kc * (D // 2) + (mg + 1) * 128],
                    gcbf[:, kc:kc + 1], start=(kc == 0), stop=(kc == CH - 1)))
        t1 = S.close("tensor", "te")
        S.wait("scalar", t1)
        for mg in range(4):
            S.op("scalar", lambda e, mg=mg: e.activation(
                hg1p_sb[:, mg:mg + 1], PS[0:128, 3584 + mg:3584 + mg + 1],
                AF.Gelu, bias=gb1s[:, mg:mg + 1]), inc="ac",
                writes=("hg1bf",))
        S.wait("tensor", S.tok("ac"))
        for mg in range(2):
            for kc in range(4):
                S.op("tensor", lambda e, mg=mg, kc=kc: e.matmul(
                    PS[0:128, 3590 + mg:3590 + mg + 1],
                    wgc2[:, kc * (D // 4) + mg * 128:kc * (D // 4) + (mg + 1) * 128],
                    hg1p_sb[:, kc:kc + 1], start=(kc == 0), stop=(kc == 3)))
        t2 = S.close("tensor", "te")
        S.wait("scalar", t2)
        for mg in range(2):
            S.op("scalar", lambda e, mg=mg: e.activation(
                hg2p_sb[:, mg:mg + 1], PS[0:128, 3590 + mg:3590 + mg + 1],
                AF.Gelu, bias=gb2s[:, mg:mg + 1]), inc="ac", writes=("hg2bf",))
        S.wait("tensor", S.tok("ac"))
        for kc in range(2):
            S.op("tensor", lambda e, kc=kc: e.matmul(
                PS[0:1, 3594:3595], wgc3[:, kc:kc + 1], hg2p_sb[:, kc:kc + 1],
                start=(kc == 0), stop=(kc == 1)))
        t3 = S.close("tensor", "te")
        S.wait("scalar", t3)
        S.op("scalar", lambda e: e.activation(
            sc(SC_GT), PS[0:1, 3594:3595], AF.Tanh, bias=sc(SC_GB3H)[0:1, 0:1],
            scale=0.5), inc="ac", writes=("sc8",))
        S.wait("vector", S.tok("ac"))
        S.op("vector", lambda e: e.tensor_scalar_mul(sc(SC_B1), sc(SC_AR), 0.3),
             inc="ve", reads=("sc8",), writes=("sc8",))
        S.op("vector", lambda e: e.tensor_tensor(
            sc(SC_T0), sc(SC_AR), sc(SC_GT), ALU.mult), inc="ve",
            reads=("sc8",), writes=("sc8",))
        S.op("vector", lambda e: e.tensor_scalar(
            sc(SC_A1), sc(SC_T0), 0.7, 0.1, ALU.mult, ALU.add), inc="ve",
            reads=("sc8",), writes=("sc8",))
        return S.tok("ve")

    def emit_ffn(rows_tok, ab_tok, store_cb=None):
        S.wait("tensor", ab_tok)

        def emit_apply(ctx, bank_guard):
            so = ctx["so"]
            S.wait("tensor", rows_tok)
            if bank_guard is not None:
                S.wait("tensor", bank_guard)
            for (o, n) in nslices(SL):
                S.op("tensor", lambda e, o=o, n=n, so=so: e.matmul(
                    PS[0:128, 2560 + o:2560 + o + n], ones_f[:],
                    a_row[0:1, so + o:so + o + n], start=True, stop=True))
                S.op("tensor", lambda e, o=o, n=n, so=so: e.matmul(
                    PS[0:128, 2560 + SL + o:2560 + SL + o + n], ones_f[:],
                    b_row[0:1, so + o:so + o + n], start=True, stop=True))
            ab_bc = S.close("tensor", "te")
            S.wait("vector", ab_bc)
            ping = ctx["state"]
            cast_toks = {}
            sb = ctx["sb"][0]
            for c in range(CH):
                S.op("vector", lambda e, c=c, so=so: e.tensor_tensor(
                    atmp[:], xr[:, c * T + so:c * T + so + SL],
                    PS[0:128, 2560:2560 + SL], ALU.mult), inc="ve",
                    writes=("atmp",))
                S.op("vector", lambda e, c=c, ping=ping: e.tensor_tensor(
                    ping[:, c * SL:(c + 1) * SL], atmp[:],
                    PS[0:128, 2560 + SL:2560 + 2 * SL], ALU.add), inc="ve",
                    reads=("atmp",), writes=(ping.name + str(c),))
                ap_tok = S.tok("ve")
                S.wait("gpsimd", ap_tok)
                cast_toks[c] = S.op("gpsimd", lambda e, c=c, ping=ping, sb=sb: e.tensor_copy(
                    sb[:, c * SL:(c + 1) * SL], ping[:, c * SL:(c + 1) * SL]),
                    inc="gp", reads=(ping.name + str(c),),
                    writes=("%sc%d" % (sb.name, c),))
            ctx["cast_toks"] = cast_toks
            return S.tok("ve")

        def emit_sweeps(ctx, cyc, mid=None):
            srcb = ctx["sb"][cyc % 2]
            cast_toks = ctx["cast_toks"]
            h1e = {}
            for kc in range(CH):
                S.wait("tensor", cast_toks[kc])
                for mi in range(4):
                    S.op("tensor", lambda e, mi=mi, kc=kc, srcb=srcb: e.matmul(
                        PS[0:128, mi * 512:mi * 512 + SL],
                        wcn1[:, kc * D + mi * 128:kc * D + (mi + 1) * 128],
                        srcb[:, kc * SL:(kc + 1) * SL],
                        start=(kc == 0), stop=(kc == CH - 1)))
            sw_tok = S.close("tensor", "te")
            S.wait("scalar", sw_tok)
            for mi in range(4):
                h1e[mi] = S.op("scalar", lambda e, mi=mi: e.activation(
                    h1b[:, mi * SL:(mi + 1) * SL],
                    PS[0:128, mi * 512:mi * 512 + SL],
                    AF.Gelu, bias=b1s[:, mi:mi + 1]), inc="ac",
                    writes=("h1b%d" % mi,))
            if mid is not None:
                mid()
            for mi in range(4):
                m = 4 + mi
                S.wait("tensor", h1e[mi])
                for kc in range(CH):
                    S.op("tensor", lambda e, m=m, mi=mi, kc=kc, srcb=srcb: e.matmul(
                        PS[0:128, mi * 512:mi * 512 + SL],
                        wcn1[:, kc * D + m * 128:kc * D + (m + 1) * 128],
                        srcb[:, kc * SL:(kc + 1) * SL],
                        start=(kc == 0), stop=(kc == CH - 1)))
                t = S.close("tensor", "te")
                S.wait("scalar", t)
                h1e[m] = S.op("scalar", lambda e, m=m, mi=mi: e.activation(
                    h1b[:, m * SL:(m + 1) * SL],
                    PS[0:128, mi * 512:mi * 512 + SL],
                    AF.Gelu, bias=b1s[:, m:m + 1]), inc="ac",
                    writes=("h1b%d" % m,))
            h2e = {}
            for m in range(CH // 2):
                S.wait("tensor", h1e[m + 4])
                for kc in range(CH):
                    S.wait("tensor", h1e[kc])
                    S.op("tensor", lambda e, m=m, kc=kc: e.matmul(
                        PS[0:128, m * 512:m * 512 + SL],
                        wcn2[:, kc * (D // 2) + m * 128:kc * (D // 2) + (m + 1) * 128],
                        h1b[:, kc * SL:(kc + 1) * SL],
                        start=(kc == 0), stop=(kc == CH - 1)))
                mm_tok = S.close("tensor", "te")
                S.wait("scalar", mm_tok)
                h2e[m] = S.op("scalar", lambda e, m=m: e.activation(
                    h2b[:, m * SL:(m + 1) * SL], PS[0:128, m * 512:m * 512 + SL],
                    AF.Gelu, bias=b2s[:, m:m + 1]), inc="ac", writes=("h2b%d" % m,))
            h3e = {}
            for m in range(CH // 4):
                S.wait("tensor", h2e[m])
                for kc in range(CH // 2):
                    S.wait("tensor", h2e[kc])
                    S.op("tensor", lambda e, m=m, kc=kc: e.matmul(
                        PS[0:128, m * 512:m * 512 + SL],
                        wcn3[:, kc * (D // 4) + m * 128:kc * (D // 4) + (m + 1) * 128],
                        h2b[:, kc * SL:(kc + 1) * SL],
                        start=(kc == 0), stop=(kc == CH // 2 - 1)))
                mm_tok = S.close("tensor", "te")
                S.wait("scalar", mm_tok)
                h3e[m] = S.op("scalar", lambda e, m=m: e.activation(
                    h3b[:, m * SL:(m + 1) * SL], PS[0:128, m * 512:m * 512 + SL],
                    AF.Gelu, bias=b3s[:, m:m + 1]), inc="ac", writes=("h3b%d" % m,))
            for kc in range(CH // 4):
                S.wait("tensor", h3e[kc])
                S.op("tensor", lambda e, kc=kc: e.matmul(
                    PS[0:1, 3072:3072 + SL], wcn4[:, kc:kc + 1],
                    h3b[:, kc * SL:(kc + 1) * SL],
                    start=(kc == 0), stop=(kc == CH // 4 - 1)))
            tf_tok = S.close("tensor", "te")
            S.wait("scalar", tf_tok)
            S.wait("scalar", ab_tok)
            tfr, cpr = ctx["tfr"], ctx["cpr"]
            S.op("scalar", lambda e, tfr=tfr: e.activation(
                tfr[:], PS[0:1, 3072:3072 + SL], AF.Tanh,
                bias=sc(SC_B4H)[0:1, 0:1], scale=0.5), inc="ac",
                writes=(tfr.name,))
            S.op("scalar", lambda e, tfr=tfr, cpr=cpr: e.activation(
                cpr[:], tfr[:], AF.Identity, bias=sc(SC_A1)[0:1, 0:1],
                scale=sc(SC_B1)[0:1, 0:1]), inc="ac",
                reads=(tfr.name, "sc8"), writes=(cpr.name,))
            ctx["cplv"] = S.tok("ac")

        def emit_tail(ctx, cyc):
            so = ctx["so"]
            state, other = ctx["state"], ctx["other"]
            cpr = ctx["cpr"]
            cb = ctx["cb"]
            S.wait("tensor", ctx["cplv"])
            S.wait("tensor", S.tok("ve"))
            for (o, n) in nslices(SL):
                S.op("tensor", lambda e, o=o, n=n, cpr=cpr, cb=cb: e.matmul(
                    PS[0:128, cb + o:cb + o + n], ones_f[:],
                    cpr[0:1, o:o + n], start=True, stop=True))
            bc_tok = S.close("tensor", "te")
            S.wait("vector", bc_tok)
            S.wait("gpsimd", S.tok("te"))
            cast_toks = {}
            for c in range(CH):
                pc = (c + 4) % CH
                op1 = ALU.add if c in (0, 1, 6, 7) else ALU.subtract
                tt = ttmp0 if c % 2 == 0 else ttmp1
                S.op("vector", lambda e, c=c, pc=pc, tt=tt, cb=cb,
                     state=state: e.tensor_tensor(
                    tt[:], state[:, pc * SL:(pc + 1) * SL],
                    PS[0:128, cb:cb + SL], ALU.mult), inc="ve",
                    reads=("%s%d" % (state.name, pc),), writes=(tt.name,))
                tw = S.op("vector", lambda e, c=c, tt=tt, op1=op1, state=state,
                     other=other: e.tensor_tensor(
                    other[:, c * SL:(c + 1) * SL],
                    state[:, c * SL:(c + 1) * SL], tt[:], op1),
                    inc="ve", reads=(tt.name, "%s%d" % (state.name, c)),
                    writes=("%s%d" % (other.name, c),))
                if cyc < 2:
                    nsrcb = ctx["sb"][(cyc + 1) % 2]
                    S.wait("gpsimd", tw)
                    cast_toks[c] = S.op("gpsimd", lambda e, c=c, other=other,
                                        nsrcb=nsrcb: e.tensor_copy(
                        nsrcb[:, c * SL:(c + 1) * SL],
                        other[:, c * SL:(c + 1) * SL]), inc="gp",
                        reads=("%s%d" % (other.name, c),),
                        writes=("%sc%d" % (nsrcb.name, c),))
                else:
                    S.op("vector", lambda e, c=c, so=so, other=other: e.scalar_tensor_tensor(
                        xr[:, c * T + so:c * T + so + SL],
                        other[:, c * SL:(c + 1) * SL], 0.5,
                        xr[:, c * T + so:c * T + so + SL], ALU.mult, ALU.add),
                        inc="ve", reads=("%s%d" % (other.name, c), "xr%d" % c),
                        writes=("xr%d" % c,))
            if cyc == 2 and store_cb is not None:
                store_cb(so, S.tok("ve"))
            ctx["state"], ctx["other"] = other, state
            ctx["cast_toks"] = cast_toks

        ctxs = {0: dict(state=pingA, other=pongA, sb=(srcbA0, srcbA1),
                        tfr=tf_row, cpr=cpl_row, cb=2048),
                1: dict(state=pingB, other=pongB, sb=(srcbB0, srcbB1),
                        tfr=ncpl_row, cpr=cplB_row, cb=3584)}
        last_ab = None

        def do_apply(s):
            nonlocal last_ab
            ctx = ctxs[s % 2]
            ctx["so"] = s * SL
            ctx["state"], ctx["other"] = (
                (pingA, pongA) if s % 2 == 0 else (pingB, pongB))
            S.wait("vector", S.tok("gp"))
            S.wait("gpsimd", S.tok("te"))
            emit_apply(ctx, last_ab)
            last_ab = S.tok("ve")

        do_apply(0)
        do_apply(1)
        prev = None
        for wave in range(NSL // 2):
            a, b = 2 * wave, 2 * wave + 1
            for cyc in range(3):
                emit_sweeps(ctxs[0], cyc, mid=prev)
                prev = (lambda c=cyc: emit_tail(ctxs[0], c))
                if cyc == 2:
                    ta = prev
                    def mk1(ta=ta, s=a + 2):
                        def f():
                            ta()
                            if s < NSL:
                                do_apply(s)
                        return f
                    prev = mk1()
                emit_sweeps(ctxs[1], cyc, mid=prev)
                prev = (lambda c=cyc: emit_tail(ctxs[1], c))
            if b + 2 < NSL:
                tb = prev
                def mk2(tb=tb, s=b + 2):
                    def f():
                        tb()
                        do_apply(s)
                    return f
                prev = mk2()
        prev()
        return S.tok("ve")

    S.wait("vector", consts_tok)
    S.wait("tensor", consts_tok)
    preload_set6()
    rows1, gc1 = emit_ln(want_gc=True, gates=x_toks)
    S.op("scalar", lambda e: e.preload_act_table(AF.Gelu))
    ab1 = emit_gcnet(gc1, ffnw_sw, ffnw_hw)
    S.wait("tensor", ffnw_hw)
    S.wait("scalar", ffnw_hw)
    ffn1_done = emit_ffn(rows1, ab1)
    ffn1_gp = S.tok("gp")

    S.drain("vector")
    S.drain("scalar")
    S.wait("vector", ffn1_done)
    S.wait("vector", ffn1_gp)
    S.wait("scalar", S.tok("ve"))
    S.wait("scalar", ffn1_gp)
    S.wait("tensor", ffn1_gp)
    preload_set6()
    rows2, _ = emit_ln(want_gc=False)
    rows2_ac = S.tok("ac")

    S.wait("gpsimd", rows2)
    S.wait("gpsimd", rows2_ac)
    S.dma_batch("gpsimd", "sw")
    for c in range(CH):
        S.op("gpsimd", lambda e, c=c: e.dma_start(
            Aq_sb[:, c * 8:(c + 1) * 8], Aq[c * 128:(c + 1) * 128, :]),
            inc="sw", n=16)
    S.op("gpsimd", lambda e: e.dma_start(
        Bqv[:], Bq[:, 2 * D:3 * D]), inc="sw", n=16)
    S.op("gpsimd", lambda e: e.dma_start(
        bqv[:], bqkv.rearrange("(a o) -> a o", a=1)[0:1, 2 * D:3 * D]),
        inc="sw", n=16)
    for c in range(CH):
        S.op("gpsimd", lambda e, c=c: e.dma_start(
            Wv_sb[:, c * D:(c + 1) * D], Wqkv[c * 128:(c + 1) * 128, 2 * D:3 * D]),
            inc="sw", n=16)
    attw_tok = S.tok("sw")

    S.drain("vector")
    S.op("vector", lambda e: e.memset(
        vaug[:].rearrange("p (k h o) -> p (k h) o", h=H, o=65)[:, :, 64:65], 1.0),
        inc="ve", writes=("vaug_ones",))

    ap_toks = {}
    tq_ev = {}
    prev_vt = {}

    def emit_tq(s):
        so = s * SL
        S.wait("tensor", ap_toks[s])
        S.wait("tensor", attw_tok)
        if s >= 1:
            S.wait("tensor", tq_ev[s - 1])
        for kc in range(CH):
            S.op("tensor", lambda e, so=so, kc=kc: e.matmul(
                PS[0:8, 2048:2048 + SL], Aq_sb[:, kc * 8:(kc + 1) * 8],
                xn2[:, kc * T + so:kc * T + so + SL],
                start=(kc == 0), stop=(kc == CH - 1)))
        tq_mm = S.close("tensor", "te")
        S.wait("scalar", tq_mm)
        tq_ev[s] = S.op("scalar", lambda e, so=so: e.activation(
            tq[0:8, so:so + SL], PS[0:8, 2048:2048 + SL],
            AF.Copy, scale=2.0), inc="ac", writes=("tq",))

    def emit_vtok(kt):
        pc0 = 0 if kt % 2 == 0 else 1024
        S.wait("tensor", tq_ev[(kt * 128) // SL])
        if kt >= 2:
            S.wait("tensor", prev_vt[kt - 2])
        for j, (o, n) in enumerate(nslices(D)):
            for kc in range(CH):
                S.op("tensor", lambda e, kt=kt, kc=kc, o=o, n=n, pc0=pc0: e.matmul(
                    PS[0:128, pc0 + o:pc0 + o + n],
                    xn2[:, kc * T + kt * 128:kc * T + (kt + 1) * 128],
                    Wv_sb[:, kc * D + o:kc * D + o + n],
                    start=(kc == 0), stop=False))
            S.op("tensor", lambda e, kt=kt, o=o, n=n, pc0=pc0: e.matmul(
                PS[0:128, pc0 + o:pc0 + o + n],
                tq[0:8, kt * 128:(kt + 1) * 128],
                Bqv[0:8, o:o + n], start=False, stop=False))
            S.op("tensor", lambda e, kt=kt, o=o, n=n, pc0=pc0: e.matmul(
                PS[0:128, pc0 + o:pc0 + o + n],
                ones_b[0:1, 0:128], bqv[0:1, o:o + n],
                start=False, stop=True))
        vt_mm = S.close("tensor", "te")
        S.wait("vector", vt_mm)
        prev_vt[kt] = S.op("vector", lambda e, kt=kt, pc0=pc0: e.tensor_copy(
            vaug[:].rearrange("p (k h o) -> p k h o", h=H, o=65)[:, kt, :, 0:64],
            PS[0:128, pc0:pc0 + D].rearrange("p (h o) -> p h o", o=64)),
            inc="ve", writes=("vaug%d" % kt,))

    prev = None
    KT_PER_SL = SL // 128
    for s in range(NSL):
        so = s * SL
        S.wait("tensor", rows2)
        if prev is not None:
            S.wait("tensor", prev)
        for (o, n) in nslices(SL):
            S.op("tensor", lambda e, o=o, n=n, so=so: e.matmul(
                PS[0:128, 2560 + o:2560 + o + n], ones_f[:],
                a_row[0:1, so + o:so + o + n], start=True, stop=True))
            S.op("tensor", lambda e, o=o, n=n, so=so: e.matmul(
                PS[0:128, 3072 + o:3072 + o + n], ones_f[:],
                b_row[0:1, so + o:so + o + n], start=True, stop=True))
        ab_bc = S.close("tensor", "te")
        S.wait("vector", ab_bc)
        for c in range(CH):
            S.op("vector", lambda e, c=c, so=so: e.tensor_tensor(
                atmp[:], xr[:, c * T + so:c * T + so + SL],
                PS[0:128, 2560:2560 + SL], ALU.mult), inc="ve", writes=("atmp",))
            S.op("vector", lambda e, c=c, so=so: e.tensor_tensor(
                xn2[:, c * T + so:c * T + so + SL], atmp[:],
                PS[0:128, 3072:3072 + SL], ALU.add), inc="ve",
                reads=("atmp",), writes=("xn2_%d" % c,))
        ap_toks[s] = prev = S.tok("ve")
        if s >= 1:
            emit_tq(s - 1)
        if s >= 2:
            for k in range(KT_PER_SL):
                emit_vtok(KT_PER_SL * (s - 2) + k)
    emit_tq(NSL - 1)
    for kt in range(KT_PER_SL * (NSL - 2), KT):
        emit_vtok(kt)
    vaug_tok = S.tok("ve")

    S.wait("tensor", vaug_tok)
    bq_row = bqkv.rearrange("(a o) -> a o", a=1)
    Wqc = Wqkv.rearrange("(kc q) m -> q kc m", q=128)
    Bq3 = Bq.rearrange("r (b m) -> r b m", b=3)
    br3 = bq_row.rearrange("a (b m) -> a b m", b=3)
    qk_q_toks = {}
    qk_k_toks = {}
    for p in range(CH):
        if p == 0:
            S.wait("gpsimd", S.tok("te"))
        else:
            S.wait("gpsimd", qk_q_toks[p - 1])
        S.dma_batch("gpsimd", "sw2")
        S.op("gpsimd", lambda e, p=p: e.dma_start(
            wqt[:].rearrange("q (kc m) -> q kc m", m=128),
            Wqc[:, :, p * 128:(p + 1) * 128]), inc="sw2", n=16)
        wq_tok = S.tok("sw2")
        if p > 0:
            S.wait("gpsimd", qk_k_toks[p - 1])
        S.dma_batch("gpsimd", "sw")
        S.op("gpsimd", lambda e, p=p: e.dma_start(
            wkt[:].rearrange("q (kc m) -> q kc m", m=128),
            Wqc[:, :, D + p * 128:D + (p + 1) * 128]), inc="sw", n=16)
        S.op("gpsimd", lambda e, p=p: e.dma_start(
            bqqk0[:].rearrange("r (b m) -> r b m", b=2),
            Bq3[:, 0:2, p * 128:(p + 1) * 128]), inc="sw", n=16)
        S.op("gpsimd", lambda e, p=p: e.dma_start(
            brqk0[:].rearrange("a (b m) -> a b m", b=2),
            br3[:, 0:2, p * 128:(p + 1) * 128]), inc="sw", n=16)
        wtok = S.tok("sw")
        if p > 0:
            S.wait("tensor", S.tok("ve"))
        for which, bsl in (("q", 0), ("k", 1)):
            S.wait("tensor", wq_tok if which == "q" else wtok)
            blo = bqqk0[0:8, bsl * 128:(bsl + 1) * 128]
            bro = brqk0[0:1, bsl * 128:(bsl + 1) * 128]
            wt = wqt if which == "q" else wkt
            ps0 = 0 if which == "q" else 2048
            for (o, n) in nslices(T):
                for kc in range(CH):
                    S.op("tensor", lambda e, kc=kc, o=o, n=n, wt=wt, ps0=ps0: e.matmul(
                        PS[0:128, ps0 + o:ps0 + o + n],
                        wt[:, kc * 128:(kc + 1) * 128],
                        xn2[:, kc * T + o:kc * T + o + n],
                        start=(kc == 0), stop=False))
                S.wait("tensor", wtok)
                S.op("tensor", lambda e, o=o, n=n, blo=blo, ps0=ps0: e.matmul(
                    PS[0:128, ps0 + o:ps0 + o + n],
                    blo, tq[0:8, o:o + n], start=False, stop=False))
                S.op("tensor", lambda e, o=o, n=n, bro=bro, ps0=ps0: e.matmul(
                    PS[0:128, ps0 + o:ps0 + o + n],
                    bro, ones_b[0:1, 0:n], start=False, stop=True))
            mm_tok = S.close("tensor", "te")
            if which == "q":
                qk_q_toks[p] = mm_tok
            else:
                qk_k_toks[p] = mm_tok
            S.wait("vector", mm_tok)
            dst = Qs if which == "q" else Ks
            S.op("vector", lambda e, p=p, dst=dst, ps0=ps0: e.tensor_copy(
                dst[:, p * T:(p + 1) * T], PS[0:128, ps0:ps0 + T]), inc="ve",
                writes=("%s%d" % (dst.name, p),))
    qk_tok = S.tok("ve")

    S.wait("gpsimd", qk_tok)
    S.dma_batch("gpsimd", "sw")
    for c in range(CH):
        S.op("gpsimd", lambda e, c=c: e.dma_start(
            wstage[:, c * 8:(c + 1) * 8], Apj[c * 128:(c + 1) * 128, :]),
            inc="sw", n=16)
    S.op("gpsimd", lambda e: e.dma_start(Bp_sb[:], Bpj[:]), inc="sw", n=16)
    projw_tok = S.tok("sw")
    S.wait("vector", projw_tok)
    for w in range(2):
        for g in range(4):
            S.op("vector", lambda e, w=w, g=g: e.transpose(
                aptmp[w * 32:(w + 1) * 32, g * 32:(g + 1) * 32],
                wstage[g * 32:(g + 1) * 32, w * 32:(w + 1) * 32]), inc="ve",
                reads=("wstage",), writes=("aptmp",))
    for c in range(CH):
        mask = [8 * (c % 4) + min(i, 7) for i in range(32)]
        S.op("vector", lambda e, c=c, mask=mask: e.stream_shuffle(
            ApjT[0:32, c * 128:(c + 1) * 128],
            aptmp[(c // 4) * 32:(c // 4) * 32 + 32, 0:128], mask), inc="ve",
            reads=("aptmp",), writes=("ApjT",))
    apjt_ve = S.tok("ve")
    Wpc = Wproj.rearrange("(kc q) m -> q kc m", q=128)
    wpf_toks = {}
    wb_last = [None]
    wst_tok = [None]

    def emit_wpf_dma(m):
        if wb_last[0] is not None:
            S.wait("sync", wb_last[0])
        S.dma_batch("sync", "hw")
        S.op("sync", lambda e, m=m: e.dma_start(
            wpf[:].rearrange("q (kc m) -> q kc m", m=128),
            Wpc[:, :, m * 128:(m + 1) * 128]), inc="hw", n=16)
        wpf_toks[m] = S.tok("hw")

    S.wait("sync", S.tok("te"))
    emit_wpf_dma(0)
    bp_row = bproj.rearrange("(a o) -> a o", a=1)

    def emit_wbuild(m):
        S.wait("tensor", wpf_toks[m])
        S.wait("tensor", projw_tok)
        S.wait("tensor", apjt_ve)
        if wb_last[0] is not None:
            S.wait("tensor", wb_last[0])
        if wst_tok[0] is not None:
            S.wait("vector", wst_tok[0])
        vt = {}
        for kc in range(CH):
            c0 = 2048 + (kc % 2) * 512 + ((kc // 2) % 2) * 128
            if kc >= 2:
                S.wait("tensor", vt[kc - 2])
            S.op("tensor", lambda e, kc=kc, c0=c0, m=m: e.matmul(
                PS[0:128, c0:c0 + 128], ApjT[0:8, kc * 128:(kc + 1) * 128],
                Bp_sb[:, m * 128:(m + 1) * 128], start=True, stop=True))
            mm = S.close("tensor", "te")
            S.wait("vector", mm)
            vt[kc] = S.op("vector", lambda e, kc=kc, c0=c0: e.scalar_tensor_tensor(
                wstage[:, kc * 128:(kc + 1) * 128], PS[0:128, c0:c0 + 128],
                2.0, wpf[:, kc * 128:(kc + 1) * 128], ALU.mult, ALU.add),
                inc="ve", reads=("wpf",), writes=("wstage",))
        wb_last[0] = S.tok("ve")
        S.wait("gpsimd", wb_last[0])
        S.dma_batch("gpsimd", "sw2")
        S.op("gpsimd", lambda e, m=m: e.dma_start(wph[m], wstage[:]),
             inc="sw2", n=16)
        wst_tok[0] = S.tok("sw2")
        if m + 1 < CH:
            emit_wpf_dma(m + 1)

    S.drain("vector")
    S.drain("scalar")
    S.wait("scalar", qk_tok)
    bounds_mm = []
    slot = 0
    for which, src in (("q", Qs), ("k", Ks)):
        S.wait("tensor", S.tok("ve"))
        for c in range(CH):
            qb_ = qsq if c % 2 == 0 else qsq2
            if c % 2 == 0:
                S.op("scalar", lambda e, c=c, src=src, qb_=qb_: e.activation(
                    qb_[:], src[:, c * T:(c + 1) * T], AF.Square), inc="ac",
                    writes=(qb_.name,))
                sq_tok = S.tok("ac")
            else:
                S.op("vector", lambda e, c=c, src=src, qb_=qb_: e.tensor_tensor(
                    qb_[:], src[:, c * T:(c + 1) * T],
                    src[:, c * T:(c + 1) * T], ALU.mult), inc="ve",
                    writes=(qb_.name,))
                sq_tok = S.tok("ve")
            if slot % 2 == 1:
                emit_wbuild(slot // 2)
            slot += 1
            S.wait("tensor", sq_tok)
            for (o, n) in nslices(T):
                S.op("tensor", lambda e, c=c, o=o, n=n, qb_=qb_: e.matmul(
                    PS[0:16, o:o + n], selq[:, 16 * c:16 * (c + 1)],
                    qb_[:, o:o + n], start=(c == 0), stop=(c == CH - 1)))
            mm = S.close("tensor", "te")
            if c % 2 == 0:
                S.wait("scalar", mm)
            else:
                S.wait("vector", mm)
        bounds_mm.append(S.tok("te"))
        S.wait("vector", bounds_mm[-1])
        col = 4 if which == "q" else 5
        S.op("vector", lambda e, col=col: e.reduce_max(
            sq32[0:16, col:col + 1], PS[0:16, 0:T], axis=AX.X),
            inc="ve", writes=("sq32",))
    S.op("vector", lambda e: e.tensor_tensor(
        sq32[0:16, 6:7], sq32[0:16, 4:5], sq32[0:16, 5:6], ALU.mult),
        inc="ve", reads=("sq32",), writes=("sq32",))
    S.wait("scalar", S.tok("ve"))
    S.op("scalar", lambda e: e.activation(
        sq32[0:16, 7:8], sq32[0:16, 6:7], AF.Ln), inc="ac",
        reads=("sq32",), writes=("sq32",))
    S.op("scalar", lambda e: e.activation(
        sq32[0:16, 7:8], sq32[0:16, 7:8], AF.Exp, scale=0.5), inc="ac",
        reads=("sq32",), writes=("sq32",))
    S.wait("vector", S.tok("ac"))
    S.op("vector", lambda e: e.tensor_scalar(
        sq32[0:16, 0:1], sq32[0:16, 7:8], -1.05 / 8.0, -0.125,
        ALU.mult, ALU.add), inc="ve", reads=("sq32",), writes=("sq32",))
    S.op("vector", lambda e: e.transpose(sq32b[0:32, 0:32], sq32[0:32, 0:32]),
         inc="ve", reads=("sq32",), writes=("sq32b",))
    S.op("vector", lambda e: e.tensor_tensor(
        sq32b[0:1, 16:16 + H // 2],
        sq32b[0:1, 0:H].rearrange("p (h two) -> p h two", two=2)[:, :, 0],
        sq32b[0:1, 0:H].rearrange("p (h two) -> p h two", two=2)[:, :, 1],
        ALU.min), inc="ve", reads=("sq32b",), writes=("sq32b",))
    bias_row = S.tok("ve")
    S.wait("tensor", bias_row)
    S.op("tensor", lambda e: e.matmul(
        PS[0:128, 3072:3072 + H // 2], ones_f[:], sq32b[0:1, 16:16 + H // 2],
        start=True, stop=True), inc="te")
    S.wait("vector", S.tok("te"))
    S.op("vector", lambda e: e.tensor_copy(biasP[:], PS[0:128, 3072:3072 + H // 2]),
         inc="ve", writes=("biasP",))
    biasH_tok = S.tok("ve")

    S.wait("scalar", biasH_tok)
    S.wait("tensor", vaug_tok)
    st_last = {}
    pt_last = {}
    evict_toks = {}
    prev_resc = [None]
    pending = [None]
    RS_BASE = max(2, KT - 6)

    def emit_rescale_step(step):
        if pending[0] is None:
            return
        lrow, araw, p, rcp_tok, rb0c, aq = pending[0][:6]
        if step == 0:
            S.wait("tensor", rcp_tok)
            if wb_last[0] is not None:
                S.wait("tensor", wb_last[0])
            if prev_resc[0] is not None:
                S.wait("tensor", prev_resc[0])
            S.op("tensor", lambda e, lrow=lrow, rb0c=rb0c: e.matmul(
                PS[0:64, rb0c:rb0c + QBL], ones_f[0:1, 0:64],
                lrow[0:1, 0:QBL], start=True, stop=True), inc="te")
            pending[0] = pending[0][:6] + (S.tok("te"),)
        elif step == 1:
            rbA = pending[0][6]
            S.wait("vector", rbA)
            S.op("vector", lambda e, p=p, araw=araw, rb0c=rb0c, aq=aq: e.tensor_tensor(
                aq[0:64, p * QBL:(p + 1) * QBL], araw[:, 0:QBL],
                PS[0:64, rb0c:rb0c + QBL], ALU.mult), inc="ve",
                reads=(araw.name,), writes=("%sA%d" % (aq.name, p),))
            pending[0] = pending[0][:6] + (S.tok("ve"),)
        elif step == 2:
            ttA = pending[0][6]
            S.wait("tensor", ttA)
            S.op("tensor", lambda e, lrow=lrow, rb0c=rb0c: e.matmul(
                PS[0:64, rb0c:rb0c + QBL], ones_f[0:1, 0:64],
                lrow[0:1, QBL:2 * QBL], start=True, stop=True), inc="te")
            pending[0] = pending[0][:6] + (S.tok("te"),)
        else:
            rbB = pending[0][6]
            S.wait("vector", rbB)
            S.op("vector", lambda e, p=p, araw=araw, rb0c=rb0c, aq=aq: e.tensor_tensor(
                aq[64:128, p * QBL:(p + 1) * QBL], araw[:, QBL:2 * QBL],
                PS[0:64, rb0c:rb0c + QBL], ALU.mult), inc="ve",
                reads=(araw.name,), writes=("%sB%d" % (aq.name, p),))
            prev_resc[0] = S.tok("ve")
            pending[0] = None

    def flush_rescale():
        for st in range(4):
            emit_rescale_step(st)

    prm_te = {}
    wp_toks = {}
    prev_projev = {}
    NCHUNK = NQB * CH

    def emit_wp_dma(gi):
        wp = (wpt, wpt2)[gi % 2]
        bpr = (bp_sl0, bp_sl1)[gi % 2]
        mf = gi % CH
        dsem = ("sw", "sw2")[gi % 2]
        if gi >= 2:
            S.wait("gpsimd", prm_te[gi - 2])
        S.dma_batch("gpsimd", dsem)
        S.op("gpsimd", lambda e, wp=wp, mf=mf: e.dma_start(wp[:], wph[mf]),
             inc=dsem, n=16)
        S.op("gpsimd", lambda e, mf=mf, bpr=bpr: e.dma_start(
            bpr[:], bp_row[0:1, mf * 128:(mf + 1) * 128]), inc=dsem, n=16)
        wp_toks[gi] = S.tok(dsem)

    def emit_proj_piece(gi, fb, piece, claim_key, ev_tok):
        src_qb, mf = divmod(gi, CH)
        aq = (attnq, attnq2)[src_qb % 2]
        qo2 = src_qb * QBL
        wp = (wpt, wpt2)[gi % 2]
        bpr = (bp_sl0, bp_sl1)[gi % 2]
        kcs = (range(0, 3), range(3, 6), range(6, CH))[piece]
        if piece == 0:
            S.wait("tensor", wp_toks[gi])
            if ev_tok is not None:
                S.wait("tensor", ev_tok)
            if prev_resc[0] is not None:
                S.wait("tensor", prev_resc[0])
        for kc in kcs:
            S.op("tensor", lambda e, kc=kc, wp=wp, aq=aq, fb=fb: e.matmul(
                PS[0:128, fb:fb + QBL], wp[:, kc * 128:(kc + 1) * 128],
                aq[:, kc * QBL:kc * QBL + QBL],
                start=(kc == 0), stop=False))
        if piece < 2:
            return None
        S.op("tensor", lambda e, bpr=bpr, fb=fb: e.matmul(
            PS[0:128, fb:fb + QBL], bpr[0:1, 0:128], ones_b[0:1, 0:QBL],
            start=False, stop=True))
        prm = S.close("tensor", "te")
        prm_te[gi] = prm
        S.wait("vector", prm)
        ev = S.op("vector", lambda e, mf=mf, qo2=qo2, fb=fb: e.tensor_tensor(
            xr[:, mf * T + qo2:mf * T + qo2 + QBL], PS[0:128, fb:fb + QBL],
            xr[:, mf * T + qo2:mf * T + qo2 + QBL], ALU.add), inc="ve",
            reads=("xr%d" % mf,), writes=("xr%d" % mf,))
        if claim_key is not None:
            prev_projev[claim_key] = ev
        if gi + 2 < NCHUNK:
            emit_wp_dma(gi + 2)
        return ev

    def emit_proj_chunk(gi, fb, claim_key, ev_tok):
        for piece in range(3):
            ev = emit_proj_piece(gi, fb, piece, claim_key, ev_tok)
        return ev

    PTS = (PT0, PT1, PT2, PT3)
    exp_all = {}
    NSLOT = NQB * CH * KT
    ev_final = {}

    def emit_st_exp(gs):
        pp, kt = divmod(gs, KT)
        qb, p = divmod(pp, CH)
        qo = qb * QBL
        pt = PTS[gs % 4]
        if gs >= 2:
            S.wait("tensor", exp_all[gs - 2])
        S.op("tensor", lambda e, kt=kt, p=p, qo=qo, gs=gs: e.matmul(
            PS[0:128, 2 * QBL * (gs % 2):2 * QBL * (gs % 2) + QBL],
            Ks[0:64, p * T + kt * 128:p * T + (kt + 1) * 128],
            Qs[0:64, p * T + qo:p * T + qo + QBL],
            start=True, stop=True, tile_position=(0, 0)))
        S.op("tensor", lambda e, kt=kt, p=p, qo=qo, gs=gs: e.matmul(
            PS[0:128, 2 * QBL * (gs % 2) + QBL:2 * QBL * (gs % 2) + 2 * QBL],
            Ks[64:128, p * T + kt * 128:p * T + (kt + 1) * 128],
            Qs[64:128, p * T + qo:p * T + qo + QBL],
            start=True, stop=True, tile_position=(64, 0)))
        st_tok = S.close("tensor", "te")
        S.wait("scalar", st_tok)
        exp_all[gs] = S.op("scalar", lambda e, gs=gs, pt=pt, p=p: e.activation(
            pt[:], PS[0:128, 2 * QBL * (gs % 2):2 * QBL * (gs % 2) + 2 * QBL],
            AF.Exp, bias=biasP[:, p:p + 1], scale=0.125), inc="ac",
            writes=(pt.name,))

    def emit_pv_slot(gp):
        pp, kt = divmod(gp, KT)
        qb, p = divmod(pp, CH)
        par_p = pp % 2
        lrow = (lrow0, lrow1)[par_p]
        araw = (araw0, araw1)[par_p]
        pv0 = 2048 + 1024 * par_p
        pt = PTS[gp % 4]
        if kt == 0:
            S.wait("tensor", evict_toks.get(pp - 2))
            if pp <= 1 and wb_last[0] is not None:
                S.wait("tensor", wb_last[0])
            S.wait("tensor", prev_projev.get(pp - 1))
            if prev_resc[0] is not None:
                S.wait("tensor", prev_resc[0])
        S.op("tensor", lambda e, kt=kt, pt=pt, p=p, pv0=pv0: e.matmul(
            PS[0:65, pv0:pv0 + QBL],
            vaug[:, kt * H * 65 + 2 * p * 65:kt * H * 65 + (2 * p + 1) * 65],
            pt[:, 0:QBL], start=(kt == 0), stop=(kt == KT - 1)))
        S.op("tensor", lambda e, kt=kt, pt=pt, p=p, pv0=pv0: e.matmul(
            PS[0:65, pv0 + 512:pv0 + 512 + QBL],
            vaug[:, kt * H * 65 + (2 * p + 1) * 65:kt * H * 65 + (2 * p + 2) * 65],
            pt[:, QBL:2 * QBL], start=(kt == 0), stop=(kt == KT - 1)))
        pvc = S.close("tensor", "te")
        if kt == KT - 1:
            S.wait("vector", pvc)
            pv2 = PS[:, pv0:pv0 + 1024].rearrange(
                "p (b c) -> p b c", b=2)[:, :, 0:QBL]
            S.op("vector", lambda e, lrow=lrow, pv2=pv2: e.tensor_copy(
                lrow[:].rearrange("p (b c) -> p b c", b=2), pv2[64:65]),
                inc="ve", writes=(lrow.name,))
            S.op("vector", lambda e, araw=araw, pv2=pv2: e.tensor_copy(
                araw[:].rearrange("p (b c) -> p b c", b=2), pv2[0:64]),
                inc="ve", writes=(araw.name,))
            evict_toks[pp] = S.tok("ve")
            S.op("vector", lambda e, lrow=lrow: e.reciprocal(
                lrow[:], lrow[:]), inc="ve", reads=(lrow.name,),
                writes=(lrow.name,))
            pending[0] = (lrow, araw, p, S.tok("ve"), pv0,
                          (attnq, attnq2)[qb % 2])
            if p == CH - 1 and qb == 0:
                for g in range(2):
                    emit_wp_dma(g)
        else:
            if RS_BASE <= kt < RS_BASE + 4:
                emit_rescale_step(kt - RS_BASE)
            if qb >= 1 and p >= 1 and (
                    (KT >= 12 and 5 <= kt <= 7) or (KT < 12 and kt == 5)):
                gi_m = (qb - 1) * CH + (p - 1)
                fb_m = 2048 + 1024 * ((pp - 1) % 2) + 512
                if KT >= 12:
                    emit_proj_piece(gi_m, fb_m, kt - 5, pp,
                                    evict_toks.get(pp - 1))
                else:
                    emit_proj_chunk(gi_m, fb_m, pp, evict_toks.get(pp - 1))
            if p == CH - 1 and qb >= 1 and (
                    (KT >= 12 and KT - 4 <= kt <= KT - 2) or
                    (KT < 12 and kt == KT - 2)):
                gi_b = (qb - 1) * CH + 7
                fb_b = 2048 + 1024 * ((pp - 1) % 2) + 512
                if KT >= 12:
                    emit_proj_piece(gi_b, fb_b, kt - (KT - 4), pp,
                                    prev_projev.get(pp))
                else:
                    emit_proj_chunk(gi_b, fb_b, pp, prev_projev.get(pp))

    for gs in range(NSLOT + 2):
        if gs < NSLOT:
            emit_st_exp(gs)
        else:
            S.wait("tensor", exp_all[gs - 2])
        if gs >= 2:
            emit_pv_slot(gs - 2)
    flush_rescale()
    prev_prm = {}
    for mf in range(CH):
        gi = (NQB - 1) * CH + mf
        fb = 1024 if mf % 2 == 0 else 2048
        if mf >= 2:
            S.wait("tensor", prev_prm[mf - 2])
        prev_prm[mf] = emit_proj_chunk(gi, fb, None, None)
    attn_done = S.tok("ve")

    S.wait("gpsimd", S.tok("te"))
    S.wait("sync", S.tok("ac"))
    S.wait("sync", S.tok("te"))
    ffnw2_sw, ffnw2_hw = emit_ffn_weight_loads()
    S.drain("vector")
    S.drain("scalar")
    S.wait("vector", attn_done)
    S.wait("scalar", attn_done)
    preload_set6()
    rows3, gc3 = emit_ln(want_gc=True)
    S.op("scalar", lambda e: e.preload_act_table(AF.Gelu))
    S.wait("tensor", ffnw2_sw)
    S.wait("scalar", ffnw2_sw)
    S.wait("tensor", ffnw2_hw)
    S.wait("scalar", ffnw2_hw)
    ab3 = emit_gcnet(gc3, ffnw2_sw, ffnw2_hw)

    def store_slice(so, tok):
        S.wait("sync", tok)
        S.dma_batch("sync", "hw")
        for c in range(CH):
            S.op("sync", lambda e, c=c, so=so: e.dma_start(
                y[c * 128:(c + 1) * 128, so:so + SL],
                xr[:, c * T + so:c * T + so + SL]), inc="hw", n=16)

    ffn2_done = emit_ffn(rows3, ab3, store_cb=store_slice)
    S.wait("sync", S.tok("hw"))

    S.build()
    return nc


_CACHE = {}


def _get_program(T):
    if T not in _CACHE:
        _CACHE[T] = build_program(T)
    return _CACHE[T]


def kernel(**inputs):
    x = np.asarray(inputs["L0_attn_x"], dtype=np.float32)
    B, T, D = x.shape
    nc = _get_program(T)
    wmap = {
        "Wqkv": "L0_attn_W_qkv", "bqkv": "L0_attn_b_qkv",
        "Aq": "L0_attn_A_qkv", "Bq": "L0_attn_B_qkv",
        "Wproj": "L0_attn_W_proj", "bproj": "L0_attn_b_proj",
        "Apj": "L0_attn_A_proj", "Bpj": "L0_attn_B_proj",
        "cw1": "L0_ffn_cn_w1", "cb1": "L0_ffn_cn_b1",
        "cw2": "L0_ffn_cn_w2", "cb2": "L0_ffn_cn_b2",
        "cw3": "L0_ffn_cn_w3", "cb3": "L0_ffn_cn_b3",
        "cw4": "L0_ffn_cn_w4", "cb4": "L0_ffn_cn_b4",
        "gw1": "L0_ffn_gc_w1", "gb1": "L0_ffn_gc_b1",
        "gw2": "L0_ffn_gc_w2", "gb2": "L0_ffn_gc_b2",
        "gw3": "L0_ffn_gc_w3", "gb3": "L0_ffn_gc_b3",
        "ar": "L0_ffn_adaptive_range",
    }
    shared = {k: np.ascontiguousarray(np.asarray(inputs[v], dtype=np.float32))
              for k, v in wmap.items()}
    in_maps = []
    for b in range(B):
        m = dict(shared)
        m["xT"] = np.ascontiguousarray(x[b].T)
        in_maps.append(m)
    res = run_bass_kernel_spmd(nc, in_maps, core_ids=list(range(N_CORES)))
    out = np.stack([np.asarray(res.results[b]["y"]).T for b in range(B)])
    return out.astype(np.float32)


# version 39
# speedup vs baseline: 1.0108x; 1.0084x over previous
import numpy as np
import concourse.bass as bass
import concourse.mybir as mybir
from concourse.bass_utils import run_bass_kernel_spmd

F32 = mybir.dt.float32
BF16 = mybir.dt.bfloat16
AF = mybir.ActivationFunctionType
ALU = mybir.AluOpType
AX = mybir.AxisListType

ENGINES = ("sync", "tensor", "scalar", "vector", "gpsimd")
SBUF_BASE = 16512
SBUF_END = 229376
N_CORES = 8


def nslices(width, maxn=512):
    out = []
    o = 0
    while o < width:
        n = min(maxn, width - o)
        out.append((o, n))
        o += n
    return out


class Sched:

    def __init__(self, nc):
        self.nc = nc
        self.q = {e: [] for e in ENGINES}
        self.sems = {}
        self.cnt = {}
        self.waited = {e: {} for e in ENGINES}
        self.written = {e: set() for e in ENGINES}
        self.readset = {e: set() for e in ENGINES}

    def sem(self, name):
        self.sems[name] = self.nc.alloc_semaphore(name)
        self.cnt[name] = 0

    def op(self, eng, fn, inc=None, n=1, reads=(), writes=()):
        q = self.q[eng]
        if eng in ("vector", "scalar", "gpsimd"):
            r, w = set(reads), set(writes)
            if (r & self.written[eng]) or (w & self.written[eng]) \
                    or (w & self.readset[eng]):
                q.append(("drain",))
                self.written[eng].clear()
                self.readset[eng].clear()
            self.readset[eng].update(r)
        tok = None
        if inc is not None:
            self.cnt[inc] += n
            tok = (inc, self.cnt[inc])
        q.append(("op", fn, inc, self.cnt[inc] if inc else 0, n))
        self.written[eng].update(writes)
        return tok

    def tok(self, sem):
        return (sem, self.cnt[sem])

    def wait(self, eng, tok):
        if tok is None:
            return
        sem, val = tok
        if val <= 0:
            return
        if self.waited[eng].get(sem, -1) >= val:
            return
        self.waited[eng][sem] = val
        self.q[eng].append(("wait", sem, val))

    def close(self, eng, sem, n=1):
        for i in range(len(self.q[eng]) - 1, -1, -1):
            if self.q[eng][i][0] == "op":
                _, fn, inc0, _, _ = self.q[eng][i]
                assert inc0 is None, "op already increments"
                self.cnt[sem] += n
                self.q[eng][i] = ("op", fn, sem, self.cnt[sem], n)
                return (sem, self.cnt[sem])
        raise RuntimeError("no op to close")

    def dma_batch(self, eng, sem):
        self.wait(eng, (sem, self.cnt[sem]))

    def drain(self, eng):
        self.q[eng].append(("drain",))
        self.written[eng].clear()
        self.readset[eng].clear()

    def replay(self, eng_name, eng):
        for item in self.q[eng_name]:
            kind = item[0]
            if kind == "op":
                _, fn, inc, val, n = item
                inst = fn(eng)
                if inc is not None:
                    inst.then_inc(self.sems[inc], n)
            elif kind == "wait":
                _, sem, val = item
                eng.wait_ge(self.sems[sem], val)
            elif kind == "drain":
                eng.drain()

    def build(self):
        with self.nc.Block() as block:
            for e in ENGINES:
                if not self.q[e]:
                    continue
                getattr(block, e)(lambda eng, e=e: self.replay(e, eng))


class Mem:

    def __init__(self, nc):
        self.nc = nc
        self.tensors = {}

    def at(self, name, shape, dtype, off):
        size = int(np.prod(shape[1:])) * mybir.dt.size(dtype)
        assert SBUF_BASE + off + size <= SBUF_END, (name, off, size)
        assert (SBUF_BASE + off) % 32 == 0, (name, off)
        t = self.nc.alloc_sbuf_tensor_at(name, list(shape), dtype,
                                         offset=SBUF_BASE + off)
        self.tensors[name] = t
        return t


def build_program(T=2048):
    CH = 8
    D = CH * 128
    SL = T // 8
    NSL = 8
    QBL = T // 4
    NQB = 4
    KT = T // 128
    H = 16
    inv_d = 1.0 / D

    nc = bass.Bass("TRN2", target_bir_lowering=False, debug=False)

    def din(name, shape):
        return nc.dram_tensor(name, list(shape), F32, kind="ExternalInput").ap()

    xT = din("xT", [D, T])
    Wqkv = din("Wqkv", [D, 3 * D])
    bqkv = din("bqkv", [3 * D])
    Aq = din("Aq", [D, 8])
    Bq = din("Bq", [8, 3 * D])
    Wproj = din("Wproj", [D, D])
    bproj = din("bproj", [D])
    Apj = din("Apj", [D, 8])
    Bpj = din("Bpj", [8, D])
    cw1 = din("cw1", [D, D])
    cb1 = din("cb1", [D])
    cw2 = din("cw2", [D, D // 2])
    cb2 = din("cb2", [D // 2])
    cw3 = din("cw3", [D // 2, D // 4])
    cb3 = din("cb3", [D // 4])
    cw4 = din("cw4", [D // 4, 1])
    cb4 = din("cb4", [1])
    gw1 = din("gw1", [D, D // 2])
    gb1 = din("gb1", [D // 2])
    gw2 = din("gw2", [D // 2, D // 4])
    gb2 = din("gb2", [D // 4])
    gw3 = din("gw3", [D // 4, 1])
    gb3 = din("gb3", [1])
    ar_in = din("ar", [1])
    y = nc.dram_tensor("y", [D, T], F32, kind="ExternalOutput").ap()
    wph = nc.dram_tensor("wph", [CH, 128, CH * 128], BF16, kind="Internal").ap()

    M = Mem(nc)
    f32b, bf16b = 4, 2
    off = 0
    xr = M.at("xr", [128, CH * T], F32, off); off += CH * T * f32b
    A0 = off
    off += CH * T * f32b
    B0 = off
    off += 37024
    C0 = off

    pingA = M.at("pingA", [128, CH * SL], F32, A0)
    pongA = M.at("pongA", [128, CH * SL], F32, A0 + CH * SL * f32b)
    pingB = M.at("pingB", [128, CH * SL], F32, A0 + 2 * CH * SL * f32b)
    pongB = M.at("pongB", [128, CH * SL], F32, A0 + 3 * CH * SL * f32b)
    a_ffn = A0 + 4 * CH * SL * f32b
    srcbA0 = M.at("srcbA0", [128, CH * SL], BF16, a_ffn)
    srcbA1 = M.at("srcbA1", [128, CH * SL], BF16, a_ffn + CH * SL * bf16b)
    srcbB0 = M.at("srcbB0", [128, CH * SL], BF16, a_ffn + 2 * CH * SL * bf16b)
    srcbB1 = M.at("srcbB1", [128, CH * SL], BF16, a_ffn + 3 * CH * SL * bf16b)
    b_ffn = a_ffn + 4 * CH * SL * bf16b
    h1b = M.at("h1b", [128, CH * SL], BF16, b_ffn)
    h2b = M.at("h2b", [128, (CH // 2) * SL], BF16, b_ffn + CH * SL * bf16b)
    h3b = M.at("h3b", [128, (CH // 4) * SL], BF16,
               b_ffn + CH * SL * bf16b + (CH // 2) * SL * bf16b)
    assert b_ffn + (CH + CH // 2 + CH // 4) * SL * bf16b <= B0
    xbf = M.at("xbf", [128, T], BF16, a_ffn)
    xbf2 = M.at("xbf2", [128, T], BF16, a_ffn + T * bf16b)
    xsq = M.at("xsq", [128, T], BF16, a_ffn + 2 * T * bf16b)
    xsq2 = M.at("xsq2", [128, T], BF16, a_ffn + 3 * T * bf16b)
    assert a_ffn + 4 * T * bf16b <= B0
    gtmp = M.at("gtmp", [128, T], F32, A0)

    bo = B0
    wcn1 = M.at("wcn1", [128, CH * D], BF16, bo); bo += CH * D * bf16b
    wcn2 = M.at("wcn2", [128, CH * (D // 2)], BF16, bo); bo += CH * (D // 2) * bf16b
    wcn3 = M.at("wcn3", [128, (CH // 2) * (D // 4)], BF16, bo)
    bo += (CH // 2) * (D // 4) * bf16b
    wcn4 = M.at("wcn4", [128, 2], BF16, bo); bo += 32
    wgc1 = M.at("wgc1", [128, CH * (D // 2)], BF16, bo); bo += CH * (D // 2) * bf16b
    wgc2 = M.at("wgc2", [128, (CH // 2) * (D // 4)], BF16, bo)
    bo += (CH // 2) * (D // 4) * bf16b
    wgc3 = M.at("wgc3", [128, 2], BF16, bo); bo += 32
    b1s = M.at("b1s", [128, CH], F32, bo); bo += 32
    b2s = M.at("b2s", [128, CH // 2], F32, bo); bo += 32
    b3s = M.at("b3s", [128, CH // 4], F32, bo); bo += 32
    assert bo - B0 <= 37024, bo - B0
    vaug = M.at("vaug", [128, KT * H * 65], BF16, B0)

    assert T >= 1024
    xn2 = M.at("xn2", [128, CH * T], BF16, A0)
    Wv_sb = M.at("Wv_sb", [128, CH * D], BF16, A0 + CH * T * bf16b)
    Ks = M.at("Ks2", [128, CH * T], BF16, A0 + CH * T * bf16b)
    ao = A0
    def alloA(name, shape, dtype):
        nonlocal ao
        t = M.at(name, shape, dtype, ao)
        ao += ((int(np.prod(shape[1:])) * mybir.dt.size(dtype) + 31) // 32) * 32
        return t
    attnq = alloA("attnq", [128, CH * QBL], BF16)
    attnq2 = alloA("attnq2", [128, CH * QBL], BF16)
    PT0 = alloA("PT0", [128, 2 * QBL], BF16)
    PT1 = alloA("PT1", [128, 2 * QBL], BF16)
    lrow0 = alloA("lrow0", [1, 2 * QBL], F32)
    lrow1 = alloA("lrow1", [1, 2 * QBL], F32)
    ARAW_OFF = ao
    araw0 = alloA("araw0", [64, 2 * QBL], BF16)
    araw1 = alloA("araw1", [64, 2 * QBL], BF16)
    qsq = M.at("qsq", [128, T], BF16, A0)
    qsq2 = M.at("qsq2", [128, T], BF16, ARAW_OFF)
    assert ao <= A0 + CH * T * bf16b

    co = C0
    def calloc(name, shape, dtype):
        nonlocal co
        t = M.at(name, shape, dtype, co)
        co += ((int(np.prod(shape[1:])) * mybir.dt.size(dtype) + 31) // 32) * 32
        return t

    sc8 = calloc("sc8", [1, 16], F32)
    meanb_col = calloc("meanb_col", [128, 1], F32)
    ones_f = calloc("ones_f", [1, 128], F32)
    ones_b = calloc("ones_b", [1, 512], BF16)
    onesc_b = calloc("onesc_b", [128, 1], BF16)
    selq = calloc("selq", [128, 128], BF16)
    biasH = calloc("biasH", [128, H], F32)
    biasP = calloc("biasP", [128, H // 2], F32)
    sq32 = calloc("sq32", [32, 32], F32)
    sq32b = calloc("sq32b", [32, 32], F32)
    cpool = calloc("cpool", [128, 8], F32)
    P0 = co

    po = P0
    def alloP(name, shape, dtype):
        nonlocal po
        t = M.at(name, shape, dtype, po)
        po += ((int(np.prod(shape[1:])) * mybir.dt.size(dtype) + 31) // 32) * 32
        return t
    a_row = alloP("a_row", [1, T], F32)
    b_row = alloP("b_row", [1, T], F32)
    atmp = alloP("atmp", [128, SL], F32)
    v_row = alloP("v_row", [1, T], F32)
    tf_row = alloP("tf_row", [1, SL], F32)
    cpl_row = alloP("cpl_row", [1, SL], F32)
    ncpl_row = alloP("ncpl_row", [1, SL], F32)
    cplB_row = alloP("cplB_row", [1, SL], F32)
    ttmp0 = alloP("ttmp0", [128, SL], F32)
    ttmp1 = alloP("ttmp1", [128, SL], F32)
    gcsum = alloP("gcsum", [128, CH], F32)
    gcbf = alloP("gcbf", [128, CH], BF16)
    hg1p_sb = alloP("hg1bf", [128, 4], BF16)
    hg2p_sb = alloP("hg2bf", [128, 2], BF16)
    gb1s = alloP("gb1s", [128, 4], F32)
    gb2s = alloP("gb2s", [128, 2], F32)
    P_FFN_END = po

    po = P0
    Qs = alloP("Qs", [128, CH * T], BF16)
    tq = alloP("tq", [8, T], BF16)
    AV0 = po
    wqt = alloP("wqt", [128, CH * 128], BF16)
    wkt = alloP("wkt", [128, CH * 128], BF16)
    bqqk0 = alloP("bqqk0", [8, 256], BF16)
    brqk0 = alloP("brqk0", [1, 256], BF16)
    P_ATT1_END = po
    po = AV0
    Aq_sb = alloP("Aq_sb", [128, CH * 8], BF16)
    Bqv = alloP("Bqv", [8, D], BF16)
    bqv = alloP("bqv", [1, D], BF16)
    P_VT_END = po

    po = P0 + ((CH * T * bf16b + 31) // 32) * 32
    WPT_OFF = po
    wpt = alloP("wpt", [128, CH * 128], BF16)
    wpt3 = alloP("wpt3", [128, CH * 128], BF16)
    PT2 = M.at("PT2", [128, 2 * QBL], BF16, WPT_OFF + CH * 128 * bf16b)
    wpf = M.at("wpf", [128, CH * 128], F32, WPT_OFF)
    ApjT = alloP("ApjT", [32, CH * 128], BF16)
    aptmp = alloP("aptmp", [64, 128], BF16)
    BPSB_OFF = po
    Bp_sb = alloP("Bp_sb", [8, D], BF16)
    PT3 = M.at("PT3", [128, 2 * QBL], BF16, BPSB_OFF)
    wpt2 = M.at("wpt2", [128, CH * 128], BF16, B0 + 33280)
    wstage = M.at("wstage", [128, CH * 128], BF16, B0 + 33280)
    assert 33280 + CH * 128 * bf16b <= 37024
    bp_sl0 = alloP("bp_sl0", [1, 128], BF16)
    bp_sl1 = alloP("bp_sl1", [1, 128], BF16)
    bp_sl2 = alloP("bp_sl2", [1, 128], BF16)
    P_ATT2_END = po
    PLIM = SBUF_END - SBUF_BASE
    assert max(P_FFN_END, P_ATT1_END, P_VT_END, P_ATT2_END) <= PLIM, (
        P_FFN_END, P_ATT1_END, P_VT_END, P_ATT2_END, PLIM)

    SC_AR, SC_B1, SC_A1, SC_GT, SC_T0, SC_MB, SC_B4H, SC_GB3H = range(8)

    def sc(i):
        return sc8[0:1, i:i + 1]

    PS = nc.alloc_psum_tensor("PS", [128, 4096], F32)

    from concourse.hw_specs import get_activation_tables
    _tabs = list(get_activation_tables(nc.m.arch).values())
    NLE_IDX = next(i for i, fs in enumerate(_tabs)
                   if AF.Ln in fs and AF.Exp in fs)

    S = Sched(nc)

    def preload_set6():
        S.op("scalar", lambda e: e.add_instruction(
            mybir.InstLoadActFuncSet(
                name=nc.get_next_instruction_name(),
                act_func_set_id=NLE_IDX, ins=[], outs=[])))
    for s in ("hw", "sw", "sw2", "te", "ac", "ve", "gp"):
        S.sem(s)
    for c in range(CH):
        S.sem("x%d" % c)

    x_toks = {}
    for c in range(CH):
        S.op("sync", lambda e, c=c: e.dma_start(
            xr[:, c * T:(c + 1) * T], xT[c * 128:(c + 1) * 128, :]),
            inc="x%d" % c, n=16)
        x_toks[c] = S.tok("x%d" % c)

    def load_w_chunks(dst, src, ncols, nch, tag):
        for c in range(nch):
            S.op("gpsimd", lambda e, c=c, dst=dst, src=src, ncols=ncols: e.dma_start(
                dst[:, c * ncols:(c + 1) * ncols],
                src[c * 128:(c + 1) * 128, :]), inc="sw", n=16)
        return S.tok("sw")

    def load_bias_cols(dst, src, nch):
        S.dma_batch("sync", "hw")
        src_r = src.rearrange("(c p o) -> c p o", p=128, o=1)
        for c in range(nch):
            S.op("sync", lambda e, c=c, dst=dst, src_r=src_r: e.dma_start(
                dst[:, c:c + 1], src_r[c]), inc="hw", n=16)
        return S.tok("hw")

    def emit_ffn_weight_loads():
        S.dma_batch("gpsimd", "sw")
        load_w_chunks(wgc1, gw1, D // 2, CH, "gw1")
        load_w_chunks(wgc2, gw2, D // 4, CH // 2, "gw2")
        for c in range(2):
            S.op("gpsimd", lambda e, c=c: e.dma_start(
                wgc3[:, c:c + 1], gw3.rearrange("(c p) o -> c p o", p=128)[c]),
                inc="sw", n=16)
        gc_tok = S.tok("sw")
        S.dma_batch("gpsimd", "sw")
        load_w_chunks(wcn1, cw1, D, CH, "cw1")
        load_w_chunks(wcn2, cw2, D // 2, CH, "cw2")
        load_w_chunks(wcn3, cw3, D // 4, CH // 2, "cw3")
        for c in range(2):
            S.op("gpsimd", lambda e, c=c: e.dma_start(
                wcn4[:, c:c + 1], cw4.rearrange("(c p) o -> c p o", p=128)[c]),
                inc="sw", n=16)
        load_bias_cols(b1s, cb1, CH)
        load_bias_cols(b2s, cb2, CH // 2)
        load_bias_cols(b3s, cb3, CH // 4)
        load_bias_cols(gb1s, gb1, 4)
        load_bias_cols(gb2s, gb2, 2)
        return gc_tok, S.tok("sw"), S.tok("hw")

    ffnw_gc, ffnw_sw, ffnw_hw = emit_ffn_weight_loads()
    S.dma_batch("sync", "hw")
    S.op("sync", lambda e: e.dma_start(sc8[0:1, SC_AR:SC_AR + 1],
                                       ar_in.rearrange("(a o) -> a o", a=1)), inc="hw", n=16)
    S.op("sync", lambda e: e.dma_start(sc8[0:1, SC_T0:SC_T0 + 1],
                                       cb4.rearrange("(a o) -> a o", a=1)), inc="hw", n=16)
    S.op("sync", lambda e: e.dma_start(sc8[0:1, SC_MB:SC_MB + 1],
                                       gb3.rearrange("(a o) -> a o", a=1)), inc="hw", n=16)
    tiny_tok = S.tok("hw")

    for i, cv in enumerate((1e-5, 0.125, 2.0, 0.5, 1.0 / D, -0.5)):
        nc.const_aps.register(F32, cv, cpool[:, i:i + 1])
        S.op("vector", lambda e, i=i, cv=cv: e.memset(cpool[:, i:i + 1], cv),
             inc="ve", writes=("cpool",))
    S.op("vector", lambda e: e.memset(ones_f[:], 1.0), inc="ve", writes=("ones_f",))
    S.op("vector", lambda e: e.memset(ones_b[:], 1.0), inc="ve", writes=("ones_b",))
    S.op("vector", lambda e: e.memset(onesc_b[:], 1.0), inc="ve", writes=("onesc_b",))
    S.op("vector", lambda e: e.memset(sq32[:], 0.0), inc="ve", writes=("sq32",))
    S.op("vector", lambda e: e.memset(selq[:], 0.0), inc="ve", writes=("selq",))
    for c in range(CH):
        S.op("vector", lambda e, c=c: e.memset(
            selq[0:64, 16 * c + 2 * c:16 * c + 2 * c + 1], 1.0), writes=("selq",))
        S.op("vector", lambda e, c=c: e.memset(
            selq[64:128, 16 * c + 2 * c + 1:16 * c + 2 * c + 2], 1.0),
            inc="ve", writes=("selq",))
    consts_tok = S.tok("ve")

    S.wait("vector", tiny_tok)
    S.op("vector", lambda e: e.tensor_scalar_mul(sc(SC_B4H), sc(SC_T0), 0.5),
         inc="ve", reads=("sc8",), writes=("sc8",))
    S.op("vector", lambda e: e.tensor_scalar_mul(sc(SC_GB3H), sc(SC_MB), 0.5),
         inc="ve", reads=("sc8",), writes=("sc8",))

    def emit_ln(want_gc, gates=None):
        prev_mm = []
        for c in range(CH):
            xb = xbf if c % 2 == 0 else xbf2
            xq = xsq if c % 2 == 0 else xsq2
            if gates is not None:
                S.wait("vector", gates[c])
                S.wait("scalar", gates[c])
            if c >= 2:
                S.wait("vector", prev_mm[c - 2])
                S.wait("scalar", prev_mm[c - 2])
            S.op("vector", lambda e, c=c, xb=xb: e.tensor_copy(
                xb[:], xr[:, c * T:(c + 1) * T]), inc="ve",
                writes=(xb.name,))
            S.op("scalar", lambda e, c=c, xq=xq: e.activation(
                xq[:], xr[:, c * T:(c + 1) * T], AF.Square), inc="ac",
                writes=(xq.name,))
            S.wait("tensor", S.tok("ve"))
            S.wait("tensor", S.tok("ac"))
            for (o, n) in nslices(T):
                S.op("tensor", lambda e, c=c, o=o, n=n, xb=xb: e.matmul(
                    PS[0:1, o:o + n], onesc_b[:], xb[:, o:o + n],
                    start=(c == 0), stop=(c == CH - 1)))
                S.op("tensor", lambda e, c=c, o=o, n=n, xq=xq: e.matmul(
                    PS[0:1, 2048 + o:2048 + o + n], onesc_b[:], xq[:, o:o + n],
                    start=(c == 0), stop=(c == CH - 1)))
            prev_mm.append(S.close("tensor", "te"))
        sums_tok = S.tok("te")

        HT = T // 2
        S.wait("scalar", sums_tok)
        S.wait("vector", sums_tok)
        var_t = {}
        exp_t = {}
        for h in range(2):
            o = h * HT
            S.op("scalar", lambda e, o=o: e.activation(
                v_row[0:1, o:o + HT], PS[0:1, o:o + HT], AF.Square,
                scale=inv_d), inc="ac", writes=("v_row%d" % h,))
            sq_t = S.tok("ac")
            S.wait("vector", sq_t)
            S.op("vector", lambda e, o=o: e.scalar_tensor_tensor(
                v_row[0:1, o:o + HT], PS[0:1, 2048 + o:2048 + o + HT], inv_d,
                v_row[0:1, o:o + HT], ALU.mult, ALU.subtract), inc="ve",
                reads=("v_row%d" % h,), writes=("v_row%d" % h,))
            var_t[h] = S.tok("ve")
        for h in range(2):
            o = h * HT
            S.wait("scalar", var_t[h])
            S.op("scalar", lambda e, o=o: e.activation(
                v_row[0:1, o:o + HT], v_row[0:1, o:o + HT], AF.Ln,
                bias=1e-5), inc="ac", reads=("v_row%d" % h,),
                writes=("v_row%d" % h,))
            S.op("scalar", lambda e, o=o: e.activation(
                a_row[0:1, o:o + HT], v_row[0:1, o:o + HT], AF.Exp,
                scale=-0.5), inc="ac", reads=("v_row%d" % h,),
                writes=("a_row%d" % h,))
            exp_t[h] = S.tok("ac")
        for h in range(2):
            o = h * HT
            S.wait("vector", exp_t[h])
            S.op("vector", lambda e, o=o: e.scalar_tensor_tensor(
                b_row[0:1, o:o + HT], PS[0:1, o:o + HT], -inv_d,
                a_row[0:1, o:o + HT], ALU.mult, ALU.mult), inc="ve",
                reads=("a_row%d" % h,), writes=("b_row%d" % h, "b_row"))
        rows_tok = S.tok("ve")

        gc_tok = None
        if want_gc:
            S.op("vector", lambda e: e.reduce_sum(
                sc(SC_MB), b_row[:], axis=AX.X), inc="ve", reads=("b_row",),
                writes=("sc8",))
            S.op("vector", lambda e: e.tensor_scalar_mul(
                sc(SC_MB), sc(SC_MB), 1.0 / T), inc="ve", reads=("sc8",),
                writes=("sc8",))
            S.wait("tensor", S.tok("ve"))
            S.op("tensor", lambda e: e.matmul(
                PS[0:128, 3072:3073], ones_f[:], sc(SC_MB),
                start=True, stop=True), inc="te")
            S.wait("vector", S.tok("te"))
            S.op("vector", lambda e: e.tensor_copy(meanb_col[:],
                 PS[0:128, 3072:3073]), inc="ve", writes=("meanb_col",))
            S.wait("tensor", S.tok("ve"))
            for (o, n) in nslices(T):
                S.op("tensor", lambda e, o=o, n=n: e.matmul(
                    PS[0:128, 2048 + o:2048 + o + n], ones_f[:],
                    a_row[0:1, o:o + n], start=True, stop=True))
            abc_tok = S.close("tensor", "te")
            S.wait("vector", abc_tok)
            for c in range(CH):
                S.op("vector", lambda e, c=c: e.tensor_tensor(
                    gtmp[:], xr[:, c * T:(c + 1) * T], PS[0:128, 2048:2048 + T],
                    ALU.mult), inc="ve", reads=("xr%d" % c,), writes=("gtmp",))
                S.op("vector", lambda e, c=c: e.reduce_sum(
                    gcsum[:, c:c + 1], gtmp[:], axis=AX.X), inc="ve",
                    reads=("gtmp",), writes=("gcsum",))
            S.op("vector", lambda e: e.tensor_scalar(
                gcbf[:], gcsum[:], 1.0 / T, meanb_col[:], ALU.mult, ALU.add),
                inc="ve", reads=("gcsum", "meanb_col"), writes=("gcbf",))
            gc_tok = S.tok("ve")
        return rows_tok, gc_tok

    def emit_gcnet(gc_tok, w_tok, bias_tok):
        S.wait("tensor", gc_tok)
        S.wait("tensor", w_tok)
        S.wait("scalar", bias_tok)
        for mg in range(4):
            for kc in range(CH):
                S.op("tensor", lambda e, mg=mg, kc=kc: e.matmul(
                    PS[0:128, 3584 + mg:3584 + mg + 1],
                    wgc1[:, kc * (D // 2) + mg * 128:kc * (D // 2) + (mg + 1) * 128],
                    gcbf[:, kc:kc + 1], start=(kc == 0), stop=(kc == CH - 1)))
        t1 = S.close("tensor", "te")
        S.wait("scalar", t1)
        for mg in range(4):
            S.op("scalar", lambda e, mg=mg: e.activation(
                hg1p_sb[:, mg:mg + 1], PS[0:128, 3584 + mg:3584 + mg + 1],
                AF.Gelu, bias=gb1s[:, mg:mg + 1]), inc="ac",
                writes=("hg1bf",))
        S.wait("tensor", S.tok("ac"))
        for mg in range(2):
            for kc in range(4):
                S.op("tensor", lambda e, mg=mg, kc=kc: e.matmul(
                    PS[0:128, 3590 + mg:3590 + mg + 1],
                    wgc2[:, kc * (D // 4) + mg * 128:kc * (D // 4) + (mg + 1) * 128],
                    hg1p_sb[:, kc:kc + 1], start=(kc == 0), stop=(kc == 3)))
        t2 = S.close("tensor", "te")
        S.wait("scalar", t2)
        for mg in range(2):
            S.op("scalar", lambda e, mg=mg: e.activation(
                hg2p_sb[:, mg:mg + 1], PS[0:128, 3590 + mg:3590 + mg + 1],
                AF.Gelu, bias=gb2s[:, mg:mg + 1]), inc="ac", writes=("hg2bf",))
        S.wait("tensor", S.tok("ac"))
        for kc in range(2):
            S.op("tensor", lambda e, kc=kc: e.matmul(
                PS[0:1, 3594:3595], wgc3[:, kc:kc + 1], hg2p_sb[:, kc:kc + 1],
                start=(kc == 0), stop=(kc == 1)))
        t3 = S.close("tensor", "te")
        S.wait("scalar", t3)
        S.op("scalar", lambda e: e.activation(
            sc(SC_GT), PS[0:1, 3594:3595], AF.Tanh, bias=sc(SC_GB3H)[0:1, 0:1],
            scale=0.5), inc="ac", writes=("sc8",))
        S.wait("vector", S.tok("ac"))
        S.op("vector", lambda e: e.tensor_scalar_mul(sc(SC_B1), sc(SC_AR), 0.3),
             inc="ve", reads=("sc8",), writes=("sc8",))
        S.op("vector", lambda e: e.tensor_tensor(
            sc(SC_T0), sc(SC_AR), sc(SC_GT), ALU.mult), inc="ve",
            reads=("sc8",), writes=("sc8",))
        S.op("vector", lambda e: e.tensor_scalar(
            sc(SC_A1), sc(SC_T0), 0.7, 0.1, ALU.mult, ALU.add), inc="ve",
            reads=("sc8",), writes=("sc8",))
        return S.tok("ve")

    def emit_ffn(rows_tok, ab_tok, store_cb=None):
        S.wait("tensor", ab_tok)

        def emit_apply(ctx, bank_guard):
            so = ctx["so"]
            S.wait("tensor", rows_tok)
            if bank_guard is not None:
                S.wait("tensor", bank_guard)
            for (o, n) in nslices(SL):
                S.op("tensor", lambda e, o=o, n=n, so=so: e.matmul(
                    PS[0:128, 2560 + o:2560 + o + n], ones_f[:],
                    a_row[0:1, so + o:so + o + n], start=True, stop=True))
                S.op("tensor", lambda e, o=o, n=n, so=so: e.matmul(
                    PS[0:128, 2560 + SL + o:2560 + SL + o + n], ones_f[:],
                    b_row[0:1, so + o:so + o + n], start=True, stop=True))
            ab_bc = S.close("tensor", "te")
            S.wait("vector", ab_bc)
            ping = ctx["state"]
            cast_toks = {}
            sb = ctx["sb"][0]
            for c in range(CH):
                S.op("vector", lambda e, c=c, so=so: e.tensor_tensor(
                    atmp[:], xr[:, c * T + so:c * T + so + SL],
                    PS[0:128, 2560:2560 + SL], ALU.mult), inc="ve",
                    writes=("atmp",))
                S.op("vector", lambda e, c=c, ping=ping: e.tensor_tensor(
                    ping[:, c * SL:(c + 1) * SL], atmp[:],
                    PS[0:128, 2560 + SL:2560 + 2 * SL], ALU.add), inc="ve",
                    reads=("atmp",), writes=(ping.name + str(c),))
                ap_tok = S.tok("ve")
                S.wait("gpsimd", ap_tok)
                cast_toks[c] = S.op("gpsimd", lambda e, c=c, ping=ping, sb=sb: e.tensor_copy(
                    sb[:, c * SL:(c + 1) * SL], ping[:, c * SL:(c + 1) * SL]),
                    inc="gp", reads=(ping.name + str(c),),
                    writes=("%sc%d" % (sb.name, c),))
            ctx["cast_toks"] = cast_toks
            return S.tok("ve")

        def emit_sweeps(ctx, cyc, mid=None):
            srcb = ctx["sb"][cyc % 2]
            cast_toks = ctx["cast_toks"]
            h1e = {}
            for kc in range(CH):
                S.wait("tensor", cast_toks[kc])
                for mi in range(4):
                    S.op("tensor", lambda e, mi=mi, kc=kc, srcb=srcb: e.matmul(
                        PS[0:128, mi * 512:mi * 512 + SL],
                        wcn1[:, kc * D + mi * 128:kc * D + (mi + 1) * 128],
                        srcb[:, kc * SL:(kc + 1) * SL],
                        start=(kc == 0), stop=(kc == CH - 1)))
            sw_tok = S.close("tensor", "te")
            S.wait("scalar", sw_tok)
            for mi in range(4):
                h1e[mi] = S.op("scalar", lambda e, mi=mi: e.activation(
                    h1b[:, mi * SL:(mi + 1) * SL],
                    PS[0:128, mi * 512:mi * 512 + SL],
                    AF.Gelu, bias=b1s[:, mi:mi + 1]), inc="ac",
                    writes=("h1b%d" % mi,))
            if mid is not None:
                mid()
            for mi in range(4):
                m = 4 + mi
                S.wait("tensor", h1e[mi])
                for kc in range(CH):
                    S.op("tensor", lambda e, m=m, mi=mi, kc=kc, srcb=srcb: e.matmul(
                        PS[0:128, mi * 512:mi * 512 + SL],
                        wcn1[:, kc * D + m * 128:kc * D + (m + 1) * 128],
                        srcb[:, kc * SL:(kc + 1) * SL],
                        start=(kc == 0), stop=(kc == CH - 1)))
                t = S.close("tensor", "te")
                S.wait("scalar", t)
                h1e[m] = S.op("scalar", lambda e, m=m, mi=mi: e.activation(
                    h1b[:, m * SL:(m + 1) * SL],
                    PS[0:128, mi * 512:mi * 512 + SL],
                    AF.Gelu, bias=b1s[:, m:m + 1]), inc="ac",
                    writes=("h1b%d" % m,))
            h2e = {}
            for m in range(CH // 2):
                S.wait("tensor", h1e[m + 4])
                for kc in range(CH):
                    S.wait("tensor", h1e[kc])
                    S.op("tensor", lambda e, m=m, kc=kc: e.matmul(
                        PS[0:128, m * 512:m * 512 + SL],
                        wcn2[:, kc * (D // 2) + m * 128:kc * (D // 2) + (m + 1) * 128],
                        h1b[:, kc * SL:(kc + 1) * SL],
                        start=(kc == 0), stop=(kc == CH - 1)))
                mm_tok = S.close("tensor", "te")
                S.wait("scalar", mm_tok)
                h2e[m] = S.op("scalar", lambda e, m=m: e.activation(
                    h2b[:, m * SL:(m + 1) * SL], PS[0:128, m * 512:m * 512 + SL],
                    AF.Gelu, bias=b2s[:, m:m + 1]), inc="ac", writes=("h2b%d" % m,))
            h3e = {}
            for m in range(CH // 4):
                S.wait("tensor", h2e[m])
                for kc in range(CH // 2):
                    S.wait("tensor", h2e[kc])
                    S.op("tensor", lambda e, m=m, kc=kc: e.matmul(
                        PS[0:128, m * 512:m * 512 + SL],
                        wcn3[:, kc * (D // 4) + m * 128:kc * (D // 4) + (m + 1) * 128],
                        h2b[:, kc * SL:(kc + 1) * SL],
                        start=(kc == 0), stop=(kc == CH // 2 - 1)))
                mm_tok = S.close("tensor", "te")
                S.wait("scalar", mm_tok)
                h3e[m] = S.op("scalar", lambda e, m=m: e.activation(
                    h3b[:, m * SL:(m + 1) * SL], PS[0:128, m * 512:m * 512 + SL],
                    AF.Gelu, bias=b3s[:, m:m + 1]), inc="ac", writes=("h3b%d" % m,))
            for kc in range(CH // 4):
                S.wait("tensor", h3e[kc])
                S.op("tensor", lambda e, kc=kc: e.matmul(
                    PS[0:1, 3072:3072 + SL], wcn4[:, kc:kc + 1],
                    h3b[:, kc * SL:(kc + 1) * SL],
                    start=(kc == 0), stop=(kc == CH // 4 - 1)))
            tf_tok = S.close("tensor", "te")
            S.wait("scalar", tf_tok)
            S.wait("scalar", ab_tok)
            tfr, cpr = ctx["tfr"], ctx["cpr"]
            S.op("scalar", lambda e, tfr=tfr: e.activation(
                tfr[:], PS[0:1, 3072:3072 + SL], AF.Tanh,
                bias=sc(SC_B4H)[0:1, 0:1], scale=0.5), inc="ac",
                writes=(tfr.name,))
            S.op("scalar", lambda e, tfr=tfr, cpr=cpr: e.activation(
                cpr[:], tfr[:], AF.Identity, bias=sc(SC_A1)[0:1, 0:1],
                scale=sc(SC_B1)[0:1, 0:1]), inc="ac",
                reads=(tfr.name, "sc8"), writes=(cpr.name,))
            ctx["cplv"] = S.tok("ac")

        def emit_tail(ctx, cyc):
            so = ctx["so"]
            state, other = ctx["state"], ctx["other"]
            cpr = ctx["cpr"]
            cb = ctx["cb"]
            S.wait("tensor", ctx["cplv"])
            S.wait("tensor", S.tok("ve"))
            for (o, n) in nslices(SL):
                S.op("tensor", lambda e, o=o, n=n, cpr=cpr, cb=cb: e.matmul(
                    PS[0:128, cb + o:cb + o + n], ones_f[:],
                    cpr[0:1, o:o + n], start=True, stop=True))
            bc_tok = S.close("tensor", "te")
            S.wait("vector", bc_tok)
            S.wait("gpsimd", S.tok("te"))
            cast_toks = {}
            for c in range(CH):
                pc = (c + 4) % CH
                op1 = ALU.add if c in (0, 1, 6, 7) else ALU.subtract
                tt = ttmp0 if c % 2 == 0 else ttmp1
                S.op("vector", lambda e, c=c, pc=pc, tt=tt, cb=cb,
                     state=state: e.tensor_tensor(
                    tt[:], state[:, pc * SL:(pc + 1) * SL],
                    PS[0:128, cb:cb + SL], ALU.mult), inc="ve",
                    reads=("%s%d" % (state.name, pc),), writes=(tt.name,))
                tw = S.op("vector", lambda e, c=c, tt=tt, op1=op1, state=state,
                     other=other: e.tensor_tensor(
                    other[:, c * SL:(c + 1) * SL],
                    state[:, c * SL:(c + 1) * SL], tt[:], op1),
                    inc="ve", reads=(tt.name, "%s%d" % (state.name, c)),
                    writes=("%s%d" % (other.name, c),))
                if cyc < 2:
                    nsrcb = ctx["sb"][(cyc + 1) % 2]
                    S.wait("gpsimd", tw)
                    cast_toks[c] = S.op("gpsimd", lambda e, c=c, other=other,
                                        nsrcb=nsrcb: e.tensor_copy(
                        nsrcb[:, c * SL:(c + 1) * SL],
                        other[:, c * SL:(c + 1) * SL]), inc="gp",
                        reads=("%s%d" % (other.name, c),),
                        writes=("%sc%d" % (nsrcb.name, c),))
                else:
                    S.op("vector", lambda e, c=c, so=so, other=other: e.scalar_tensor_tensor(
                        xr[:, c * T + so:c * T + so + SL],
                        other[:, c * SL:(c + 1) * SL], 0.5,
                        xr[:, c * T + so:c * T + so + SL], ALU.mult, ALU.add),
                        inc="ve", reads=("%s%d" % (other.name, c), "xr%d" % c),
                        writes=("xr%d" % c,))
            if cyc == 2 and store_cb is not None:
                store_cb(so, S.tok("ve"))
            ctx["state"], ctx["other"] = other, state
            ctx["cast_toks"] = cast_toks

        ctxs = {0: dict(state=pingA, other=pongA, sb=(srcbA0, srcbA1),
                        tfr=tf_row, cpr=cpl_row, cb=2048),
                1: dict(state=pingB, other=pongB, sb=(srcbB0, srcbB1),
                        tfr=ncpl_row, cpr=cplB_row, cb=3584)}
        last_ab = None

        def do_apply(s):
            nonlocal last_ab
            ctx = ctxs[s % 2]
            ctx["so"] = s * SL
            ctx["state"], ctx["other"] = (
                (pingA, pongA) if s % 2 == 0 else (pingB, pongB))
            S.wait("vector", S.tok("gp"))
            S.wait("gpsimd", S.tok("te"))
            emit_apply(ctx, last_ab)
            last_ab = S.tok("ve")

        do_apply(0)
        do_apply(1)
        prev = None
        for wave in range(NSL // 2):
            a, b = 2 * wave, 2 * wave + 1
            for cyc in range(3):
                emit_sweeps(ctxs[0], cyc, mid=prev)
                prev = (lambda c=cyc: emit_tail(ctxs[0], c))
                if cyc == 2:
                    ta = prev
                    def mk1(ta=ta, s=a + 2):
                        def f():
                            ta()
                            if s < NSL:
                                do_apply(s)
                        return f
                    prev = mk1()
                emit_sweeps(ctxs[1], cyc, mid=prev)
                prev = (lambda c=cyc: emit_tail(ctxs[1], c))
            if b + 2 < NSL:
                tb = prev
                def mk2(tb=tb, s=b + 2):
                    def f():
                        tb()
                        do_apply(s)
                    return f
                prev = mk2()
        prev()
        return S.tok("ve")

    S.wait("vector", consts_tok)
    S.wait("tensor", consts_tok)
    preload_set6()
    rows1, gc1 = emit_ln(want_gc=True, gates=x_toks)
    S.op("scalar", lambda e: e.preload_act_table(AF.Gelu))
    ab1 = emit_gcnet(gc1, ffnw_gc, ffnw_hw)
    S.wait("tensor", ffnw_sw)
    S.wait("tensor", ffnw_hw)
    S.wait("scalar", ffnw_hw)
    ffn1_done = emit_ffn(rows1, ab1)
    ffn1_gp = S.tok("gp")

    S.drain("vector")
    S.drain("scalar")
    S.wait("vector", ffn1_done)
    S.wait("vector", ffn1_gp)
    S.wait("scalar", S.tok("ve"))
    S.wait("scalar", ffn1_gp)
    S.wait("tensor", ffn1_gp)
    preload_set6()
    rows2, _ = emit_ln(want_gc=False)
    rows2_ac = S.tok("ac")

    S.wait("gpsimd", rows2)
    S.wait("gpsimd", rows2_ac)
    S.dma_batch("gpsimd", "sw")
    for c in range(CH):
        S.op("gpsimd", lambda e, c=c: e.dma_start(
            Aq_sb[:, c * 8:(c + 1) * 8], Aq[c * 128:(c + 1) * 128, :]),
            inc="sw", n=16)
    S.op("gpsimd", lambda e: e.dma_start(
        Bqv[:], Bq[:, 2 * D:3 * D]), inc="sw", n=16)
    S.op("gpsimd", lambda e: e.dma_start(
        bqv[:], bqkv.rearrange("(a o) -> a o", a=1)[0:1, 2 * D:3 * D]),
        inc="sw", n=16)
    for c in range(CH):
        S.op("gpsimd", lambda e, c=c: e.dma_start(
            Wv_sb[:, c * D:(c + 1) * D], Wqkv[c * 128:(c + 1) * 128, 2 * D:3 * D]),
            inc="sw", n=16)
    attw_tok = S.tok("sw")

    S.drain("vector")
    S.op("vector", lambda e: e.memset(
        vaug[:].rearrange("p (k h o) -> p (k h) o", h=H, o=65)[:, :, 64:65], 1.0),
        inc="ve", writes=("vaug_ones",))

    ap_toks = {}
    tq_ev = {}
    prev_vt = {}

    def emit_tq(s):
        so = s * SL
        S.wait("tensor", ap_toks[s])
        S.wait("tensor", attw_tok)
        if s >= 1:
            S.wait("tensor", tq_ev[s - 1])
        for kc in range(CH):
            S.op("tensor", lambda e, so=so, kc=kc: e.matmul(
                PS[0:8, 2048:2048 + SL], Aq_sb[:, kc * 8:(kc + 1) * 8],
                xn2[:, kc * T + so:kc * T + so + SL],
                start=(kc == 0), stop=(kc == CH - 1)))
        tq_mm = S.close("tensor", "te")
        S.wait("scalar", tq_mm)
        tq_ev[s] = S.op("scalar", lambda e, so=so: e.activation(
            tq[0:8, so:so + SL], PS[0:8, 2048:2048 + SL],
            AF.Copy, scale=2.0), inc="ac", writes=("tq",))

    def emit_vtok(kt):
        pc0 = 0 if kt % 2 == 0 else 1024
        S.wait("tensor", tq_ev[(kt * 128) // SL])
        if kt >= 2:
            S.wait("tensor", prev_vt[kt - 2])
        for j, (o, n) in enumerate(nslices(D)):
            for kc in range(CH):
                S.op("tensor", lambda e, kt=kt, kc=kc, o=o, n=n, pc0=pc0: e.matmul(
                    PS[0:128, pc0 + o:pc0 + o + n],
                    xn2[:, kc * T + kt * 128:kc * T + (kt + 1) * 128],
                    Wv_sb[:, kc * D + o:kc * D + o + n],
                    start=(kc == 0), stop=False))
            S.op("tensor", lambda e, kt=kt, o=o, n=n, pc0=pc0: e.matmul(
                PS[0:128, pc0 + o:pc0 + o + n],
                tq[0:8, kt * 128:(kt + 1) * 128],
                Bqv[0:8, o:o + n], start=False, stop=False))
            S.op("tensor", lambda e, kt=kt, o=o, n=n, pc0=pc0: e.matmul(
                PS[0:128, pc0 + o:pc0 + o + n],
                ones_b[0:1, 0:128], bqv[0:1, o:o + n],
                start=False, stop=True))
        vt_mm = S.close("tensor", "te")
        S.wait("vector", vt_mm)
        prev_vt[kt] = S.op("vector", lambda e, kt=kt, pc0=pc0: e.tensor_copy(
            vaug[:].rearrange("p (k h o) -> p k h o", h=H, o=65)[:, kt, :, 0:64],
            PS[0:128, pc0:pc0 + D].rearrange("p (h o) -> p h o", o=64)),
            inc="ve", writes=("vaug%d" % kt,))

    prev = None
    KT_PER_SL = SL // 128
    for s in range(NSL):
        so = s * SL
        S.wait("tensor", rows2)
        if prev is not None:
            S.wait("tensor", prev)
        for (o, n) in nslices(SL):
            S.op("tensor", lambda e, o=o, n=n, so=so: e.matmul(
                PS[0:128, 2560 + o:2560 + o + n], ones_f[:],
                a_row[0:1, so + o:so + o + n], start=True, stop=True))
            S.op("tensor", lambda e, o=o, n=n, so=so: e.matmul(
                PS[0:128, 3072 + o:3072 + o + n], ones_f[:],
                b_row[0:1, so + o:so + o + n], start=True, stop=True))
        ab_bc = S.close("tensor", "te")
        S.wait("vector", ab_bc)
        for c in range(CH):
            S.op("vector", lambda e, c=c, so=so: e.tensor_tensor(
                atmp[:], xr[:, c * T + so:c * T + so + SL],
                PS[0:128, 2560:2560 + SL], ALU.mult), inc="ve", writes=("atmp",))
            S.op("vector", lambda e, c=c, so=so: e.tensor_tensor(
                xn2[:, c * T + so:c * T + so + SL], atmp[:],
                PS[0:128, 3072:3072 + SL], ALU.add), inc="ve",
                reads=("atmp",), writes=("xn2_%d" % c,))
        ap_toks[s] = prev = S.tok("ve")
        if s >= 1:
            emit_tq(s - 1)
        if s >= 2:
            for k in range(KT_PER_SL):
                emit_vtok(KT_PER_SL * (s - 2) + k)
    emit_tq(NSL - 1)
    for kt in range(KT_PER_SL * (NSL - 2), KT):
        emit_vtok(kt)
    vaug_tok = S.tok("ve")

    S.wait("tensor", vaug_tok)
    bq_row = bqkv.rearrange("(a o) -> a o", a=1)
    Wqc = Wqkv.rearrange("(kc q) m -> q kc m", q=128)
    Bq3 = Bq.rearrange("r (b m) -> r b m", b=3)
    br3 = bq_row.rearrange("a (b m) -> a b m", b=3)
    qk_q_toks = {}
    qk_k_toks = {}
    for p in range(CH):
        if p == 0:
            S.wait("gpsimd", S.tok("te"))
        else:
            S.wait("gpsimd", qk_q_toks[p - 1])
        S.dma_batch("gpsimd", "sw2")
        S.op("gpsimd", lambda e, p=p: e.dma_start(
            wqt[:].rearrange("q (kc m) -> q kc m", m=128),
            Wqc[:, :, p * 128:(p + 1) * 128]), inc="sw2", n=16)
        wq_tok = S.tok("sw2")
        if p > 0:
            S.wait("gpsimd", qk_k_toks[p - 1])
        S.dma_batch("gpsimd", "sw")
        S.op("gpsimd", lambda e, p=p: e.dma_start(
            wkt[:].rearrange("q (kc m) -> q kc m", m=128),
            Wqc[:, :, D + p * 128:D + (p + 1) * 128]), inc="sw", n=16)
        S.op("gpsimd", lambda e, p=p: e.dma_start(
            bqqk0[:].rearrange("r (b m) -> r b m", b=2),
            Bq3[:, 0:2, p * 128:(p + 1) * 128]), inc="sw", n=16)
        S.op("gpsimd", lambda e, p=p: e.dma_start(
            brqk0[:].rearrange("a (b m) -> a b m", b=2),
            br3[:, 0:2, p * 128:(p + 1) * 128]), inc="sw", n=16)
        wtok = S.tok("sw")
        if p > 0:
            S.wait("tensor", S.tok("ve"))
        for which, bsl in (("q", 0), ("k", 1)):
            S.wait("tensor", wq_tok if which == "q" else wtok)
            blo = bqqk0[0:8, bsl * 128:(bsl + 1) * 128]
            bro = brqk0[0:1, bsl * 128:(bsl + 1) * 128]
            wt = wqt if which == "q" else wkt
            ps0 = 0 if which == "q" else 2048
            for (o, n) in nslices(T):
                for kc in range(CH):
                    S.op("tensor", lambda e, kc=kc, o=o, n=n, wt=wt, ps0=ps0: e.matmul(
                        PS[0:128, ps0 + o:ps0 + o + n],
                        wt[:, kc * 128:(kc + 1) * 128],
                        xn2[:, kc * T + o:kc * T + o + n],
                        start=(kc == 0), stop=False))
                S.wait("tensor", wtok)
                S.op("tensor", lambda e, o=o, n=n, blo=blo, ps0=ps0: e.matmul(
                    PS[0:128, ps0 + o:ps0 + o + n],
                    blo, tq[0:8, o:o + n], start=False, stop=False))
                S.op("tensor", lambda e, o=o, n=n, bro=bro, ps0=ps0: e.matmul(
                    PS[0:128, ps0 + o:ps0 + o + n],
                    bro, ones_b[0:1, 0:n], start=False, stop=True))
            mm_tok = S.close("tensor", "te")
            if which == "q":
                qk_q_toks[p] = mm_tok
            else:
                qk_k_toks[p] = mm_tok
            S.wait("vector", mm_tok)
            dst = Qs if which == "q" else Ks
            S.op("vector", lambda e, p=p, dst=dst, ps0=ps0: e.tensor_copy(
                dst[:, p * T:(p + 1) * T], PS[0:128, ps0:ps0 + T]), inc="ve",
                writes=("%s%d" % (dst.name, p),))
    qk_tok = S.tok("ve")

    S.wait("gpsimd", qk_tok)
    S.dma_batch("gpsimd", "sw")
    for c in range(CH):
        S.op("gpsimd", lambda e, c=c: e.dma_start(
            wstage[:, c * 8:(c + 1) * 8], Apj[c * 128:(c + 1) * 128, :]),
            inc="sw", n=16)
    S.op("gpsimd", lambda e: e.dma_start(Bp_sb[:], Bpj[:]), inc="sw", n=16)
    projw_tok = S.tok("sw")
    S.wait("vector", projw_tok)
    for w in range(2):
        for g in range(4):
            S.op("vector", lambda e, w=w, g=g: e.transpose(
                aptmp[w * 32:(w + 1) * 32, g * 32:(g + 1) * 32],
                wstage[g * 32:(g + 1) * 32, w * 32:(w + 1) * 32]), inc="ve",
                reads=("wstage",), writes=("aptmp",))
    for c in range(CH):
        mask = [8 * (c % 4) + min(i, 7) for i in range(32)]
        S.op("vector", lambda e, c=c, mask=mask: e.stream_shuffle(
            ApjT[0:32, c * 128:(c + 1) * 128],
            aptmp[(c // 4) * 32:(c // 4) * 32 + 32, 0:128], mask), inc="ve",
            reads=("aptmp",), writes=("ApjT",))
    apjt_ve = S.tok("ve")
    Wpc = Wproj.rearrange("(kc q) m -> q kc m", q=128)
    wpf_toks = {}
    wb_last = [None]
    wst_tok = [None]

    def emit_wpf_dma(m):
        if wb_last[0] is not None:
            S.wait("sync", wb_last[0])
        S.dma_batch("sync", "hw")
        S.op("sync", lambda e, m=m: e.dma_start(
            wpf[:].rearrange("q (kc m) -> q kc m", m=128),
            Wpc[:, :, m * 128:(m + 1) * 128]), inc="hw", n=16)
        wpf_toks[m] = S.tok("hw")

    S.wait("sync", S.tok("te"))
    emit_wpf_dma(0)
    bp_row = bproj.rearrange("(a o) -> a o", a=1)

    def emit_wbuild(m):
        S.wait("tensor", wpf_toks[m])
        S.wait("tensor", projw_tok)
        S.wait("tensor", apjt_ve)
        if wb_last[0] is not None:
            S.wait("tensor", wb_last[0])
        if wst_tok[0] is not None:
            S.wait("vector", wst_tok[0])
        vt = {}
        for kc in range(CH):
            c0 = 2048 + (kc % 2) * 512 + ((kc // 2) % 2) * 128
            if kc >= 2:
                S.wait("tensor", vt[kc - 2])
            S.op("tensor", lambda e, kc=kc, c0=c0, m=m: e.matmul(
                PS[0:128, c0:c0 + 128], ApjT[0:8, kc * 128:(kc + 1) * 128],
                Bp_sb[:, m * 128:(m + 1) * 128], start=True, stop=True))
            mm = S.close("tensor", "te")
            S.wait("vector", mm)
            vt[kc] = S.op("vector", lambda e, kc=kc, c0=c0: e.scalar_tensor_tensor(
                wstage[:, kc * 128:(kc + 1) * 128], PS[0:128, c0:c0 + 128],
                2.0, wpf[:, kc * 128:(kc + 1) * 128], ALU.mult, ALU.add),
                inc="ve", reads=("wpf",), writes=("wstage",))
        wb_last[0] = S.tok("ve")
        S.wait("gpsimd", wb_last[0])
        S.dma_batch("gpsimd", "sw2")
        S.op("gpsimd", lambda e, m=m: e.dma_start(wph[m], wstage[:]),
             inc="sw2", n=16)
        wst_tok[0] = S.tok("sw2")
        if m + 1 < CH:
            emit_wpf_dma(m + 1)

    S.drain("vector")
    S.drain("scalar")
    S.wait("scalar", qk_tok)
    bounds_mm = []
    slot = 0
    for which, src in (("q", Qs), ("k", Ks)):
        S.wait("tensor", S.tok("ve"))
        for c in range(CH):
            qb_ = qsq if c % 2 == 0 else qsq2
            if c % 2 == 0:
                S.op("scalar", lambda e, c=c, src=src, qb_=qb_: e.activation(
                    qb_[:], src[:, c * T:(c + 1) * T], AF.Square), inc="ac",
                    writes=(qb_.name,))
                sq_tok = S.tok("ac")
            else:
                S.op("vector", lambda e, c=c, src=src, qb_=qb_: e.tensor_tensor(
                    qb_[:], src[:, c * T:(c + 1) * T],
                    src[:, c * T:(c + 1) * T], ALU.mult), inc="ve",
                    writes=(qb_.name,))
                sq_tok = S.tok("ve")
            if slot % 2 == 1:
                emit_wbuild(slot // 2)
            slot += 1
            S.wait("tensor", sq_tok)
            for (o, n) in nslices(T):
                S.op("tensor", lambda e, c=c, o=o, n=n, qb_=qb_: e.matmul(
                    PS[0:16, o:o + n], selq[:, 16 * c:16 * (c + 1)],
                    qb_[:, o:o + n], start=(c == 0), stop=(c == CH - 1)))
            mm = S.close("tensor", "te")
            if c % 2 == 0:
                S.wait("scalar", mm)
            else:
                S.wait("vector", mm)
        bounds_mm.append(S.tok("te"))
        S.wait("vector", bounds_mm[-1])
        col = 4 if which == "q" else 5
        S.op("vector", lambda e, col=col: e.reduce_max(
            sq32[0:16, col:col + 1], PS[0:16, 0:T], axis=AX.X),
            inc="ve", writes=("sq32",))
    S.op("vector", lambda e: e.tensor_tensor(
        sq32[0:16, 6:7], sq32[0:16, 4:5], sq32[0:16, 5:6], ALU.mult),
        inc="ve", reads=("sq32",), writes=("sq32",))
    S.wait("scalar", S.tok("ve"))
    S.op("scalar", lambda e: e.activation(
        sq32[0:16, 7:8], sq32[0:16, 6:7], AF.Ln), inc="ac",
        reads=("sq32",), writes=("sq32",))
    S.op("scalar", lambda e: e.activation(
        sq32[0:16, 7:8], sq32[0:16, 7:8], AF.Exp, scale=0.5), inc="ac",
        reads=("sq32",), writes=("sq32",))
    S.wait("vector", S.tok("ac"))
    S.op("vector", lambda e: e.tensor_scalar(
        sq32[0:16, 0:1], sq32[0:16, 7:8], -1.05 / 8.0, -0.125,
        ALU.mult, ALU.add), inc="ve", reads=("sq32",), writes=("sq32",))
    S.op("vector", lambda e: e.transpose(sq32b[0:32, 0:32], sq32[0:32, 0:32]),
         inc="ve", reads=("sq32",), writes=("sq32b",))
    S.op("vector", lambda e: e.tensor_tensor(
        sq32b[0:1, 16:16 + H // 2],
        sq32b[0:1, 0:H].rearrange("p (h two) -> p h two", two=2)[:, :, 0],
        sq32b[0:1, 0:H].rearrange("p (h two) -> p h two", two=2)[:, :, 1],
        ALU.min), inc="ve", reads=("sq32b",), writes=("sq32b",))
    bias_row = S.tok("ve")
    S.wait("tensor", bias_row)
    S.op("tensor", lambda e: e.matmul(
        PS[0:128, 3072:3072 + H // 2], ones_f[:], sq32b[0:1, 16:16 + H // 2],
        start=True, stop=True), inc="te")
    S.wait("vector", S.tok("te"))
    S.op("vector", lambda e: e.tensor_copy(biasP[:], PS[0:128, 3072:3072 + H // 2]),
         inc="ve", writes=("biasP",))
    biasH_tok = S.tok("ve")

    S.wait("scalar", biasH_tok)
    S.wait("tensor", vaug_tok)
    st_last = {}
    pt_last = {}
    evict_toks = {}
    prev_resc = [None]
    pending = [None]
    RS_BASE = max(2, KT - 6)

    def emit_rescale_step(step):
        if pending[0] is None:
            return
        lrow, araw, p, rcp_tok, rb0c, aq = pending[0][:6]
        if step == 0:
            S.wait("tensor", rcp_tok)
            if wb_last[0] is not None:
                S.wait("tensor", wb_last[0])
            if prev_resc[0] is not None:
                S.wait("tensor", prev_resc[0])
            S.op("tensor", lambda e, lrow=lrow, rb0c=rb0c: e.matmul(
                PS[0:64, rb0c:rb0c + QBL], ones_f[0:1, 0:64],
                lrow[0:1, 0:QBL], start=True, stop=True), inc="te")
            pending[0] = pending[0][:6] + (S.tok("te"),)
        elif step == 1:
            rbA = pending[0][6]
            S.wait("vector", rbA)
            S.op("vector", lambda e, p=p, araw=araw, rb0c=rb0c, aq=aq: e.tensor_tensor(
                aq[0:64, p * QBL:(p + 1) * QBL], araw[:, 0:QBL],
                PS[0:64, rb0c:rb0c + QBL], ALU.mult), inc="ve",
                reads=(araw.name,), writes=("%sA%d" % (aq.name, p),))
            pending[0] = pending[0][:6] + (S.tok("ve"),)
        elif step == 2:
            ttA = pending[0][6]
            S.wait("tensor", ttA)
            S.op("tensor", lambda e, lrow=lrow, rb0c=rb0c: e.matmul(
                PS[0:64, rb0c:rb0c + QBL], ones_f[0:1, 0:64],
                lrow[0:1, QBL:2 * QBL], start=True, stop=True), inc="te")
            pending[0] = pending[0][:6] + (S.tok("te"),)
        else:
            rbB = pending[0][6]
            S.wait("vector", rbB)
            S.op("vector", lambda e, p=p, araw=araw, rb0c=rb0c, aq=aq: e.tensor_tensor(
                aq[64:128, p * QBL:(p + 1) * QBL], araw[:, QBL:2 * QBL],
                PS[0:64, rb0c:rb0c + QBL], ALU.mult), inc="ve",
                reads=(araw.name,), writes=("%sB%d" % (aq.name, p),))
            prev_resc[0] = S.tok("ve")
            pending[0] = None

    def flush_rescale():
        for st in range(4):
            emit_rescale_step(st)

    prm_te = {}
    wp_toks = {}
    prev_projev = {}
    NCHUNK = NQB * CH

    def emit_wp_dma(gi):
        wp = (wpt, wpt2)[gi % 2]
        bpr = (bp_sl0, bp_sl1)[gi % 2]
        mf = gi % CH
        dsem = ("sw", "sw2")[gi % 2]
        if gi >= 2:
            S.wait("gpsimd", prm_te[gi - 2])
        S.dma_batch("gpsimd", dsem)
        S.op("gpsimd", lambda e, wp=wp, mf=mf: e.dma_start(wp[:], wph[mf]),
             inc=dsem, n=16)
        S.op("gpsimd", lambda e, mf=mf, bpr=bpr: e.dma_start(
            bpr[:], bp_row[0:1, mf * 128:(mf + 1) * 128]), inc=dsem, n=16)
        wp_toks[gi] = S.tok(dsem)

    def emit_proj_chunk(gi, fb, claim_key, ev_tok):
        src_qb, mf = divmod(gi, CH)
        aq = (attnq, attnq2)[src_qb % 2]
        qo2 = src_qb * QBL
        wp = (wpt, wpt2)[gi % 2]
        bpr = (bp_sl0, bp_sl1)[gi % 2]
        S.wait("tensor", wp_toks[gi])
        if ev_tok is not None:
            S.wait("tensor", ev_tok)
        if prev_resc[0] is not None:
            S.wait("tensor", prev_resc[0])
        for kc in range(CH):
            S.op("tensor", lambda e, kc=kc, wp=wp, aq=aq, fb=fb: e.matmul(
                PS[0:128, fb:fb + QBL], wp[:, kc * 128:(kc + 1) * 128],
                aq[:, kc * QBL:kc * QBL + QBL],
                start=(kc == 0), stop=False))
        S.op("tensor", lambda e, bpr=bpr, fb=fb: e.matmul(
            PS[0:128, fb:fb + QBL], bpr[0:1, 0:128], ones_b[0:1, 0:QBL],
            start=False, stop=True))
        prm = S.close("tensor", "te")
        prm_te[gi] = prm
        S.wait("vector", prm)
        ev = S.op("vector", lambda e, mf=mf, qo2=qo2, fb=fb: e.tensor_tensor(
            xr[:, mf * T + qo2:mf * T + qo2 + QBL], PS[0:128, fb:fb + QBL],
            xr[:, mf * T + qo2:mf * T + qo2 + QBL], ALU.add), inc="ve",
            reads=("xr%d" % mf,), writes=("xr%d" % mf,))
        if claim_key is not None:
            prev_projev[claim_key] = ev
        if gi + 2 < NCHUNK:
            emit_wp_dma(gi + 2)
        return ev

    PTS = (PT0, PT1, PT2, PT3)
    exp_all = {}
    NSLOT = NQB * CH * KT
    ev_final = {}

    def emit_st_exp(gs):
        pp, kt = divmod(gs, KT)
        qb, p = divmod(pp, CH)
        qo = qb * QBL
        pt = PTS[gs % 4]
        if gs >= 2:
            S.wait("tensor", exp_all[gs - 2])
        S.op("tensor", lambda e, kt=kt, p=p, qo=qo, gs=gs: e.matmul(
            PS[0:128, 2 * QBL * (gs % 2):2 * QBL * (gs % 2) + QBL],
            Ks[0:64, p * T + kt * 128:p * T + (kt + 1) * 128],
            Qs[0:64, p * T + qo:p * T + qo + QBL],
            start=True, stop=True, tile_position=(0, 0)))
        S.op("tensor", lambda e, kt=kt, p=p, qo=qo, gs=gs: e.matmul(
            PS[0:128, 2 * QBL * (gs % 2) + QBL:2 * QBL * (gs % 2) + 2 * QBL],
            Ks[64:128, p * T + kt * 128:p * T + (kt + 1) * 128],
            Qs[64:128, p * T + qo:p * T + qo + QBL],
            start=True, stop=True, tile_position=(64, 0)))
        st_tok = S.close("tensor", "te")
        S.wait("scalar", st_tok)
        exp_all[gs] = S.op("scalar", lambda e, gs=gs, pt=pt, p=p: e.activation(
            pt[:], PS[0:128, 2 * QBL * (gs % 2):2 * QBL * (gs % 2) + 2 * QBL],
            AF.Exp, bias=biasP[:, p:p + 1], scale=0.125), inc="ac",
            writes=(pt.name,))

    def emit_pv_slot(gp):
        pp, kt = divmod(gp, KT)
        qb, p = divmod(pp, CH)
        par_p = pp % 2
        lrow = (lrow0, lrow1)[par_p]
        araw = (araw0, araw1)[par_p]
        pv0 = 2048 + 1024 * par_p
        pt = PTS[gp % 4]
        if kt == 0:
            S.wait("tensor", evict_toks.get(pp - 2))
            if pp <= 1 and wb_last[0] is not None:
                S.wait("tensor", wb_last[0])
            S.wait("tensor", prev_projev.get(pp - 1))
            if prev_resc[0] is not None:
                S.wait("tensor", prev_resc[0])
        S.op("tensor", lambda e, kt=kt, pt=pt, p=p, pv0=pv0: e.matmul(
            PS[0:65, pv0:pv0 + QBL],
            vaug[:, kt * H * 65 + 2 * p * 65:kt * H * 65 + (2 * p + 1) * 65],
            pt[:, 0:QBL], start=(kt == 0), stop=(kt == KT - 1)))
        S.op("tensor", lambda e, kt=kt, pt=pt, p=p, pv0=pv0: e.matmul(
            PS[0:65, pv0 + 512:pv0 + 512 + QBL],
            vaug[:, kt * H * 65 + (2 * p + 1) * 65:kt * H * 65 + (2 * p + 2) * 65],
            pt[:, QBL:2 * QBL], start=(kt == 0), stop=(kt == KT - 1)))
        pvc = S.close("tensor", "te")
        if kt == KT - 1:
            S.wait("vector", pvc)
            pv2 = PS[:, pv0:pv0 + 1024].rearrange(
                "p (b c) -> p b c", b=2)[:, :, 0:QBL]
            S.op("vector", lambda e, lrow=lrow, pv2=pv2: e.tensor_copy(
                lrow[:].rearrange("p (b c) -> p b c", b=2), pv2[64:65]),
                inc="ve", writes=(lrow.name,))
            S.op("vector", lambda e, araw=araw, pv2=pv2: e.tensor_copy(
                araw[:].rearrange("p (b c) -> p b c", b=2), pv2[0:64]),
                inc="ve", writes=(araw.name,))
            evict_toks[pp] = S.tok("ve")
            S.op("vector", lambda e, lrow=lrow: e.reciprocal(
                lrow[:], lrow[:]), inc="ve", reads=(lrow.name,),
                writes=(lrow.name,))
            pending[0] = (lrow, araw, p, S.tok("ve"), pv0,
                          (attnq, attnq2)[qb % 2])
            if p == CH - 1 and qb == 0:
                for g in range(2):
                    emit_wp_dma(g)
        else:
            if RS_BASE <= kt < RS_BASE + 4:
                emit_rescale_step(kt - RS_BASE)
            if kt == 5 and qb >= 1 and p >= 1:
                emit_proj_chunk((qb - 1) * CH + (p - 1),
                                2048 + 1024 * ((pp - 1) % 2) + 512,
                                pp, evict_toks.get(pp - 1))
            if kt == KT - 2 and p == CH - 1 and qb >= 1:
                emit_proj_chunk((qb - 1) * CH + 7,
                                2048 + 1024 * ((pp - 1) % 2) + 512,
                                pp, prev_projev.get(pp))

    for gs in range(NSLOT + 2):
        if gs < NSLOT:
            emit_st_exp(gs)
        else:
            S.wait("tensor", exp_all[gs - 2])
        if gs >= 2:
            emit_pv_slot(gs - 2)
    flush_rescale()
    prev_prm = {}
    for mf in range(CH):
        gi = (NQB - 1) * CH + mf
        fb = 1024 if mf % 2 == 0 else 2048
        if mf >= 2:
            S.wait("tensor", prev_prm[mf - 2])
        prev_prm[mf] = emit_proj_chunk(gi, fb, None, None)
    attn_done = S.tok("ve")

    S.wait("gpsimd", S.tok("te"))
    S.wait("sync", S.tok("ac"))
    S.wait("sync", S.tok("te"))
    ffnw2_gc, ffnw2_sw, ffnw2_hw = emit_ffn_weight_loads()
    S.drain("vector")
    S.drain("scalar")
    S.wait("vector", attn_done)
    S.wait("scalar", attn_done)
    preload_set6()
    rows3, gc3 = emit_ln(want_gc=True)
    S.op("scalar", lambda e: e.preload_act_table(AF.Gelu))
    S.wait("tensor", ffnw2_sw)
    S.wait("scalar", ffnw2_sw)
    S.wait("tensor", ffnw2_hw)
    S.wait("scalar", ffnw2_hw)
    ab3 = emit_gcnet(gc3, ffnw2_gc, ffnw2_hw)

    def store_slice(so, tok):
        S.wait("sync", tok)
        S.dma_batch("sync", "hw")
        for c in range(CH):
            S.op("sync", lambda e, c=c, so=so: e.dma_start(
                y[c * 128:(c + 1) * 128, so:so + SL],
                xr[:, c * T + so:c * T + so + SL]), inc="hw", n=16)

    ffn2_done = emit_ffn(rows3, ab3, store_cb=store_slice)
    S.wait("sync", S.tok("hw"))

    S.build()
    return nc


_CACHE = {}


def _get_program(T):
    if T not in _CACHE:
        _CACHE[T] = build_program(T)
    return _CACHE[T]


def kernel(**inputs):
    x = np.asarray(inputs["L0_attn_x"], dtype=np.float32)
    B, T, D = x.shape
    nc = _get_program(T)
    wmap = {
        "Wqkv": "L0_attn_W_qkv", "bqkv": "L0_attn_b_qkv",
        "Aq": "L0_attn_A_qkv", "Bq": "L0_attn_B_qkv",
        "Wproj": "L0_attn_W_proj", "bproj": "L0_attn_b_proj",
        "Apj": "L0_attn_A_proj", "Bpj": "L0_attn_B_proj",
        "cw1": "L0_ffn_cn_w1", "cb1": "L0_ffn_cn_b1",
        "cw2": "L0_ffn_cn_w2", "cb2": "L0_ffn_cn_b2",
        "cw3": "L0_ffn_cn_w3", "cb3": "L0_ffn_cn_b3",
        "cw4": "L0_ffn_cn_w4", "cb4": "L0_ffn_cn_b4",
        "gw1": "L0_ffn_gc_w1", "gb1": "L0_ffn_gc_b1",
        "gw2": "L0_ffn_gc_w2", "gb2": "L0_ffn_gc_b2",
        "gw3": "L0_ffn_gc_w3", "gb3": "L0_ffn_gc_b3",
        "ar": "L0_ffn_adaptive_range",
    }
    shared = {k: np.ascontiguousarray(np.asarray(inputs[v], dtype=np.float32))
              for k, v in wmap.items()}
    in_maps = []
    for b in range(B):
        m = dict(shared)
        m["xT"] = np.ascontiguousarray(x[b].T)
        in_maps.append(m)
    res = run_bass_kernel_spmd(nc, in_maps, core_ids=list(range(N_CORES)))
    out = np.stack([np.asarray(res.results[b]["y"]).T for b in range(B)])
    return out.astype(np.float32)
